# version 03
# baseline (speedup 1.0000x reference)
import contextlib
import zlib
import numpy as np
import concourse.bass as bass
import concourse.mybir as mybir
from concourse.bass_utils import run_bass_kernel_spmd

F32 = mybir.dt.float32
BF16 = mybir.dt.bfloat16
AF = mybir.ActivationFunctionType
ALU = mybir.AluOpType
AX = mybir.AxisListType.X

D, S, NB, NOB, OWNQ, B = 1024, 4096, 32, 16, 2048, 4
NH, DH, DHR, DL, DQK = 16, 64, 32, 256, 96
INVSQ = float(1.0 / np.sqrt(np.float64(DQK)))
EXPLO = float(np.exp(np.float64(-80.0) * INVSQ))
EXPHI = float(np.exp(np.float64(80.0) * INVSQ))

ENGS = ("pe", "dve", "act", "pool", "sp")
_ENG_ATTR = {"pe": "tensor", "dve": "vector", "act": "scalar", "pool": "gpsimd", "sp": "sync"}
_SEM_OF = {"pe": "spe", "dve": "sdv", "act": "sac", "pool": "spl", "sp": "ssp"}


class Q:

    def __init__(self, nc, ctx):
        self.nc, self.ctx = nc, ctx
        self.ops = {e: [] for e in ENGS}
        self.meta = {e: [] for e in ENGS}
        self.sems, self.cnt = {}, {}
        self.last_writer = {}
        self.readers = {}
        self.last_tok = {e: None for e in ENGS}

    def sem(self, name):
        if name not in self.sems:
            self.sems[name] = self.ctx.enter_context(self.nc.semaphore(name))
            self.cnt[name] = 0
        return self.sems[name]

    @staticmethod
    def _tid(t):
        h = getattr(t, "tensor", None)
        if h is not None:
            return getattr(h, "name", id(h))
        return getattr(t, "name", id(t))

    def do(self, eng, fn, reads=(), writes=(), inc=None, n=1, is_dma=False, acc=False):
        for t in reads:
            self.wt(eng, self.last_writer.get(self._tid(t)))
        if not acc:
            for t in writes:
                ti = self._tid(t)
                self.wt(eng, self.last_writer.get(ti))
                for rt in self.readers.get(ti, ()):
                    self.wt(eng, rt)
        if inc is not None:
            h = self.sem(inc)
            self.cnt[inc] += n
            tok = ("eager", inc, self.cnt[inc])
            self.ops[eng].append(lambda e, fn=fn, h=h, n=n: fn(e).then_inc(h, n))
            self.meta[eng].append(("dma", inc, self.cnt[inc]) if is_dma else ("op", None))
        else:
            sem_name = _SEM_OF[eng]
            h = self.sem(sem_name)
            cell = {"sem": sem_name, "on": False, "v": None}
            tok = ("lazy", cell, eng)

            def run(e, fn=fn, h=h, cell=cell):
                ins = fn(e)
                if cell["on"]:
                    ins.then_inc(h, 1)
                return ins
            self.ops[eng].append(run)
            self.meta[eng].append(("op", cell))
        for t in reads:
            self.readers.setdefault(self._tid(t), []).append(tok)
        for t in writes:
            ti = self._tid(t)
            self.last_writer[ti] = tok
            self.readers[ti] = []
        self.last_tok[eng] = tok
        return tok

    def dma(self, eng, out_ap, in_ap, inc=None, reads=(), writes=()):
        dst = writes[0] if writes else out_ap
        grp = zlib.crc32(str(self._tid(dst)).encode()) % 4
        sem_name = inc if inc is not None else ("d_%s%d" % (eng, grp))
        self.sem(sem_name)
        if self.cnt[sem_name] > 0:
            self.wt(eng, ("eager", sem_name, self.cnt[sem_name]))
        return self.do(eng, lambda e, o=out_ap, i=in_ap: e.dma_start(o, i),
                       reads=reads, writes=writes, inc=sem_name, n=16, is_dma=True)

    def wt(self, eng, tok):
        if tok is None:
            return
        if tok[0] == "lazy":
            tok[1]["on"] = True
            wcell = {"sem": tok[1]["sem"], "pc": tok[1], "v": None, "skip": False}
        else:
            wcell = {"sem": tok[1], "pc": None, "v": tok[2], "skip": False}
        h = self.sem(wcell["sem"])

        def run(e, h=h, c=wcell):
            if not c["skip"]:
                return e.wait_ge(h, c["v"])

        class _W:
            def then_inc(self, *a):
                return self
        self.ops[eng].append(lambda e, h=h, c=wcell: (run(e)))
        self.meta[eng].append(("wait", wcell))

    def wts(self, engs, toks):
        for e in engs:
            for t in toks:
                self.wt(e, t)

    def barrier(self):
        toks = {id(t): t for t in self.last_tok.values() if t is not None}
        for t in self.last_writer.values():
            toks[id(t)] = t
        for lst in self.readers.values():
            for t in lst:
                toks[id(t)] = t
        for e in ENGS:
            for t in toks.values():
                self.wt(e, t)

    def copy(self, eng, dst, src, reads=(), writes=()):
        r = list(reads) or [src]
        w = list(writes) or [dst]
        if eng == "act":
            return self.do(eng, lambda e, d=dst, s=src: e.copy(d, s), reads=r, writes=w)
        return self.do(eng, lambda e, d=dst, s=src: e.tensor_copy(d, s), reads=r, writes=w)

    def _finalize(self):
        for eng in ENGS:
            for m in self.meta[eng]:
                if m[0] == "op" and m[1] is not None and m[1]["on"]:
                    c = m[1]
                    self.cnt[c["sem"]] += 1
                    c["v"] = self.cnt[c["sem"]]
        for eng in ENGS:
            for m in self.meta[eng]:
                if m[0] == "wait" and m[1]["pc"] is not None:
                    m[1]["v"] = m[1]["pc"]["v"]
        for eng in ENGS:
            hi = {}
            for m in self.meta[eng]:
                if m[0] == "wait":
                    c = m[1]
                    if c["v"] is None or hi.get(c["sem"], -1) >= c["v"]:
                        c["skip"] = True
                    else:
                        hi[c["sem"]] = c["v"]
        burst_end = {}
        for eng in ENGS:
            run = {}
            for m in self.meta[eng] + [("op", None)]:
                if m[0] == "dma":
                    _, s_, v_ = m
                    run.setdefault(s_, []).append(v_)
                else:
                    for s_, vals in run.items():
                        for v_ in vals:
                            burst_end.setdefault(s_, {})[v_] = vals[-1]
                    run = {}
        for eng in ENGS:
            for m in self.meta[eng]:
                if m[0] == "wait" and not m[1]["skip"]:
                    c = m[1]
                    be = burst_end.get(c["sem"], {}).get(c["v"])
                    if be is not None and be > c["v"]:
                        c["v"] = be

    def build(self):
        self._finalize()
        with self.nc.allow_low_precision(reason="intentional bf16 staging"), \
             self.nc.Block() as block:
            for e in ENGS:
                if not self.ops[e]:
                    continue

                def mk(ops):
                    def body(eng):
                        for f in ops:
                            f(eng)
                    return body
                getattr(block, _ENG_ATTR[e])(mk(self.ops[e]))


class Psum:
    def __init__(self, q, nc, ctx):
        self.q = q
        self.banks = [ctx.enter_context(nc.psum_tensor(f"ps{i}", [128, 512], F32))
                      for i in range(8)]
        self.lastc = [None] * 8
        self.rr = 0
        self.held = set()

    def get(self, idx=None):
        if idx is None:
            for _ in range(16):
                i = self.rr % 8
                self.rr += 1
                if i not in self.held:
                    idx = i
                    break
        self.q.wt("pe", self.lastc[idx])
        return idx, self.banks[idx]

    def hold(self, idx):
        self.held.add(idx)

    def release(self, idx, consumer_tok):
        self.lastc[idx] = consumer_tok
        self.held.discard(idx)


def cdiv(a, b):
    return (a + b - 1) // b


L0_IN = [
    ("xb", [S, D]), ("wdq", [D, DL]), ("wuq", [DL, NH * DH]),
    ("wqr_a", [DL, NH * DHR]), ("wqr_b", [DL, NH * DHR]), ("wdkv", [D, DL]),
    ("wuk", [DL, NH * DH]), ("wuv", [DL, NH * DH]), ("wkr_a", [D, DHR]),
    ("wkr_b", [D, DHR]), ("wo", [NH * DH, D]), ("u1", [D, 256]), ("v1", [256, 4096]),
    ("b1r", [1, 4096]), ("u2", [4096, 256]), ("v2", [256, D]), ("b2r", [1, D]),
    ("cosq", [128, OWNQ]), ("sinq", [128, OWNQ]), ("cosk", [DHR, S]), ("sink", [DHR, S]),
    ("trimask", [128, 128]), ("m2col", [128, 1]), ("onesr", [1, 512]),
    ("sel16", [16, 16 * 128]),
]


def build_l0(dbg=(), upto=99, env=None):
    if env is None:
        nc = bass.Bass("TRN2", target_bir_lowering=False)
        ctx = contextlib.ExitStack()
        q = Q(nc, ctx)
        pp = Psum(q, nc, ctx)
    else:
        nc, ctx, q, pp = env
    ps = pp.banks

    W = {n: nc.dram_tensor(n, list(s), F32, kind="ExternalInput") for n, s in L0_IN}
    xb = W["xb"]
    x1own = nc.dram_tensor("x1own", [OWNQ, D], F32,
                           kind="ExternalOutput" if env is None else "Internal")
    zscratch = nc.dram_tensor("zscratch", [16, 512], F32)
    dbg_o = {n: nc.dram_tensor(n, list(s), F32, kind="ExternalOutput") for n, s in dbg}

    def mksb(c):
        return lambda name, shape, dt=BF16: c.enter_context(
            nc.sbuf_tensor(name, list(shape), dt))

    sb = mksb(ctx)
    cqT = [sb(f"cqT{k}", [128, OWNQ]) for k in range(2)]
    ckvT = [sb(f"ckvT{k}", [128, S]) for k in range(2)]
    krT = sb("krT", [DHR, S])
    pT = [sb(f"pT{k}", [128, OWNQ]) for k in range(2)]
    trim = sb("trim", [128, 128])
    m2c = sb("m2c", [128, 1], F32)
    onesc = sb("onesc", [1, 512])
    sel16 = sb("sel16t", [16, 16 * 128])
    WT = {}

    def alloc_w(c, names):
        s_ = mksb(c)
        for n in names:
            R, C = W[n].shape
            WT[n] = s_("w_" + n, [1 if R == 1 else 128, cdiv(R, 128) * C])

    def cast_weights(names):
        for n in names:
            R, C = W[n].shape
            for k in range(cdiv(R, 128)):
                r = min(128, R - 128 * k)
                q.dma("pool", WT[n][:r, k * C:(k + 1) * C],
                      W[n][128 * k:128 * k + r, :],
                      reads=[W[n]], writes=[WT[n]])

    alloc_w(ctx, ["wuq", "wqr_a", "wqr_b", "wuk", "wuv"])
    s1 = ctx.enter_context(contextlib.ExitStack())
    sb1 = mksb(s1)
    xnT = [sb1(f"xnT{k}", [128, S]) for k in range(8)]
    coskt, sinkt = sb1("coskt", [DHR, S], F32), sb1("sinkt", [DHR, S], F32)
    alloc_w(s1, ["wdq", "wdkv", "wkr_a", "wkr_b", "u1"])

    for dst, srch in ((trim, W["trimask"]), (onesc, W["onesr"]), (sel16, W["sel16"])):
        q.dma("pool", dst[:, :], srch[:, :], reads=[srch], writes=[dst])
    q.dma("sp", m2c[:, :], W["m2col"][:, :], reads=[W["m2col"]], writes=[m2c])
    q.dma("sp", coskt[:, :], W["cosk"][:, :], reads=[W["cosk"]], writes=[coskt])
    q.dma("sp", sinkt[:, :], W["sink"][:, :], reads=[W["sink"]], writes=[sinkt])
    cast_weights(["wuq", "wqr_a", "wqr_b", "wuk", "wuv", "wdq", "wdkv", "wkr_a",
                  "wkr_b", "u1"])

    s_ln = contextlib.ExitStack()
    sbl = mksb(s_ln)
    xrow = [sbl(f"xrow{i}", [128, D], F32) for i in range(2)]
    sqs = [sbl(f"sqs{i}", [128, D]) for i in range(2)]
    mus = sbl("mus", [128, NB], F32)
    s2s = sbl("s2s", [128, NB], F32)
    vars_ = sbl("vars", [128, NB], F32)
    rstds = sbl("rstds", [128, NB], F32)
    xnrow = [sbl(f"xnrow{i}", [128, D]) for i in range(2)]

    for t in range(NB):
        xt = xrow[t % 2]
        q.dma("sp", xt[:, :], xb[128 * t:128 * (t + 1), :], reads=[xb], writes=[xt])
        q.do("dve", lambda e, xt=xt, t=t: e.reduce_sum(mus[:, t:t + 1], xt[:, :], axis=AX),
             reads=[xt], writes=[mus], acc=(t > 0))
        q.do("act", lambda e, xt=xt, t=t: e.activation(
            sqs[t % 2][:, :], xt[:, :], AF.Square, accum_out=s2s[:, t:t + 1]),
            reads=[xt], writes=[sqs[t % 2], s2s], acc=(t > 1))
    q.do("dve", lambda e: e.tensor_scalar_mul(mus[:, :], mus[:, :], 1.0 / D),
         reads=[mus], writes=[mus])
    q.do("dve", lambda e: e.tensor_scalar_mul(s2s[:, :], s2s[:, :], 1.0 / D),
         reads=[s2s], writes=[s2s])
    q.do("dve", lambda e: e.scalar_tensor_tensor(vars_[:, :], mus[:, :], 0.0, mus[:, :],
                                                 ALU.add, ALU.mult),
         reads=[mus], writes=[vars_])
    q.do("dve", lambda e: e.tensor_sub(vars_[:, :], s2s[:, :], vars_[:, :]),
         reads=[s2s, vars_], writes=[vars_])
    q.do("dve", lambda e: e.tensor_scalar_add(vars_[:, :], vars_[:, :], 1e-5),
         reads=[vars_], writes=[vars_])
    q.do("act", lambda e: e.activation(rstds[:, :], vars_[:, :], AF.Sqrt),
         reads=[vars_], writes=[rstds])
    q.do("dve", lambda e: e.reciprocal(rstds[:, :], rstds[:, :]),
         reads=[rstds], writes=[rstds])

    for t in range(NB):
        xt, xo = xrow[t % 2], xnrow[t % 2]
        q.dma("sp", xt[:, :], xb[128 * t:128 * (t + 1), :], reads=[xb], writes=[xt])
        q.do("dve", lambda e, xt=xt, xo=xo, t=t: e.tensor_scalar(
            xo[:, :], xt[:, :], mus[:, t:t + 1], rstds[:, t:t + 1], ALU.subtract, ALU.mult),
            reads=[xt, mus, rstds], writes=[xo])
        for k in range(8):
            sem_name = "dt%d" % k
            q.sem(sem_name)
            if q.cnt[sem_name] > 0:
                q.wt("sp", ("eager", sem_name, q.cnt[sem_name]))
            q.do("sp", lambda e, xo=xo, t=t, k=k: e.dma_start_transpose(
                xnT[k][:, 128 * t:128 * (t + 1)], xo[:, 128 * k:128 * (k + 1)]),
                reads=[xo], writes=[xnT[k]], inc=sem_name, n=16, is_dma=True)

    evac_rr = [0]

    def evac(dst, srcb, eng=None, extra_reads=(), dwr=None):
        if eng is None:
            eng = ("dve", "act")[evac_rr[0] % 2]
            evac_rr[0] += 1
        return q.copy(eng, dst, srcb, reads=[srcb] + list(extra_reads),
                      writes=[dwr if dwr is not None else dst])

    def proj(dsts, wname, Kc, Mtot, N, src, CW=None, wcol0=0):
        CW = CW if CW is not None else WT[wname].shape[1] // Kc
        for mi in range(cdiv(Mtot, 128)):
            m = min(128, Mtot - 128 * mi)
            for nj in range(cdiv(N, 512)):
                nn = min(512, N - 512 * nj)
                bi, pb = pp.get()
                for k in range(Kc):
                    q.do("pe", lambda e, pb=pb, m=m, nn=nn, k=k, mi=mi, nj=nj, CW=CW,
                         wname=wname, wcol0=wcol0: e.matmul(
                             pb[:m, :nn],
                             WT[wname][:, k * CW + wcol0 + 128 * mi:
                                       k * CW + wcol0 + 128 * mi + m],
                             src[k][:, 512 * nj:512 * nj + nn],
                             start=(k == 0), stop=(k == Kc - 1)),
                         reads=[WT[wname], src[k]], writes=[pb], acc=(k > 0))
                evac(dsts[mi][:m, 512 * nj:512 * nj + nn], pb[:m, :nn], dwr=dsts[mi])
        return

    proj(cqT, "wdq", 8, DL, OWNQ, xnT)
    proj(ckvT, "wdkv", 8, DL, S, xnT)
    proj(pT, "u1", 8, 256, OWNQ, xnT)

    s_rope2 = contextlib.ExitStack()
    ropek = [mksb(s_rope2)(f"ropek{i}", [128, 512], F32) for i in range(2)]
    for nj in range(8):
        bia, pa = pp.get()
        bib, pb_ = pp.get()
        for k in range(8):
            q.do("pe", lambda e, k=k, nj=nj, pa=pa: e.matmul(
                pa[:DHR, :512], WT["wkr_a"][:, k * DHR:(k + 1) * DHR],
                xnT[k][:, 512 * nj:512 * (nj + 1)], start=(k == 0), stop=(k == 7)),
                reads=[WT["wkr_a"], xnT[k]], writes=[pa], acc=(k > 0))
        for k in range(8):
            q.do("pe", lambda e, k=k, nj=nj, pb_=pb_: e.matmul(
                pb_[:DHR, :512], WT["wkr_b"][:, k * DHR:(k + 1) * DHR],
                xnT[k][:, 512 * nj:512 * (nj + 1)], start=(k == 0), stop=(k == 7)),
                reads=[WT["wkr_b"], xnT[k]], writes=[pb_], acc=(k > 0))
        q.do("dve", lambda e, nj=nj, pa=pa, r0=ropek[0]: e.tensor_mul(
            r0[:DHR, :512], pa[:DHR, :512], coskt[:, 512 * nj:512 * (nj + 1)]),
            reads=[pa, coskt], writes=[ropek[0]])
        q.do("dve", lambda e, nj=nj, pb_=pb_, r1=ropek[1]: e.tensor_mul(
            r1[:DHR, :512], pb_[:DHR, :512], sinkt[:, 512 * nj:512 * (nj + 1)]),
            reads=[pb_, sinkt], writes=[ropek[1]])
        q.do("dve", lambda e, nj=nj, r0=ropek[0], r1=ropek[1]: e.tensor_add(
            krT[:DHR, 512 * nj:512 * (nj + 1)], r0[:DHR, :512], r1[:DHR, :512]),
            reads=[ropek[0], ropek[1]], writes=[krT])

    if upto <= 2:
        t_out = []
        dbgmap = {"d_xnT0": (xnT[0], 128, 512), "d_cqT0": (cqT[0], 128, 512),
                  "d_pT0": (pT[0], 128, 512), "d_krT": (krT, DHR, 512),
                  "d_ckvT0": (ckvT[0], 128, 512)}
        for nname, (tt, r, cc) in dbgmap.items():
            if nname in dbg_o:
                t_out.append(q.dma("pool", dbg_o[nname][:, :], tt[:r, 0:cc], reads=[tt], writes=[dbg_o[nname]]))
        q.wts(["pool"], t_out)
        q.build()
        s_rope2.close()
        s_ln.close()
        ctx.close()
        return nc

    q.barrier()
    s_rope2.close()
    s_ln.close()
    s1.close()

    attnT = [sb(f"attnT{k}", [128, OWNQ]) for k in range(8)]
    s2 = ctx.enter_context(contextlib.ExitStack())
    sb2 = mksb(s2)
    QT = [sb2(f"QT{h}", [DQK, OWNQ]) for h in range(4)]
    KT = [sb2(f"KT{h}", [DQK, S]) for h in range(4)]
    VT = [sb2(f"VT{i}", [128, 4 * 65]) for i in range(NB)]
    expt = [sb2(f"expt{i}", [128, 512]) for i in range(4)]
    ropet = [sb2(f"ropetw{i}", [128, 512], F32) for i in range(2)]
    cosqt, sinqt = sb2("cosqt", [128, OWNQ]), sb2("sinqt", [128, OWNQ])
    rowsums = sb2("rowsums", [16, 512], F32)
    zcol = sb2("zcol", [128, 2048], F32)
    rowinv = sb2("rowinv", [16, 512])
    invbs = sb2("invbs", [128, 512])
    q.dma("pool", cosqt[:, :], W["cosq"][:, :], reads=[W["cosq"]], writes=[cosqt])
    q.dma("pool", sinqt[:, :], W["sinq"][:, :], reads=[W["sinq"]], writes=[sinqt])

    NWAVE = 1 if upto <= 5 else 4
    for wv in range(NWAVE):
        for mi in range(2):
            for nj in range(4):
                bi, pb = pp.get()
                for k in range(2):
                    q.do("pe", lambda e, pb=pb, k=k, mi=mi, nj=nj, wv=wv: e.matmul(
                        pb[:, :512],
                        WT["wuq"][:, k * NH * DH + 4 * wv * DH + 128 * mi:
                                  k * NH * DH + 4 * wv * DH + 128 * mi + 128],
                        cqT[k][:, 512 * nj:512 * (nj + 1)], start=(k == 0), stop=(k == 1)),
                        reads=[WT["wuq"], cqT[k]], writes=[pb], acc=(k > 0))
                evac(QT[2 * mi][0:64, 512 * nj:512 * (nj + 1)], pb[0:64, :512],
                     dwr=QT[2 * mi])
                evac(QT[2 * mi + 1][0:64, 512 * nj:512 * (nj + 1)], pb[64:128, :512],
                     dwr=QT[2 * mi + 1])
        for nj in range(4):
            bia, pa = pp.get()
            bib, pb_ = pp.get()
            for k in range(2):
                q.do("pe", lambda e, pa=pa, k=k, nj=nj, wv=wv: e.matmul(
                    pa[:, :512],
                    WT["wqr_a"][:, k * NH * DHR + 4 * wv * DHR:
                                k * NH * DHR + 4 * wv * DHR + 128],
                    cqT[k][:, 512 * nj:512 * (nj + 1)], start=(k == 0), stop=(k == 1)),
                    reads=[WT["wqr_a"], cqT[k]], writes=[pa], acc=(k > 0))
            for k in range(2):
                q.do("pe", lambda e, pb_=pb_, k=k, nj=nj, wv=wv: e.matmul(
                    pb_[:, :512],
                    WT["wqr_b"][:, k * NH * DHR + 4 * wv * DHR:
                                k * NH * DHR + 4 * wv * DHR + 128],
                    cqT[k][:, 512 * nj:512 * (nj + 1)], start=(k == 0), stop=(k == 1)),
                    reads=[WT["wqr_b"], cqT[k]], writes=[pb_], acc=(k > 0))
            q.do("dve", lambda e, pa=pa, nj=nj, r0=ropet[0]: e.tensor_mul(
                r0[:, :512], pa[:, :512], cosqt[:, 512 * nj:512 * (nj + 1)]),
                reads=[pa, cosqt], writes=[ropet[0]])
            q.do("dve", lambda e, pb_=pb_, nj=nj, r1=ropet[1]: e.tensor_mul(
                r1[:, :512], pb_[:, :512], sinqt[:, 512 * nj:512 * (nj + 1)]),
                reads=[pb_, sinqt], writes=[ropet[1]])
            q.do("dve", lambda e, r0=ropet[0], r1=ropet[1]: e.tensor_add(
                r0[:, :512], r0[:, :512], r1[:, :512]),
                reads=[ropet[0], ropet[1]], writes=[ropet[0]])
            for hi in range(4):
                q.dma("pool", QT[hi][64:96, 512 * nj:512 * (nj + 1)],
                      ropet[0][32 * hi:32 * hi + 32, :512],
                      reads=[ropet[0]], writes=[QT[hi]])
        for mi in range(2):
            for nj in range(8):
                bi, pb = pp.get()
                for k in range(2):
                    q.do("pe", lambda e, pb=pb, k=k, mi=mi, nj=nj, wv=wv: e.matmul(
                        pb[:, :512],
                        WT["wuk"][:, k * NH * DH + 4 * wv * DH + 128 * mi:
                                  k * NH * DH + 4 * wv * DH + 128 * mi + 128],
                        ckvT[k][:, 512 * nj:512 * (nj + 1)], start=(k == 0), stop=(k == 1)),
                        reads=[WT["wuk"], ckvT[k]], writes=[pb], acc=(k > 0))
                evac(KT[2 * mi][0:64, 512 * nj:512 * (nj + 1)], pb[0:64, :512],
                     dwr=KT[2 * mi])
                evac(KT[2 * mi + 1][0:64, 512 * nj:512 * (nj + 1)], pb[64:128, :512],
                     dwr=KT[2 * mi + 1])
        for hi in range(4):
            q.dma("pool", KT[hi][64:96, :], krT[:DHR, :], reads=[krT], writes=[KT[hi]])
        for i in range(NB):
            q.do("pool", lambda e, i=i: e.memset(VT[i][:, :], 1.0),
                 reads=[], writes=[VT[i]])
        for i in range(NB):
            bi, pb = pp.get()
            for k in range(2):
                q.do("pe", lambda e, pb=pb, k=k, i=i, wv=wv: e.matmul(
                    pb[:, :256], ckvT[k][:, 128 * i:128 * (i + 1)],
                    WT["wuv"][:, k * NH * DH + 4 * wv * DH:k * NH * DH + 4 * wv * DH + 256],
                    start=(k == 0), stop=(k == 1)),
                    reads=[ckvT[k], WT["wuv"]], writes=[pb], acc=(k > 0))
            dst = VT[i][:, :].rearrange("p (h c) -> p h c", c=65)[:, :, 0:64]
            srcv = pb[:, :256].rearrange("p (h c) -> p h c", h=4)
            q.copy("dve", dst, srcv, reads=[pb], writes=[VT[i]])

        if upto <= 3:
            break
        for hi in range(4 if upto >= 5 else 1):
            h = 4 * wv + hi
            slab, row0 = attnT[h // 2], 64 * (h % 2)
            avb = [pp.banks[g] for g in range(4)]
            exp_cnt = [0]
            for c in range(NB):
                j0 = c if c < NOB else c - NOB
                qs = 128 * j0
                for g in range(j0 // 4, 4):
                    g0 = max(qs, 512 * g)
                    nn = 512 * (g + 1) - g0
                    off = g0 % 512
                    sp2 = pp.banks[4 + (exp_cnt[0] % 2)]
                    eb = expt[exp_cnt[0] % 4]
                    q.do("pe", lambda e, sp2=sp2, c=c, g0=g0, nn=nn, hi=hi: e.matmul(
                        sp2[:, :nn], KT[hi][:, 128 * c:128 * (c + 1)],
                        QT[hi][:, g0:g0 + nn], start=True, stop=True),
                        reads=[KT[hi], QT[hi]], writes=[sp2])
                    q.do("act", lambda e, eb=eb, sp2=sp2, nn=nn: e.activation(
                        eb[:, :nn], sp2[:, :nn], AF.Exp, scale=INVSQ),
                        reads=[sp2], writes=[eb])
                    q.do("dve", lambda e, eb=eb, nn=nn: e.tensor_scalar(
                        eb[:, :nn], eb[:, :nn], EXPLO, EXPHI, ALU.max, ALU.min),
                        reads=[eb], writes=[eb])
                    if c < NOB and g0 == qs:
                        q.do("dve", lambda e, eb=eb: e.tensor_mul(
                            eb[:, 0:128], eb[:, 0:128], trim[:, :]),
                            reads=[eb, trim], writes=[eb])
                    if c >= NOB and g0 == qs:
                        q.do("dve", lambda e, eb=eb: e.tensor_scalar(
                            eb[:, 0:128], eb[:, 0:128], m2c[:, :], None, ALU.mult),
                            reads=[eb, m2c], writes=[eb])
                    q.do("pe", lambda e, av=avb[g], hi=hi, c=c, eb=eb, off=off, nn=nn,
                         st=(c == 0), sp_=(c == NOB + 4 * g + 3): e.matmul(
                             av[0:65, off:off + nn], VT[c][:, 65 * hi:65 * hi + 65],
                             eb[:, :nn], start=st, stop=sp_),
                         reads=[VT[c], eb], writes=[avb[g]], acc=(c != 0))
                    exp_cnt[0] += 1
            for g in range(4):
                q.copy("dve", zcol[32 * hi:32 * hi + 1, 512 * g:512 * g + 512],
                       avb[g][64:65, :512], reads=[avb[g]], writes=[zcol])
                evac(slab[row0:row0 + 64, 512 * g:512 * (g + 1)], avb[g][0:64, :512],
                     eng="dve", dwr=slab)
        if upto <= 4:
            break
        for hi4 in range(4):
            q.dma("sp", zscratch[4 * hi4:4 * hi4 + 4, :].rearrange("a b -> (a b)").unsqueeze(0),
                  zcol[32 * hi4:32 * hi4 + 1, :], reads=[zcol], writes=[zscratch])
        q.dma("sp", rowsums[:, :], zscratch[:, :], reads=[zscratch], writes=[rowsums])
        q.do("dve", lambda e: e.reciprocal(rowinv[:, :], rowsums[:, :]),
             reads=[rowsums], writes=[rowinv])
        for hi in range(4):
            h = 4 * wv + hi
            slab, row0 = attnT[h // 2], 64 * (h % 2)
            for g in range(4):
                bi, pb = pp.get()
                q.do("pe", lambda e, pb=pb, hi=hi, g=g: e.matmul(
                    pb[0:128, :512], sel16[0:16, 128 * (4 * hi + g):128 * (4 * hi + g) + 128],
                    rowinv[0:16, :512], start=True, stop=True),
                    reads=[sel16, rowinv], writes=[pb])
                q.copy("dve", invbs[:, :], pb[0:128, :512], reads=[pb], writes=[invbs])
                q.do("dve", lambda e, slab=slab, row0=row0, g=g: e.tensor_mul(
                    slab[row0:row0 + 64, 512 * g:512 * (g + 1)],
                    slab[row0:row0 + 64, 512 * g:512 * (g + 1)],
                    invbs[row0:row0 + 64, :]),
                    reads=[slab, invbs], writes=[slab])

    if upto <= 6:
        t_out = []
        dbgmap = {"d_attnT0": (attnT[0], 128, 2048), "d_QT0": (QT[0], DQK, 512),
                  "d_KT0": (KT[0], DQK, 512), "d_VT0": (VT[0], 128, 260)}
        for nname, (tt, r, cc) in dbgmap.items():
            if nname in dbg_o:
                t_out.append(q.dma("pool", dbg_o[nname][:, :], tt[:r, 0:cc], reads=[tt], writes=[dbg_o[nname]]))
        q.wts(["pool"], t_out)
        q.build()
        ctx.close()
        return nc

    q.barrier()
    s2.close()
    s3 = ctx.enter_context(contextlib.ExitStack())
    sb3 = mksb(s3)
    alloc_w(s3, ["wo", "v1", "b1r", "u2", "v2", "b2r"])
    cast_weights(["wo", "v1", "b1r", "u2", "v2", "b2r"])
    hT = [sb3(f"hT{i}", [128, 512]) for i in range(2)]
    q2T = [sb3(f"q2T{k}", [128, OWNQ]) for k in range(2)]
    gtile = [sb3(f"gtile{i}", [128, D]) for i in range(2)]
    sgt = sb3("sgt", [128, D])
    x1r = [sb3(f"x1r{i}", [128, D], F32) for i in range(2)]

    for tg in range(4):
        pq0, pq1 = pp.banks[6], pp.banks[7]
        for f in range(32):
            bi, pb = pp.get(f % 3)
            ht = hT[f % 2]
            for k in range(2):
                q.do("pe", lambda e, pb=pb, k=k, f=f, tg=tg: e.matmul(
                    pb[:, :512], WT["v1"][:, k * 4096 + 128 * f:k * 4096 + 128 * f + 128],
                    pT[k][:, 512 * tg:512 * (tg + 1)], start=(k == 0), stop=False),
                    reads=[WT["v1"], pT[k]], writes=[pb], acc=(k > 0))
            q.do("pe", lambda e, pb=pb, f=f: e.matmul(
                pb[:, :512], WT["b1r"][0:1, 128 * f:128 * f + 128], onesc[0:1, 0:512],
                start=False, stop=True),
                reads=[WT["b1r"], onesc], writes=[pb], acc=True)
            q.do("act", lambda e, pb=pb, ht=ht: e.activation(
                ht[:, :512], pb[:, :512], AF.Gelu_apprx_tanh),
                reads=[pb], writes=[ht])
            for m in range(2):
                q.do("pe", lambda e, m=m, f=f, tg=tg, ht=ht, pq=(pq0, pq1)[m]: e.matmul(
                    pq[:, :512], WT["u2"][:, 256 * f + 128 * m:256 * f + 128 * m + 128],
                    ht[:, :512], start=(f == 0), stop=(f == 31)),
                    reads=[WT["u2"], ht], writes=[(pq0, pq1)[m]], acc=(f > 0))
        evac(q2T[0][:, 512 * tg:512 * (tg + 1)], pq0[:, :512], eng="dve", dwr=q2T[0])
        evac(q2T[1][:, 512 * tg:512 * (tg + 1)], pq1[:, :512], eng="act", dwr=q2T[1])

    for i in range(NOB):
        xt = x1r[i % 2]
        q.dma("sp", xt[:, :], xb[128 * i:128 * (i + 1), :], reads=[xb], writes=[xt])
        gt = gtile[i % 2]
        for half in range(2):
            bi, pb = pp.get()
            for k in range(2):
                q.do("pe", lambda e, pb=pb, k=k, i=i, half=half: e.matmul(
                    pb[:, :512], q2T[k][:, 128 * i:128 * (i + 1)],
                    WT["v2"][:, k * D + 512 * half:k * D + 512 * half + 512],
                    start=(k == 0), stop=False),
                    reads=[q2T[k], WT["v2"]], writes=[pb], acc=(k > 0))
            q.do("pe", lambda e, pb=pb, half=half: e.matmul(
                pb[:, :512], onesc[0:1, 0:128], WT["b2r"][0:1, 512 * half:512 * half + 512],
                start=False, stop=True),
                reads=[onesc, WT["b2r"]], writes=[pb], acc=True)
            q.do("act", lambda e, pb=pb, gt=gt, half=half: e.activation(
                gt[:, 512 * half:512 * half + 512], pb[:, :512], AF.Silu),
                reads=[pb], writes=[gt] if half == 1 else [gt])
            bi2, pb2 = pp.get()
            for k in range(8):
                q.do("pe", lambda e, pb2=pb2, k=k, i=i, half=half: e.matmul(
                    pb2[:, :512], attnT[k][:, 128 * i:128 * (i + 1)],
                    WT["wo"][:, k * D + 512 * half:k * D + 512 * half + 512],
                    start=(k == 0), stop=(k == 7)),
                    reads=[attnT[k], WT["wo"]], writes=[pb2], acc=(k > 0))
            q.do("dve", lambda e, pb2=pb2, gt=gt, half=half: e.tensor_mul(
                sgt[:, 512 * half:512 * half + 512], pb2[:, :512],
                gt[:, 512 * half:512 * half + 512]),
                reads=[pb2, gt], writes=[sgt])
            q.do("dve", lambda e, xt=xt, half=half: e.tensor_add(
                xt[:, 512 * half:512 * half + 512], xt[:, 512 * half:512 * half + 512],
                sgt[:, 512 * half:512 * half + 512]),
                reads=[xt, sgt], writes=[xt])
        tout = q.dma("sp", x1own[128 * i:128 * (i + 1), :], xt[:, :],
                     reads=[xt], writes=[x1own])

    if env is not None:
        return x1own
    q.wt("sp", q.last_tok["sp"])
    q.build()
    ctx.close()
    return nc




L1S_IN = [
    ("x1b", [S, D]),
    ("wq_a", [D, 512]), ("wq_b", [D, 512]), ("wk_a", [D, 512]), ("wk_b", [D, 512]),
    ("wv_o", [D, 512]), ("cosl", [128, 512]), ("sinl", [128, 512]),
    ("lrwT", [D, 8]), ("lrb", [8, 1]), ("invr16", [1, 16]), ("lti", [1, 16]),
    ("w1c0", [128, 1024]), ("b1c0", [8, 256]),
    ("kh0", [128, 512]), ("kh1", [128, 512]), ("w2t0", [128, 1024]), ("b2c0", [8, 64]),
    ("selwA", [8, 128]), ("selwB", [8, 128]), ("selzTA", [128, 8]), ("selzTB", [128, 8]),
    ("triu16", [128, 16]), ("gslotA", [128, 64]), ("gslotB", [128, 64]),
    ("gbc", [128, 512]), ("bbc", [128, 512]), ("ident", [128, 128]),
    ("onesr", [1, 512]), ("zeror", [1, 128]),
]

NSTEP = 256
HEFF = 4


def build_l1s(dbg=(), nstep=NSTEP, env=None, x1own=None):
    fused = env is not None
    pfx = "s_" if fused else ""
    if env is None:
        nc = bass.Bass("TRN2", target_bir_lowering=False)
        ctx = contextlib.ExitStack()
        q = Q(nc, ctx)
        pp = Psum(q, nc, ctx)
    else:
        nc, ctx, q, pp = env
    ps = pp.banks

    W = {n: nc.dram_tensor(pfx + n, list(s), F32, kind="ExternalInput")
         for n, s in L1S_IN if not (fused and n == "x1b")}
    if not fused:
        x1b = W["x1b"]
        xn1o = nc.dram_tensor("xn1o", [S, D], F32, kind="ExternalOutput")
        xqro = nc.dram_tensor("xqro", [4, 128, 2 * S], F32, kind="ExternalOutput")
    else:
        xn_own = nc.dram_tensor("xn_own", [OWNQ, D], F32)
        gx1 = nc.dram_tensor("gx1", [2 * OWNQ, D], F32)
        gxn = nc.dram_tensor("gxn", [2 * OWNQ, D], F32)
        xqwo = nc.dram_tensor("xqwo", [S, 512], F32)
        gxq = nc.dram_tensor("gxq", [2 * S, 512], F32)
    z2bo = nc.dram_tensor(pfx + "z2bo", [S, 512], F32,
                          kind="Internal" if fused else "ExternalOutput")
    tgt2_d = nc.dram_tensor(pfx + "tgt2_d", [S, 512], BF16)
    xkpp_d = nc.dram_tensor(pfx + "xkpp_d", [S, 512], BF16)
    lr_d = nc.dram_tensor(pfx + "lr_d", [8, S], F32)
    dbg_o = {n: nc.dram_tensor(n, list(s), F32, kind="ExternalOutput") for n, s in dbg}

    def mksb(c):
        return lambda name, shape, dt=BF16: c.enter_context(
            nc.sbuf_tensor(pfx + name, list(shape), dt))

    sb = mksb(ctx)
    xqkT = [sb(f"xqkT{m}", [128, 2 * S]) for m in range(4)]
    lrS = sb("lrS", [8, S])
    toki = sb("toki", [1, 16], F32)
    t15c = sb("t15c", [128, 1], F32)
    tokbc = sb("tokbc", [128, 16], F32)
    p2A = sb("p2A", [128, 16], F32)
    p2B = sb("p2B", [128, 16], F32)
    idt = sb("idt", [128, 128])
    onesc = sb("onesc", [1, 512])
    ones32 = sb("ones32", [1, 128], F32)
    zeroc = sb("zeroc", [1, 128])
    CON = {}
    for n in ("selwA", "selwB", "selzTA", "selzTB", "triu16", "gslotA", "gslotB",
              "gbc", "bbc"):
        CON[n] = sb("c_" + n, list(W[n].shape),
                    BF16 if n.startswith("sel") else F32)
    WT = {}

    def alloc_w(c, names):
        s_ = mksb(c)
        for n in names:
            R, C = W[n].shape
            WT[n] = s_("w_" + n, [1 if R == 1 else 128, cdiv(R, 128) * C])

    def cast_weights(names):
        for n in names:
            R, C = W[n].shape
            for k in range(cdiv(R, 128)):
                r = min(128, R - 128 * k)
                q.dma("pool", WT[n][:r, k * C:(k + 1) * C],
                      W[n][128 * k:128 * k + r, :], reads=[W[n]], writes=[WT[n]])

    for n, dst in (("ident", idt), ("onesr", onesc), ("zeror", zeroc)):
        q.dma("pool", dst[:, :], W[n][:, :], reads=[W[n]], writes=[dst])
    q.dma("sp", ones32[:, :], W["onesr"][0:1, 0:128], reads=[W["onesr"]], writes=[ones32])
    for n in CON:
        q.dma("pool", CON[n][:, :], W[n][:, :], reads=[W[n]], writes=[CON[n]])

    ivr = sb("ivr", [1, 16], F32)
    lti_t = sb("lti_t", [1, 16], F32)
    q.dma("sp", ivr[:, :], W["invr16"][:, :], reads=[W["invr16"]], writes=[ivr])
    q.dma("sp", lti_t[:, :], W["lti"][:, :], reads=[W["lti"]], writes=[lti_t])
    q.do("dve", lambda e: e.tensor_add(toki[:, :], ivr[:, :], lti_t[:, :]),
         reads=[ivr, lti_t], writes=[toki])
    q.do("dve", lambda e: e.tensor_relu(toki[:, :], toki[:, :]), reads=[toki], writes=[toki])
    bi0, pb0 = pp.get()
    q.do("pe", lambda e, pb=pb0: e.matmul(pb[0:128, 0:16], ones32[0:1, 0:128],
                                          toki[0:1, 0:16], start=True, stop=True),
         reads=[ones32, toki], writes=[pb0])
    q.copy("dve", tokbc[:, :], pb0[0:128, 0:16], reads=[pb0], writes=[tokbc])
    q.do("dve", lambda e, pb=pb0: e.tensor_scalar_mul(
        t15c[:, :], pb[0:128, 15:16], 1.0 / 64.0), reads=[pb0], writes=[t15c])
    for p2 in (p2A, p2B):
        q.do("dve", lambda e, p2=p2: e.tensor_mul(p2[:, :], CON["triu16"][:, :], tokbc[:, :]),
             reads=[CON["triu16"], tokbc], writes=[p2])
        q.do("dve", lambda e, p2=p2: e.tensor_scalar_mul(p2[:, :], p2[:, :], -1.0 / 64.0),
             reads=[p2], writes=[p2])

    s1 = ctx.enter_context(contextlib.ExitStack())
    sb1 = mksb(s1)
    xnT = [sb1(f"xnT{k}", [128, S]) for k in range(8)]
    coslt, sinlt = sb1("coslt", [128, 512]), sb1("sinlt", [128, 512])
    lrbt = sb1("lrbt", [8, 1], F32)
    alloc_w(s1, ["wq_a", "wq_b", "wk_a", "wk_b", "wv_o", "lrwT"])
    cast_weights(["wq_a", "wq_b", "wk_a", "wk_b", "wv_o", "lrwT"])
    q.dma("pool", coslt[:, :], W["cosl"][:, :], reads=[W["cosl"]], writes=[coslt])
    q.dma("pool", sinlt[:, :], W["sinl"][:, :], reads=[W["sinl"]], writes=[sinlt])

    s_ln = contextlib.ExitStack()
    sbl = mksb(s_ln)
    xrow = [sbl(f"xrow{i}", [128, D], F32) for i in range(2)]
    mus = sbl("mus", [128, NB], F32)
    s2s = sbl("s2s", [128, NB], F32)
    vars_ = sbl("vars", [128, NB], F32)
    rstds = sbl("rstds", [128, NB], F32)
    xnrow = [sbl(f"xnrow{i}", [128, D]) for i in range(2)]
    xnrow32 = [sbl("xnrow32", [128, D], F32)] * 2

    nln = 16 if fused else NB
    xsrc = x1own if fused else x1b
    for t in range(nln):
        xt = xrow[t % 2]
        q.dma("sp", xt[:, :], xsrc[128 * t:128 * (t + 1), :], reads=[xsrc], writes=[xt])
        q.do("dve", lambda e, xt=xt, t=t: e.reduce_sum(mus[:, t:t + 1], xt[:, :], axis=AX),
             reads=[xt], writes=[mus], acc=(t > 0))
        q.do("act", lambda e, xt=xt, t=t: e.activation(
            xnrow[t % 2][:, :], xt[:, :], AF.Square, accum_out=s2s[:, t:t + 1]),
            reads=[xt], writes=[xnrow[t % 2], s2s], acc=(t > 1))
    q.do("dve", lambda e: e.tensor_scalar_mul(mus[:, 0:nln], mus[:, 0:nln], 1.0 / D),
         reads=[mus], writes=[mus])
    q.do("dve", lambda e: e.tensor_scalar_mul(s2s[:, 0:nln], s2s[:, 0:nln], 1.0 / D),
         reads=[s2s], writes=[s2s])
    q.do("dve", lambda e: e.scalar_tensor_tensor(vars_[:, 0:nln], mus[:, 0:nln], 0.0,
                                                 mus[:, 0:nln], ALU.add, ALU.mult),
         reads=[mus], writes=[vars_])
    q.do("dve", lambda e: e.tensor_sub(vars_[:, 0:nln], s2s[:, 0:nln], vars_[:, 0:nln]),
         reads=[s2s, vars_], writes=[vars_])
    q.do("dve", lambda e: e.tensor_scalar_add(vars_[:, 0:nln], vars_[:, 0:nln], 1e-5),
         reads=[vars_], writes=[vars_])
    q.do("act", lambda e: e.activation(rstds[:, 0:nln], vars_[:, 0:nln], AF.Sqrt),
         reads=[vars_], writes=[rstds])
    q.do("dve", lambda e: e.reciprocal(rstds[:, 0:nln], rstds[:, 0:nln]),
         reads=[rstds], writes=[rstds])
    for t in range(nln):
        xt, xo, xo32 = xrow[t % 2], xnrow[t % 2], xnrow32[t % 2]
        q.dma("sp", xt[:, :], xsrc[128 * t:128 * (t + 1), :], reads=[xsrc], writes=[xt])
        q.do("dve", lambda e, xt=xt, xo32=xo32, t=t: e.tensor_scalar(
            xo32[:, :], xt[:, :], mus[:, t:t + 1], rstds[:, t:t + 1], ALU.subtract, ALU.mult),
            reads=[xt, mus, rstds], writes=[xo32])
        if fused:
            q.dma("sp", xn_own[128 * t:128 * (t + 1), :], xo32[:, :],
                  reads=[xo32], writes=[xn_own])
            continue
        q.do("dve", lambda e, xt=xt, xo=xo, t=t: e.tensor_scalar(
            xo[:, :], xt[:, :], mus[:, t:t + 1], rstds[:, t:t + 1], ALU.subtract, ALU.mult),
            reads=[xt, mus, rstds], writes=[xo])
        q.dma("sp", xn1o[128 * t:128 * (t + 1), :], xo32[:, :], reads=[xo32], writes=[xn1o])
        for k in range(8):
            sem_name = "dt%d" % k
            q.sem(sem_name)
            if q.cnt[sem_name] > 0:
                q.wt("sp", ("eager", sem_name, q.cnt[sem_name]))
            q.do("sp", lambda e, xo=xo, t=t, k=k: e.dma_start_transpose(
                xnT[k][:, 128 * t:128 * (t + 1)], xo[:, 128 * k:128 * (k + 1)]),
                reads=[xo], writes=[xnT[k]], inc=sem_name, n=16, is_dma=True)
    if fused:
        if nstep < NSTEP:
            s_zz = contextlib.ExitStack()
            zz0 = mksb(s_zz)("zz0", [128, 512], F32)
            q.do("pool", lambda e: e.memset(zz0[:, :], 0.0), reads=[], writes=[zz0])
            for t in range(NB):
                q.dma("pool", z2bo[128 * t:128 * (t + 1), :], zz0[:, :],
                      reads=[zz0], writes=[z2bo])
            q.barrier()
            s_zz.close()
        q.do("pool", lambda e: e.collective_compute(
            "AllGather", ALU.bypass, [[0, 4], [1, 5], [2, 6], [3, 7]],
            [x1own[:, :]], [gx1[:, :]]), reads=[x1own], writes=[gx1])
        q.do("pool", lambda e: e.collective_compute(
            "AllGather", ALU.bypass, [[0, 4], [1, 5], [2, 6], [3, 7]],
            [xn_own[:, :]], [gxn[:, :]]), reads=[xn_own], writes=[gxn])
        for t in range(NB):
            xo = xnrow[t % 2]
            src_off = (t % 2) * OWNQ + 128 * (t // 2)
            q.dma("pool", xo[:, :], gxn[src_off:src_off + 128, :],
                  reads=[gxn], writes=[xo])
            for k in range(8):
                sem_name = "dt%d" % k
                q.sem(sem_name)
                if q.cnt[sem_name] > 0:
                    q.wt("sp", ("eager", sem_name, q.cnt[sem_name]))
                q.do("sp", lambda e, xo=xo, t=t, k=k: e.dma_start_transpose(
                    xnT[k][:, 128 * t:128 * (t + 1)], xo[:, 128 * k:128 * (k + 1)]),
                    reads=[xo], writes=[xnT[k]], inc=sem_name, n=16, is_dma=True)

    s_pro = contextlib.ExitStack()
    ropep = [mksb(s_pro)(f"ropep{i}", [128, 512]) for i in range(2)]
    for m in range(4):
        for (wn_a, wn_b, base) in (("wq_a", "wq_b", 0), ("wk_a", "wk_b", 16)):
            for nj in range(8):
                bia, pa = pp.get()
                bib, pb_ = pp.get()
                for k in range(8):
                    q.do("pe", lambda e, pa=pa, k=k, nj=nj, m=m, wn=wn_a: e.matmul(
                        pa[:, :512], WT[wn][:, k * 512 + 128 * m:k * 512 + 128 * m + 128],
                        xnT[k][:, 512 * nj:512 * (nj + 1)], start=(k == 0), stop=(k == 7)),
                        reads=[WT[wn_a], xnT[k]], writes=[pa], acc=(k > 0))
                for k in range(8):
                    q.do("pe", lambda e, pb_=pb_, k=k, nj=nj, m=m, wn=wn_b: e.matmul(
                        pb_[:, :512], WT[wn][:, k * 512 + 128 * m:k * 512 + 128 * m + 128],
                        xnT[k][:, 512 * nj:512 * (nj + 1)], start=(k == 0), stop=(k == 7)),
                        reads=[WT[wn_b], xnT[k]], writes=[pb_], acc=(k > 0))
                q.do("dve", lambda e, pa=pa, r0=ropep[0]: e.tensor_mul(
                    r0[:, :512], pa[:, :512], coslt[:, 0:512]),
                    reads=[pa, coslt], writes=[ropep[0]])
                q.do("dve", lambda e, pb_=pb_, r1=ropep[1]: e.tensor_mul(
                    r1[:, :512], pb_[:, :512], sinlt[:, 0:512]),
                    reads=[pb_, sinlt], writes=[ropep[1]])
                dst = bass.AP(xqkT[m], 32 * 32 * nj + base, [[2 * S, 128], [32, 32], [1, 16]])
                q.do("dve", lambda e, dst=dst, r0=ropep[0], r1=ropep[1]: e.tensor_add(
                    dst, r0[:, :512].rearrange("p (c i) -> p c i", i=16),
                    r1[:, :512].rearrange("p (c i) -> p c i", i=16)),
                    reads=[ropep[0], ropep[1]], writes=[xqkT[m]])
    if not fused:
        for m in range(4):
            q.dma("pool", xqro[m, :, :], xqkT[m][:, :], reads=[xqkT[m]], writes=[xqro])

    q.dma("sp", lrbt[:, :], W["lrb"][:, :], reads=[W["lrb"]], writes=[lrbt])
    for nj in range(8):
        bi, pb = pp.get()
        for k in range(8):
            q.do("pe", lambda e, pb=pb, k=k, nj=nj: e.matmul(
                pb[0:8, :512], WT["lrwT"][:, k * 8:(k + 1) * 8],
                xnT[k][:, 512 * nj:512 * (nj + 1)], start=(k == 0), stop=(k == 7)),
                reads=[WT["lrwT"], xnT[k]], writes=[pb], acc=(k > 0))
        q.do("act", lambda e, pb=pb, nj=nj: e.activation(
            lrS[0:8, 512 * nj:512 * (nj + 1)], pb[0:8, :512], AF.Sigmoid,
            bias=lrbt[:, :], scale=1.0), reads=[pb, lrbt], writes=[lrS])
    q.dma("pool", lr_d[:, :], lrS[:, :], reads=[lrS], writes=[lr_d])

    q.barrier()
    s_pro.close()
    s_bulk = contextlib.ExitStack()
    sbk = mksb(s_bulk)
    xkro = sbk("xkro", [128, 512])
    xkst = sbk("xkst", [128, 128])
    lrTt = sbk("lrTt", [128, 8], F32)
    bgt = sbk("bgt", [128, 512])
    ttile = sbk("ttile", [128, 512])
    tg2 = sbk("tg2", [128, 512])
    xpp = sbk("xpp", [128, 512])
    q.do("dve", lambda e: e.tensor_mul(bgt[:, :], CON["bbc"][:, :], CON["gbc"][:, :]),
         reads=[CON["bbc"], CON["gbc"]], writes=[bgt])
    for i in range(NB):
        biv, pbv = pp.get()
        for k in range(8):
            q.do("pe", lambda e, pb=pbv, k=k, i=i: e.matmul(
                pb[:, :512], xnT[k][:, 128 * i:128 * (i + 1)],
                WT["wv_o"][:, k * 512:(k + 1) * 512], start=(k == 0), stop=(k == 7)),
                reads=[xnT[k], WT["wv_o"]], writes=[pbv], acc=(k > 0))
        for m in range(4):
            bit, pbt = pp.get()
            inview = bass.AP(xqkT[m], 256 * i + 16,
                             [[2 * S, 128], [32, 8], [1, 16]])
            q.do("dve", lambda e, iv=inview: e.tensor_copy(xkst[:, :], iv),
                 reads=[xqkT[m]], writes=[xkst])
            q.do("pe", lambda e, pbt=pbt: e.transpose(
                pbt.bitcast(BF16)[:, 0:128], xkst[:, :], idt[:, :]),
                reads=[xkst, idt], writes=[pbt])
            q.copy("dve", xkro[:, 128 * m:128 * (m + 1)], pbt.bitcast(BF16)[:, 0:128],
                   reads=[pbt], writes=[xkro])
        bit, pbt = pp.get()
        q.do("pe", lambda e, pbt=pbt, i=i: e.transpose(
            pbt[0:8, 0:128].bitcast(F32) if False else pbt[0:128, 0:8],
            lrS[0:8, 128 * i:128 * (i + 1)], idt[:, :]),
            reads=[lrS, idt], writes=[pbt]) if False else None
        bil, pbl = pp.get()
        q.do("pe", lambda e, pbl=pbl, i=i: e.transpose(
            pbl.bitcast(BF16)[:, 0:8], lrS[0:8, 128 * i:128 * (i + 1)], idt[0:8, 0:8]),
            reads=[lrS, idt], writes=[pbl])
        q.copy("dve", lrTt[:, :], pbl.bitcast(BF16)[:, 0:8], reads=[pbl], writes=[lrTt])
        q.do("dve", lambda e, pb=pbv: e.tensor_sub(ttile[:, :], xkro[:, :], pb[:, :512]),
             reads=[xkro, pbv], writes=[ttile])
        q.do("dve", lambda e: e.tensor_mul(ttile[:, :], ttile[:, :], CON["gbc"][:, :]),
             reads=[ttile, CON["gbc"]], writes=[ttile])
        q.do("dve", lambda e: e.tensor_add(tg2[:, :], ttile[:, :], bgt[:, :]),
             reads=[ttile, bgt], writes=[tg2])
        q.dma("sp", tgt2_d[128 * i:128 * (i + 1), :], tg2[:, :], reads=[tg2], writes=[tgt2_d])
        for p in range(8):
            q.do("dve", lambda e, p=p: e.tensor_scalar(
                xpp[:, 64 * p:64 * (p + 1)], xkro[:, 64 * p:64 * (p + 1)],
                lrTt[:, p:p + 1], t15c[:, :], ALU.mult, ALU.mult),
                reads=[xkro, lrTt, t15c], writes=[xpp], acc=(p > 0))
        q.dma("sp", xkpp_d[128 * i:128 * (i + 1), :], xpp[:, :], reads=[xpp], writes=[xkpp_d])

    q.barrier()
    s_bulk.close()
    s_ln.close()
    s1.close()

    s2 = ctx.enter_context(contextlib.ExitStack())
    sb2 = mksb(s2)
    w1m = sb2("w1m", [128, 1024], F32)
    w1b = sb2("w1b", [128, 1024])
    khm = [sb2(f"khm{h}", [128, 512], F32) for h in range(2)]
    khb = [sb2(f"khb{h}", [128, 512]) for h in range(2)]
    w2tm = sb2("w2tm", [128, 1024], F32)
    w2tb = sb2("w2tb", [128, 1024])
    b1m = sb2("b1m", [8, 256], F32)
    b1bb = sb2("b1bb", [8, 512])
    b2m = sb2("b2m", [8, 64], F32)
    b2bb = sb2("b2bb", [8, 64])
    for dst, srcn in ((w1m, "w1c0"), (khm[0], "kh0"), (khm[1], "kh1"), (w2tm, "w2t0"),
                      (b1m, "b1c0"), (b2m, "b2c0")):
        q.dma("sp", dst[:, :], W[srcn][:, :], reads=[W[srcn]], writes=[dst])
    for dst, srcn in ((w1b, "w1c0"), (khb[0], "kh0"), (khb[1], "kh1"), (w2tb, "w2t0"),
                      (b2bb, "b2c0")):
        q.dma("pool", dst[:, :], W[srcn][:, :], reads=[W[srcn]], writes=[dst])
    q.dma("pool", b1bb[:, 0:256], W["b1c0"][:, :], reads=[W["b1c0"]], writes=[b1bb])
    q.dma("pool", b1bb[:, 256:512], W["b1c0"][:, :], reads=[W["b1c0"]], writes=[b1bb])
    tgs = [[sb2(f"tgs{P}_{r}", [128, 64]) for r in range(2)] for P in range(2)]
    xps = [[sb2(f"xps{P}_{r}", [128, 64]) for r in range(2)] for P in range(2)]
    lrc = [[sb2(f"lrc{P}_{r}", [128, 1], F32) for r in range(2)] for P in range(2)]
    z1sb = [sb2(f"z1sb{P}", [128, 256]) for P in range(2)]
    gu = [sb2(f"gu{P}", [128, 256]) for P in range(2)]
    gt_ = [sb2(f"gt{P}", [128, 256]) for P in range(2)]
    gw = [sb2(f"gw{P}", [128, 256]) for P in range(2)]
    x2sb = [sb2(f"x2sb{P}", [128, 256]) for P in range(2)]
    x2T = [sb2(f"x2T{P}", [128, 256]) for P in range(2)]
    z1bb = [sb2(f"z1bb{P}", [128, 256]) for P in range(2)]
    gub = [sb2(f"gub{P}", [128, 256]) for P in range(2)]
    gtb = [sb2(f"gtb{P}", [128, 256]) for P in range(2)]
    gwb = [sb2(f"gwb{P}", [128, 256]) for P in range(2)]
    x2bb = [sb2(f"x2bb{P}", [128, 256]) for P in range(2)]
    x2bT = [sb2(f"x2bT{P}", [128, 256]) for P in range(2)]
    gz2 = [sb2(f"gz2{P}", [128, 64]) for P in range(2)]
    gz2pp = [sb2(f"gz2pp{P}", [128, 64]) for P in range(2)]
    gz2T = sb2("gz2T", [128, 128])
    gz1 = [sb2(f"gz1{P}", [128, 256]) for P in range(2)]
    gz1pp = [sb2(f"gz1pp{P}", [128, 256]) for P in range(2)]
    statm = [sb2(f"statm{P}", [128, 16]) for P in range(2)]
    statm2 = [sb2(f"statm2{P}", [128, 16]) for P in range(2)]
    xcf = [sb2(f"xcf{P}", [128, 64], F32) for P in range(2)]
    xhf = [sb2(f"xhf{P}", [128, 64], F32) for P in range(2)]
    gof = [sb2(f"gof{P}", [128, 64], F32) for P in range(2)]
    scrf = [sb2(f"scrf{P}", [128, 64], F32) for P in range(2)]
    g2s = [sb2(f"g2s{P}", [128, 64], F32) for P in range(2)]
    mcol = [sb2(f"mcol{P}", [128, 1], F32) for P in range(2)]
    vcol = [sb2(f"vcol{P}", [128, 1], F32) for P in range(2)]
    scol = [sb2(f"scol{P}", [128, 1], F32) for P in range(2)]
    rstc = [sb2(f"rstc{P}", [128, 1], F32) for P in range(2)]
    hcol = [sb2(f"hcol{P}", [128, 1], F32) for P in range(2)]
    acol = [sb2(f"acol{P}", [128, 1], F32) for P in range(2)]
    gsum = [sb2(f"gsum{P}", [128, 1], F32) for P in range(2)]
    gxs = [sb2(f"gxs{P}", [128, 1], F32) for P in range(2)]
    z2es = [sb2(f"z2es{P}", [128, 64], F32) for P in range(2)]

    P2 = {0: p2A, 1: p2B}
    GS = {0: CON["gslotA"], 1: CON["gslotB"]}
    SELZT = {0: CON["selzTA"], 1: CON["selzTB"]}
    SELW = {0: CON["selwA"], 1: CON["selwB"]}
    for P in range(2):
        for rr_ in range(2):
            for tile in (tgs[P][rr_], xps[P][rr_], lrc[P][rr_]):
                q.do("pool", lambda e, tl=tile: e.memset(tl[:, :], 0.0),
                     reads=[], writes=[tile])
    for P in range(2):
        q.do("dve", lambda e, P=P: e.tensor_mul(g2s[P][:, :], GS[P][:, :], GS[P][:, :]),
             reads=[GS[P]], writes=[g2s[P]])

    def _slot_dma(dst_tile, src_d, base_off, jstride, width, t, P, semb):
        sem_name = "%s%d" % (semb, P)
        q.sem(sem_name)
        for j in range(4):
            if q.cnt[sem_name] > 0:
                q.wt("sp", ("eager", sem_name, q.cnt[sem_name]))
            tok = q.do("sp", lambda e, j=j, dst=dst_tile, src=src_d, o=base_off, js=jstride,
                       w=width: e.dma_start(
                           dst[32 * j:32 * j + 16, 0:w],
                           bass.AP(src, o + j * js, [[512 if w == 64 else 1, 16], [1, w]])),
                       reads=[src_d], writes=[dst_tile], inc=sem_name,
                       n=16, is_dma=True, acc=(j > 0))

    def stream_loads(t):
        r = t % 2
        for P in range(2):
            off = 16 * t * 512 + 64 * P
            _slot_dma(tgs[P][r], tgt2_d, off, 128, 64, t, P, "dg")
            _slot_dma(xps[P][r], xkpp_d, off, 128, 64, t, P, "dx")
            _slot_dma(lrc[P][r], lr_d, P * S + 16 * t, 2 * S, 1, t, P, "dl")

    def gelu(P, zin, u, tt, w, xout):
        q.do("dve", lambda e, z=zin, u=u: e.tensor_mul(u[:, :], z[:, :], z[:, :]),
             reads=[zin], writes=[u])
        q.do("dve", lambda e, z=zin, u=u, w=w: e.scalar_tensor_tensor(
            w[:, :], u[:, :], 0.044715, z[:, :], ALU.mult, ALU.mult),
            reads=[u, zin], writes=[w])
        q.do("dve", lambda e, z=zin, w=w: e.tensor_add(w[:, :], w[:, :], z[:, :]),
             reads=[w, zin], writes=[w])
        q.do("act", lambda e, w=w, tt=tt: e.activation(
            tt[:, :], w[:, :], AF.Tanh, scale=0.79788456), reads=[w], writes=[tt])
        q.do("dve", lambda e, w=w, tt=tt: e.tensor_scalar(
            w[:, :], tt[:, :], 0.5, 0.5, ALU.mult, ALU.add), reads=[tt], writes=[w])
        q.do("dve", lambda e, z=zin, w=w, xo=xout: e.tensor_mul(xo[:, :], w[:, :], z[:, :]),
             reads=[w, zin], writes=[xout])

    def gelu_bwd(P, zin, u, tt, w, bout):
        q.do("dve", lambda e, tt=tt, w=w: e.tensor_mul(w[:, :], tt[:, :], tt[:, :]),
             reads=[tt], writes=[w])
        q.do("dve", lambda e, w=w: e.tensor_scalar(
            w[:, :], w[:, :], -1.0, 1.0, ALU.mult, ALU.add), reads=[w], writes=[w])
        q.do("dve", lambda e, u=u, b=bout: e.tensor_scalar(
            b[:, :], u[:, :], 0.05351611215, 0.39894228, ALU.mult, ALU.add),
            reads=[u], writes=[bout])
        q.do("dve", lambda e, w=w, b=bout: e.tensor_mul(b[:, :], b[:, :], w[:, :]),
             reads=[bout, w], writes=[bout])
        q.do("dve", lambda e, z=zin, b=bout: e.tensor_mul(b[:, :], b[:, :], z[:, :]),
             reads=[bout, zin], writes=[bout])
        q.do("dve", lambda e, tt=tt, w=w: e.tensor_scalar(
            w[:, :], tt[:, :], 0.5, 0.5, ALU.mult, ALU.add), reads=[tt], writes=[w])
        q.do("dve", lambda e, w=w, b=bout: e.tensor_add(b[:, :], b[:, :], w[:, :]),
             reads=[bout, w], writes=[bout])

    stream_loads(0)
    for t in range(nstep):
        r = t % 2
        if t + 1 < nstep:
            stream_loads(t + 1)
        bz1 = {0: ps[0], 1: ps[1]}
        bC = {0: ps[2], 1: ps[3]}
        bz2 = {0: ps[4], 1: ps[5]}
        for P in range(2):
            q.do("pe", lambda e, P=P: e.matmul(
                bz1[P][:, 0:512], SELW[P][:, :], b1bb[:, :], start=True, stop=False,
                skip_group_check=True),
                reads=[SELW[P], b1bb], writes=[bz1[P]])
            for j in range(4):
                p = 2 * j + P
                q.do("pe", lambda e, P=P, j=j, p=p, t=t: e.matmul(
                    bz1[P][32 * j:32 * j + 16, 0:256],
                    xqkT[p // 2][64 * (p % 2):64 * (p % 2) + 64,
                                 32 * t + 16:32 * t + 32],
                    w1b[64 * P:64 * P + 64, 256 * j:256 * (j + 1)],
                    start=False, stop=False, tile_position=(64 * P, 32 * j),
                    skip_group_check=True),
                    reads=[xqkT[p // 2], w1b], writes=[bz1[P]], acc=True)
                q.do("pe", lambda e, P=P, j=j, p=p, t=t: e.matmul(
                    bz1[P][32 * j:32 * j + 16, 256:512],
                    xqkT[p // 2][64 * (p % 2):64 * (p % 2) + 64,
                                 32 * t:32 * t + 16],
                    w1b[64 * P:64 * P + 64, 256 * j:256 * (j + 1)],
                    start=False, stop=(j == 3), tile_position=(64 * P, 32 * j),
                    skip_group_check=True),
                    reads=[xqkT[p // 2], w1b], writes=[bz1[P]], acc=True)
        for P in range(2):
            q.do("pe", lambda e, P=P: e.matmul(
                bC[P][:, 0:16], onesc[0:1, 0:128], onesc[0:1, 0:16], start=True, stop=False,
                skip_group_check=True),
                reads=[onesc], writes=[bC[P]])
            for j in range(4):
                p = 2 * j + P
                q.do("pe", lambda e, P=P, j=j, p=p, t=t: e.matmul(
                    bC[P][32 * j:32 * j + 16, 0:16],
                    xqkT[p // 2][64 * (p % 2):64 * (p % 2) + 64,
                                 32 * t + 16:32 * t + 32],
                    xqkT[p // 2][64 * (p % 2):64 * (p % 2) + 64, 32 * t:32 * t + 16],
                    start=False, stop=(j == 3), tile_position=(64 * P, 32 * j),
                    skip_group_check=True),
                    reads=[xqkT[p // 2]], writes=[bC[P]], acc=True)
        for P in range(2):
            q.copy("dve", z1sb[P][:, :], bz1[P][:, 0:256], reads=[bz1[P]], writes=[z1sb[P]])
            gelu(P, z1sb[P], gu[P], gt_[P], gw[P], x2sb[P])
            for h in range(2):
                bi6, pb6 = pp.get(6)
                q.do("pe", lambda e, P=P, h=h, pb6=pb6: e.transpose(
                    pb6.bitcast(BF16)[:, 0:128], x2sb[P][:, 128 * h:128 * (h + 1)],
                    idt[:, :]), reads=[x2sb[P], idt], writes=[pb6])
                q.copy("dve", x2T[P][:, 128 * h:128 * (h + 1)], pb6.bitcast(BF16)[:, 0:128],
                       reads=[pb6], writes=[x2T[P]])
        for P in range(2):
            q.do("pe", lambda e, P=P: e.matmul(
                bz2[P][:, 0:64], SELW[P][:, :], b2bb[:, :], start=True, stop=False,
                skip_group_check=True),
                reads=[SELW[P], b2bb], writes=[bz2[P]])
            for j in range(4):
                for h in range(2):
                    q.do("pe", lambda e, P=P, j=j, h=h: e.matmul(
                        bz2[P][32 * j:32 * j + 16, 0:64],
                        x2T[P][:, 128 * h + 32 * j:128 * h + 32 * j + 16],
                        khb[h][:, 64 * (2 * j + P):64 * (2 * j + P) + 64],
                        start=False, stop=(j == 3 and h == 1), tile_position=(0, 32 * j),
                        skip_group_check=True),
                        reads=[x2T[P], khb[h]], writes=[bz2[P]], acc=True)
        for P in range(2):
            q.do("dve", lambda e, P=P: e.reduce_sum(mcol[P][:, :], bz2[P][:, 0:64], axis=AX),
                 reads=[bz2[P]], writes=[mcol[P]])
            q.do("dve", lambda e, P=P: e.tensor_scalar_mul(
                mcol[P][:, :], mcol[P][:, :], 1.0 / 64.0), reads=[mcol[P]], writes=[mcol[P]])
            q.do("dve", lambda e, P=P: e.tensor_scalar(
                xcf[P][:, :], bz2[P][:, 0:64], mcol[P][:, :], None, ALU.subtract),
                reads=[bz2[P], mcol[P]], writes=[xcf[P]])
            q.do("dve", lambda e, P=P: e.scalar_tensor_tensor(
                scrf[P][:, :], xcf[P][:, :], 0.0, xcf[P][:, :], ALU.add, ALU.mult,
                accum_out=vcol[P][:, :]), reads=[xcf[P]], writes=[scrf[P], vcol[P]])
            q.do("dve", lambda e, P=P: e.tensor_scalar(
                vcol[P][:, :], vcol[P][:, :], 1.0 / 64.0, 1e-6, ALU.mult, ALU.add),
                reads=[vcol[P]], writes=[vcol[P]])
            q.do("dve", lambda e, P=P: e.reciprocal(scol[P][:, :], vcol[P][:, :]),
                 reads=[vcol[P]], writes=[scol[P]])
            q.do("act", lambda e, P=P: e.activation(
                scrf[P][:, :], xcf[P][:, :], AF.Abs, accum_out=acol[P][:, :]),
                reads=[xcf[P]], writes=[scrf[P], acol[P]])
            q.do("dve", lambda e, P=P: e.tensor_scalar(
                acol[P][:, :], acol[P][:, :], 1.0 / 64.0, 8e-4, ALU.mult, ALU.add),
                reads=[acol[P]], writes=[acol[P]])
            q.do("dve", lambda e, P=P: e.reciprocal(rstc[P][:, :], acol[P][:, :]),
                 reads=[acol[P]], writes=[rstc[P]])
            q.do("dve", lambda e, P=P: e.tensor_scalar_mul(
                rstc[P][:, :], rstc[P][:, :], 1.2533141), reads=[rstc[P]], writes=[rstc[P]])
            for it in range(HEFF):
                q.do("dve", lambda e, P=P: e.reciprocal(hcol[P][:, :], rstc[P][:, :]),
                     reads=[rstc[P]], writes=[hcol[P]])
                q.do("dve", lambda e, P=P: e.tensor_mul(
                    hcol[P][:, :], hcol[P][:, :], scol[P][:, :]),
                    reads=[hcol[P], scol[P]], writes=[hcol[P]])
                q.do("dve", lambda e, P=P: e.tensor_add(
                    rstc[P][:, :], rstc[P][:, :], hcol[P][:, :]),
                    reads=[rstc[P], hcol[P]], writes=[rstc[P]])
                q.do("dve", lambda e, P=P: e.tensor_scalar_mul(
                    rstc[P][:, :], rstc[P][:, :], 0.5), reads=[rstc[P]], writes=[rstc[P]])
            q.do("dve", lambda e, P=P: e.tensor_scalar(
                xhf[P][:, :], xcf[P][:, :], rstc[P][:, :], None, ALU.mult),
                reads=[xcf[P], rstc[P]], writes=[xhf[P]])
            q.do("dve", lambda e, P=P: e.tensor_mul(gof[P][:, :], g2s[P][:, :], xhf[P][:, :]),
                 reads=[g2s[P], xhf[P]], writes=[gof[P]])
            q.do("dve", lambda e, P=P, r=r: e.scalar_tensor_tensor(
                gof[P][:, :], gof[P][:, :], 1.0, tgs[P][r][:, :], ALU.mult, ALU.add,
                accum_out=gsum[P][:, :]),
                reads=[gof[P], tgs[P][r]], writes=[gof[P], gsum[P]])
            q.do("dve", lambda e, P=P: e.scalar_tensor_tensor(
                scrf[P][:, :], gof[P][:, :], 1.0, xhf[P][:, :], ALU.mult, ALU.mult,
                accum_out=gxs[P][:, :]),
                reads=[gof[P], xhf[P]], writes=[scrf[P], gxs[P]])
            q.do("dve", lambda e, P=P: e.tensor_scalar(
                scrf[P][:, :], gof[P][:, :], 64.0, gsum[P][:, :], ALU.mult, ALU.subtract),
                reads=[gof[P], gsum[P]], writes=[scrf[P]])
            q.do("dve", lambda e, P=P: e.tensor_scalar(
                xhf[P][:, :], xhf[P][:, :], gxs[P][:, :], None, ALU.mult),
                reads=[xhf[P], gxs[P]], writes=[xhf[P]])
            q.do("dve", lambda e, P=P: e.tensor_sub(scrf[P][:, :], scrf[P][:, :], xhf[P][:, :]),
                 reads=[scrf[P], xhf[P]], writes=[scrf[P]])
            q.do("dve", lambda e, P=P: e.tensor_scalar(
                gz2[P][:, :], scrf[P][:, :], rstc[P][:, :], 1.0 / 64.0, ALU.mult, ALU.mult),
                reads=[scrf[P], rstc[P]], writes=[gz2[P]])
            q.do("dve", lambda e, P=P, r=r: e.tensor_scalar(
                gz2pp[P][:, :], gz2[P][:, :], lrc[P][r][:, :], t15c[:, :], ALU.mult, ALU.mult),
                reads=[gz2[P], lrc[P][r], t15c], writes=[gz2pp[P]])
            bi6, pb6 = pp.get(6)
            q.do("pe", lambda e, P=P, pb6=pb6: e.transpose(
                pb6.bitcast(BF16)[64 * P:64 * P + 64, 0:128], gz2[P][:, 0:64], idt[:, :]),
                reads=[gz2[P], idt], writes=[pb6])
            q.copy("dve", gz2T[64 * P:64 * P + 64, :],
                   pb6.bitcast(BF16)[64 * P:64 * P + 64, 0:128],
                   reads=[pb6], writes=[gz2T])
        for P in range(2):
            bg = ps[7]
            q.do("pe", lambda e, P=P, bg=bg: e.matmul(
                bg[:, 256 * P:256 * P + 256], zeroc[0:1, 0:128], onesc[0:1, 0:256],
                start=True, stop=False, skip_group_check=True),
                reads=[zeroc, onesc], writes=[bg])
            for j in range(4):
                q.do("pe", lambda e, P=P, j=j, bg=bg: e.matmul(
                    bg[32 * j:32 * j + 16, 256 * P:256 * P + 256],
                    gz2T[64 * P:64 * P + 64, 32 * j:32 * j + 16],
                    w2tb[64 * P:64 * P + 64, 256 * j:256 * (j + 1)],
                    start=False, stop=(j == 3), tile_position=(64 * P, 32 * j),
                    skip_group_check=True),
                    reads=[gz2T, w2tb], writes=[bg], acc=True)
            gelu_bwd(P, z1sb[P], gu[P], gt_[P], gw[P], gz1[P])
            q.do("dve", lambda e, P=P, bg=bg: e.tensor_mul(
                gz1[P][:, :], gz1[P][:, :], bg[:, 256 * P:256 * P + 256]),
                reads=[gz1[P], bg], writes=[gz1[P]])
            q.do("dve", lambda e, P=P, r=r: e.tensor_scalar(
                gz1pp[P][:, :], gz1[P][:, :], lrc[P][r][:, :], t15c[:, :], ALU.mult, ALU.mult),
                reads=[gz1[P], lrc[P][r], t15c], writes=[gz1pp[P]])
        for P in range(2):
            q.do("dve", lambda e, P=P: e.tensor_mul(
                statm[P][:, :], bC[P][:, 0:16], P2[P][:, :]),
                reads=[bC[P], P2[P]], writes=[statm[P]])
            q.do("dve", lambda e, P=P, r=r: e.tensor_scalar(
                statm[P][:, :], statm[P][:, :], lrc[P][r][:, :], None, ALU.mult),
                reads=[statm[P], lrc[P][r]], writes=[statm[P]])
            bmix = ps[7]
            q.do("pe", lambda e, P=P, bmix=bmix: e.matmul(
                bmix[:, 256 * P:256 * P + 256], zeroc[0:1, 0:128], onesc[0:1, 0:256],
                start=True, stop=False, skip_group_check=True),
                reads=[zeroc, onesc], writes=[bmix])
            for j in range(4):
                q.do("pe", lambda e, P=P, j=j, bmix=bmix: e.matmul(
                    bmix[32 * j:32 * j + 16, 256 * P:256 * P + 256],
                    statm[P][32 * j:32 * j + 16, 0:16],
                    gz1[P][32 * j:32 * j + 16, 0:256],
                    start=False, stop=(j == 3), tile_position=(32 * j, 32 * j),
                    skip_group_check=True),
                    reads=[statm[P], gz1[P]], writes=[bmix], acc=True)
        for P in range(2):
            q.copy("dve", z1bb[P][:, :], bz1[P][:, 256:512], reads=[bz1[P]], writes=[z1bb[P]])
            q.do("dve", lambda e, P=P, bmix=bmix: e.tensor_add(
                z1bb[P][:, :], z1bb[P][:, :], bmix[:, 256 * P:256 * P + 256]),
                reads=[z1bb[P], bmix], writes=[z1bb[P]])
            gelu(P, z1bb[P], gub[P], gtb[P], gwb[P], x2bb[P])
            for h in range(2):
                bi6, pb6 = pp.get(6)
                q.do("pe", lambda e, P=P, h=h, pb6=pb6: e.transpose(
                    pb6.bitcast(BF16)[:, 0:128], x2bb[P][:, 128 * h:128 * (h + 1)],
                    idt[:, :]), reads=[x2bb[P], idt], writes=[pb6])
                q.copy("dve", x2bT[P][:, 128 * h:128 * (h + 1)], pb6.bitcast(BF16)[:, 0:128],
                       reads=[pb6], writes=[x2bT[P]])
        for P in range(2):
            q.do("pe", lambda e, P=P: e.matmul(
                bC[P][:, 16:32], onesc[0:1, 0:128], onesc[0:1, 0:16], start=True, stop=False,
                skip_group_check=True),
                reads=[onesc], writes=[bC[P]])
            for j in range(4):
                for h in range(2):
                    q.do("pe", lambda e, P=P, j=j, h=h: e.matmul(
                        bC[P][32 * j:32 * j + 16, 16:32],
                        x2T[P][:, 128 * h + 32 * j:128 * h + 32 * j + 16],
                        x2bT[P][:, 128 * h + 32 * j:128 * h + 32 * j + 16],
                        start=False, stop=(j == 3 and h == 1), tile_position=(0, 32 * j),
                        skip_group_check=True),
                        reads=[x2T[P], x2bT[P]], writes=[bC[P]], acc=True)
        for P in range(2):
            q.do("pe", lambda e, P=P: e.matmul(
                bz2[P][:, 128:192], SELW[P][:, :], b2bb[:, :], start=True, stop=False,
                skip_group_check=True),
                reads=[SELW[P], b2bb], writes=[bz2[P]])
            for j in range(4):
                for h in range(2):
                    q.do("pe", lambda e, P=P, j=j, h=h: e.matmul(
                        bz2[P][32 * j:32 * j + 16, 128:192],
                        x2bT[P][:, 128 * h + 32 * j:128 * h + 32 * j + 16],
                        khb[h][:, 64 * (2 * j + P):64 * (2 * j + P) + 64],
                        start=False, stop=False, tile_position=(0, 32 * j),
                        skip_group_check=True),
                        reads=[x2bT[P], khb[h]], writes=[bz2[P]], acc=True)
        for P in range(2):
            q.do("dve", lambda e, P=P: e.tensor_mul(
                statm2[P][:, :], bC[P][:, 16:32], P2[P][:, :]),
                reads=[bC[P], P2[P]], writes=[statm2[P]])
            q.do("dve", lambda e, P=P, r=r: e.tensor_scalar(
                statm2[P][:, :], statm2[P][:, :], lrc[P][r][:, :], None, ALU.mult),
                reads=[statm2[P], lrc[P][r]], writes=[statm2[P]])
            for j in range(4):
                q.do("pe", lambda e, P=P, j=j: e.matmul(
                    bz2[P][32 * j:32 * j + 16, 128:192],
                    statm2[P][32 * j:32 * j + 16, 0:16],
                    gz2[P][32 * j:32 * j + 16, 0:64],
                    start=False, stop=(j == 3), tile_position=(32 * j, 32 * j),
                    skip_group_check=True),
                    reads=[statm2[P], gz2[P]], writes=[bz2[P]], acc=True)
        for P in range(2):
            q.copy("dve", z2es[P][:, :], bz2[P][:, 128:192], reads=[bz2[P]], writes=[z2es[P]])
            for j in range(4):
                p = 2 * j + P
                q.dma("pool", bass.AP(z2bo, (16 * t) * 512 + 64 * p, [[512, 16], [1, 64]]),
                      z2es[P][32 * j:32 * j + 16, 0:64], reads=[z2es[P]], writes=[z2bo])
        bu1, bu2 = ps[0], ps[1]
        for P in range(2):
            for j in range(4):
                bu = bu1 if j < 2 else bu2
                q.do("pe", lambda e, P=P, j=j, r=t % 2, bu=bu: e.matmul(
                    bu[64 * P:64 * P + 64, 256 * (j % 2):256 * (j % 2) + 256],
                    xps[P][r][32 * j:32 * j + 16, 0:64],
                    gz1[P][32 * j:32 * j + 16, 0:256],
                    start=True, stop=True, tile_position=(32 * j, 64 * P)),
                    reads=[xps[P][t % 2], gz1[P]], writes=[bu])
        q.do("dve", lambda e: e.scalar_tensor_tensor(
            w1m[:, 0:512], bu1[:, 0:512], -1.0, w1m[:, 0:512], ALU.mult, ALU.add),
            reads=[bu1, w1m], writes=[w1m])
        q.do("dve", lambda e: e.scalar_tensor_tensor(
            w1m[:, 512:1024], bu2[:, 0:512], -1.0, w1m[:, 512:1024], ALU.mult, ALU.add),
            reads=[bu2, w1m], writes=[w1m])
        q.copy("pool", w1b[:, 0:512], w1m[:, 0:512], reads=[w1m], writes=[w1b])
        q.copy("pool", w1b[:, 512:1024], w1m[:, 512:1024], reads=[w1m], writes=[w1b])
        bk0, bk1 = ps[4], ps[5]
        for P in range(2):
            for j in range(4):
                p = 2 * j + P
                for h, bk in ((0, bk0), (1, bk1)):
                    q.do("pe", lambda e, P=P, j=j, p=p, h=h, bk=bk: e.matmul(
                        bk[:, 64 * p:64 * p + 64],
                        x2sb[P][32 * j:32 * j + 16, 128 * h:128 * h + 128],
                        gz2pp[P][32 * j:32 * j + 16, 0:64],
                        start=True, stop=True, tile_position=(32 * j, 0)),
                        reads=[x2sb[P], gz2pp[P]], writes=[bk])
        for h, bk in ((0, bk0), (1, bk1)):
            q.do("dve", lambda e, h=h, bk=bk: e.scalar_tensor_tensor(
                khm[h][:, :], bk[:, 0:512], -1.0, khm[h][:, :], ALU.mult, ALU.add),
                reads=[bk, khm[h]], writes=[khm[h]])
            q.copy("pool", khb[h][:, :], khm[h][:, :], reads=[khm[h]], writes=[khb[h]])
        bt0, bt1 = ps[2], ps[3]
        for P in range(2):
            for j in range(4):
                p = 2 * j + P
                bt = bt0 if j < 2 else bt1
                q.do("pe", lambda e, P=P, j=j, p=p, bt=bt: e.matmul(
                    bt[64 * P:64 * P + 64, 256 * (j % 2):256 * (j % 2) + 256],
                    gz2pp[P][32 * j:32 * j + 16, 0:64],
                    x2sb[P][32 * j:32 * j + 16, 0:256],
                    start=True, stop=True, tile_position=(32 * j, 64 * P)),
                    reads=[gz2pp[P], x2sb[P]], writes=[bt])
        q.do("dve", lambda e: e.scalar_tensor_tensor(
            w2tm[:, 0:512], bt0[:, 0:512], -1.0, w2tm[:, 0:512], ALU.mult, ALU.add),
            reads=[bt0, w2tm], writes=[w2tm])
        q.do("dve", lambda e: e.scalar_tensor_tensor(
            w2tm[:, 512:1024], bt1[:, 0:512], -1.0, w2tm[:, 512:1024], ALU.mult, ALU.add),
            reads=[bt1, w2tm], writes=[w2tm])
        q.copy("pool", w2tb[:, 0:512], w2tm[:, 0:512], reads=[w2tm], writes=[w2tb])
        q.copy("pool", w2tb[:, 512:1024], w2tm[:, 512:1024], reads=[w2tm], writes=[w2tb])
        bb = ps[6]
        for P in range(2):
            q.do("pe", lambda e, P=P, bb=bb: e.matmul(
                bb[0:8, 0:256], SELZT[P][:, :], gz1pp[P][:, 0:256],
                start=(P == 0), stop=(P == 1), skip_group_check=True),
                reads=[SELZT[P], gz1pp[P]], writes=[bb], acc=(P == 1))
        for P in range(2):
            q.do("pe", lambda e, P=P, bb=bb: e.matmul(
                bb[0:8, 256:320], SELZT[P][:, :], gz2pp[P][:, 0:64],
                start=(P == 0), stop=(P == 1), skip_group_check=True),
                reads=[SELZT[P], gz2pp[P]], writes=[bb], acc=(P == 1))
        q.do("dve", lambda e, bb=bb: e.scalar_tensor_tensor(
            b1m[:, :], bb[0:8, 0:256], -1.0, b1m[:, :], ALU.mult, ALU.add),
            reads=[bb, b1m], writes=[b1m])
        q.do("dve", lambda e, bb=bb: e.scalar_tensor_tensor(
            b2m[:, :], bb[0:8, 256:320], -1.0, b2m[:, :], ALU.mult, ALU.add),
            reads=[bb, b2m], writes=[b2m])
        q.copy("pool", b1bb[:, 0:256], b1m[:, :], reads=[b1m], writes=[b1bb])
        q.copy("pool", b1bb[:, 256:512], b1m[:, :], reads=[b1m], writes=[b1bb])
        q.copy("pool", b2bb[:, :], b2m[:, :], reads=[b2m], writes=[b2bb])

    if dbg_o:
        dbgmap = {"d_w1m": (w1m, 128, 1024), "d_gz1A": (gz1[0], 128, 256),
                  "d_x2A": (x2sb[0], 128, 256), "d_lrS": (lrS, 8, 512),
                  "d_xqk0": (xqkT[0], 128, 512),
                  "d_khm0": (khm[0], 128, 512), "d_khm1": (khm[1], 128, 512),
                  "d_w2tm": (w2tm, 128, 1024), "d_b1m": (b1m, 8, 256),
                  "d_b2m": (b2m, 8, 64), "d_gz2A": (gz2[0], 128, 64),
                  "d_tgsA": (tgs[0][0], 128, 64), "d_xpsA": (xps[0][0], 128, 64),
                  "d_z1bbA": (z1bb[0], 128, 256),
                  "d_tgsA1": (tgs[0][1], 128, 64), "d_xpsA1": (xps[0][1], 128, 64),
                  "d_lrcA1": (lrc[0][1], 128, 1), "d_w1b": (w1b, 128, 1024),
                  "d_b1bb": (b1bb, 8, 512), "d_z1sbA": (z1sb[0], 128, 256),
                  "d_xcfA": (xcf[0], 128, 64), "d_rstA": (rstc[0], 128, 1),
                  "d_gofA": (gof[0], 128, 64), "d_xhfA": (xhf[0], 128, 64)}
        for nname, (tt_, rr, cc) in dbgmap.items():
            if nname in dbg_o:
                q.dma("pool", dbg_o[nname][:, :], tt_[:rr, 0:cc],
                      reads=[tt_], writes=[dbg_o[nname]])
    if fused:
        q.barrier()
        s2.close()
        sp_ = contextlib.ExitStack()
        sbp = mksb(sp_)
        z2t = [sbp(f"pz2t{r}", [128, 512], F32) for r in range(2)]
        xqs = sbp("pxqs", [128, 128])
        xqt = sbp("pxqt", [128, 512], F32)
        pxc = sbp("pxc", [128, 512], F32)
        mu8 = sbp("pmu8", [128, 8], F32)
        v8 = sbp("pv8", [128, 8], F32)
        a8 = sbp("pa8", [128, 8], F32)
        sc8 = sbp("psc8", [128, 8], F32)
        rs8 = sbp("prs8", [128, 8], F32)
        h8 = sbp("ph8", [128, 8], F32)
        sq8 = sbp("psq8", [128, 64], F32)
        ab8 = sbp("pab8", [128, 512], F32)
        for i in range(NB):
            zt = z2t[i % 2]
            q.dma("sp", zt[:, :], z2bo[128 * i:128 * (i + 1), :],
                  reads=[z2bo], writes=[zt])
            for m in range(4):
                iv = bass.AP(xqkT[m], 256 * i, [[2 * S, 128], [32, 8], [1, 16]])
                q.do("dve", lambda e, iv=iv: e.tensor_copy(xqs[:, :], iv),
                     reads=[xqkT[m]], writes=[xqs])
                bi, pb = pp.get()
                q.do("pe", lambda e, pb=pb: e.transpose(
                    pb.bitcast(BF16)[:, 0:128], xqs[:, :], idt[:, :]),
                    reads=[xqs, idt], writes=[pb])
                q.copy("dve", xqt[:, 128 * m:128 * (m + 1)], pb.bitcast(BF16)[:, 0:128],
                       reads=[pb], writes=[xqt])
            for g in range(8):
                q.do("dve", lambda e, g=g, zt=zt: e.reduce_sum(
                    mu8[:, g:g + 1], zt[:, 64 * g:64 * (g + 1)], axis=AX),
                    reads=[zt], writes=[mu8], acc=(g > 0))
            q.do("dve", lambda e: e.tensor_scalar_mul(mu8[:, :], mu8[:, :], 1.0 / 64.0),
                 reads=[mu8], writes=[mu8])
            for g in range(8):
                q.do("dve", lambda e, g=g, zt=zt: e.tensor_scalar(
                    pxc[:, 64 * g:64 * (g + 1)], zt[:, 64 * g:64 * (g + 1)],
                    mu8[:, g:g + 1], None, ALU.subtract),
                    reads=[zt, mu8], writes=[pxc], acc=(g > 0))
            for g in range(8):
                q.do("dve", lambda e, g=g: e.scalar_tensor_tensor(
                    sq8[:, :], pxc[:, 64 * g:64 * (g + 1)], 0.0,
                    pxc[:, 64 * g:64 * (g + 1)], ALU.add, ALU.mult,
                    accum_out=v8[:, g:g + 1]),
                    reads=[pxc], writes=[sq8, v8])
            for g in range(8):
                q.do("act", lambda e, g=g: e.activation(
                    ab8[:, 64 * g:64 * (g + 1)], pxc[:, 64 * g:64 * (g + 1)], AF.Abs,
                    accum_out=a8[:, g:g + 1]),
                    reads=[pxc], writes=[ab8, a8], acc=(g > 0))
            q.do("dve", lambda e: e.tensor_scalar(
                v8[:, :], v8[:, :], 1.0 / 64.0, 1e-6, ALU.mult, ALU.add),
                reads=[v8], writes=[v8])
            q.do("dve", lambda e: e.reciprocal(sc8[:, :], v8[:, :]),
                 reads=[v8], writes=[sc8])
            q.do("dve", lambda e: e.tensor_scalar(
                a8[:, :], a8[:, :], 1.0 / 64.0, 8e-4, ALU.mult, ALU.add),
                reads=[a8], writes=[a8])
            q.do("dve", lambda e: e.reciprocal(rs8[:, :], a8[:, :]),
                 reads=[a8], writes=[rs8])
            q.do("dve", lambda e: e.tensor_scalar_mul(rs8[:, :], rs8[:, :], 1.2533141),
                 reads=[rs8], writes=[rs8])
            for _ in range(HEFF):
                q.do("dve", lambda e: e.reciprocal(h8[:, :], rs8[:, :]),
                     reads=[rs8], writes=[h8])
                q.do("dve", lambda e: e.tensor_mul(h8[:, :], h8[:, :], sc8[:, :]),
                     reads=[h8, sc8], writes=[h8])
                q.do("dve", lambda e: e.tensor_add(rs8[:, :], rs8[:, :], h8[:, :]),
                     reads=[rs8, h8], writes=[rs8])
                q.do("dve", lambda e: e.tensor_scalar_mul(rs8[:, :], rs8[:, :], 0.5),
                     reads=[rs8], writes=[rs8])
            for g in range(8):
                q.do("dve", lambda e, g=g: e.tensor_scalar(
                    pxc[:, 64 * g:64 * (g + 1)], pxc[:, 64 * g:64 * (g + 1)],
                    rs8[:, g:g + 1], None, ALU.mult),
                    reads=[pxc, rs8], writes=[pxc], acc=(g > 0))
            q.do("dve", lambda e: e.tensor_mul(pxc[:, :], pxc[:, :], CON["gbc"][:, :]),
                 reads=[pxc, CON["gbc"]], writes=[pxc])
            q.do("dve", lambda e: e.tensor_add(pxc[:, :], pxc[:, :], CON["bbc"][:, :]),
                 reads=[pxc, CON["bbc"]], writes=[pxc])
            q.do("dve", lambda e: e.tensor_add(pxc[:, :], pxc[:, :], xqt[:, :]),
                 reads=[pxc, xqt], writes=[pxc])
            q.dma("sp", xqwo[128 * i:128 * (i + 1), :], pxc[:, :],
                  reads=[pxc], writes=[xqwo])
        q.barrier()
        sp_.close()
        q.do("pool", lambda e: e.collective_compute(
            "AllGather", ALU.bypass, [[0, 4], [1, 5], [2, 6], [3, 7]],
            [xqwo[:, :]], [gxq[:, :]]), reads=[xqwo], writes=[gxq])
        return {"gx1": gx1, "gxn": gxn, "gxq": gxq}
    q.wt("pool", q.last_tok["pool"])
    q.wt("sp", q.last_tok["sp"])
    q.build()
    ctx.close()
    return nc



TOK3 = S // 2
NT3 = TOK3 // 128

L1P_IN = [
    ("z2b", [TOK3, D]), ("xqp", [TOK3, D]), ("xn1p", [TOK3, D]), ("x1p", [TOK3, D]),
    ("glnb", [128, D]), ("blnb", [128, D]), ("gpnb", [128, D]), ("bpnb", [128, D]),
    ("wot", [D, D]), ("ffw", [D, D]), ("ffb", [1, D]), ("ones1", [1, 128]),
]

L1P_IN_FUSED = [
    ("gpnb", [128, D]), ("bpnb", [128, D]),
    ("wot", [D, D]), ("ffw", [D, D]), ("ffb", [1, D]), ("ones1", [1, 128]),
]


def build_l1p(dbg=(), env=None, gbufs=None):
    fused = env is not None
    pfx = "p_" if fused else ""
    if env is None:
        nc = bass.Bass("TRN2", target_bir_lowering=False)
        ctx = contextlib.ExitStack()
        q = Q(nc, ctx)
        pp = Psum(q, nc, ctx)
    else:
        nc, ctx, q, pp = env
    ps = pp.banks

    W = {n: nc.dram_tensor(pfx + n, list(s), F32, kind="ExternalInput")
         for n, s in (L1P_IN_FUSED if fused else L1P_IN)}
    ntile = NB if fused else NT3
    x2o = nc.dram_tensor("x2o", [128 * ntile, D], F32, kind="ExternalOutput")
    wsc_d = nc.dram_tensor(pfx + "wsc_d", [1, D], F32)
    dbg_o = {n: nc.dram_tensor(n, list(s), F32, kind="ExternalOutput") for n, s in dbg}
    if fused:
        gx1, gxn, gxq = gbufs["gx1"], gbufs["gxn"], gbufs["gxq"]

    def mksb(c):
        return lambda name, shape, dt=BF16: c.enter_context(
            nc.sbuf_tensor(pfx + name, list(shape), dt))

    sb = mksb(ctx)
    ones1 = sb("t_ones1", [1, 128], F32)
    if not fused:
        glnb = sb("t_glnb", [128, D], F32)
        blnb = sb("t_blnb", [128, D], F32)
    gpnb = sb("t_gpnb", [128, D], F32)
    bpnb = sb("t_bpnb", [128, D], F32)
    wotb = [sb(f"wotb{k}", [128, D]) for k in range(8)]
    wqT = [sb(f"wqT{k}", [128, D]) for k in range(8)]
    wsbc = sb("wsbc", [128, D], F32)
    bfbc = sb("bfbc", [128, D], F32)
    names = (("ones1", ones1), ("gpnb", gpnb), ("bpnb", bpnb)) if fused else (
        ("ones1", ones1), ("glnb", glnb), ("blnb", blnb),
        ("gpnb", gpnb), ("bpnb", bpnb))
    for n, dst in names:
        q.dma("sp", dst[:, :], W[n][:, :], reads=[W[n]], writes=[dst])
    for k in range(8):
        q.dma("pool", wotb[k][:, :], W["wot"][128 * k:128 * (k + 1), :],
              reads=[W["wot"]], writes=[wotb[k]])

    s_w = contextlib.ExitStack()
    sbw = mksb(s_w)
    wsl = sbw("wsl", [128, D], F32)
    wtmp = sbw("wtmp", [128, D], F32)
    wint = sbw("wint", [128, D], mybir.dt.int32)
    wqs = [sbw(f"wqs{k}", [128, D]) for k in range(8)]
    wam = sbw("wam", [128, 1], F32)
    for k in range(8):
        q.dma("sp", wsl[:, :], W["ffw"][128 * k:128 * (k + 1), :],
              reads=[W["ffw"]], writes=[wsl])
        q.do("dve", lambda e: e.reduce_max(wam[:, :], wsl[:, :], axis=AX,
                                           apply_absolute_value=True),
             reads=[wsl], writes=[wam])
        q.do("dve", lambda e: e.tensor_scalar(wam[:, :], wam[:, :], 1e-8, None, ALU.max),
             reads=[wam], writes=[wam])
        q.do("dve", lambda e: e.tensor_scalar_mul(wam[:, :], wam[:, :], 1.0 / 127.0),
             reads=[wam], writes=[wam])
        q.dma("sp", bass.AP(wsc_d, 128 * k, [[1, 128], [1, 1]]), wam[:, 0:1],
              reads=[wam], writes=[wsc_d])
        q.do("dve", lambda e: e.reciprocal(wam[:, :], wam[:, :]), reads=[wam], writes=[wam])
        q.do("dve", lambda e: e.tensor_scalar(
            wtmp[:, :], wsl[:, :], wam[:, :], None, ALU.mult),
            reads=[wsl, wam], writes=[wtmp])
        q.do("dve", lambda e: e.tensor_scalar(
            wtmp[:, :], wtmp[:, :], 12582912.0, -12582912.0, ALU.add, ALU.add),
            reads=[wtmp], writes=[wtmp])
        q.do("dve", lambda e: e.tensor_copy(wint[:, :], wtmp[:, :]),
             reads=[wtmp], writes=[wint])
        q.do("dve", lambda e, k=k: e.tensor_copy(wqs[k][:, :], wint[:, :]),
             reads=[wint], writes=[wqs[k]])
    for k in range(8):
        for f in range(8):
            sem_name = "dt%d" % f
            q.sem(sem_name)
            if q.cnt[sem_name] > 0:
                q.wt("sp", ("eager", sem_name, q.cnt[sem_name]))
            q.do("sp", lambda e, k=k, f=f: e.dma_start_transpose(
                wqT[f][:, 128 * k:128 * (k + 1)], wqs[k][:, 128 * f:128 * (f + 1)]),
                reads=[wqs[k]], writes=[wqT[f]], inc=sem_name, n=16, is_dma=True)
    q.barrier()
    s_w.close()

    wrow = sb("wrow", [1, D], F32)
    q.dma("sp", wrow[:, :], wsc_d[:, :], reads=[wsc_d], writes=[wrow])
    for half in range(2):
        bi, pb = pp.get()
        q.do("pe", lambda e, pb=pb, h=half: e.matmul(
            pb[:, 0:512], ones1[0:1, 0:128], wrow[0:1, 512 * h:512 * (h + 1)],
            start=True, stop=True), reads=[ones1, wrow], writes=[pb])
        q.copy("dve", wsbc[:, 512 * half:512 * (half + 1)], pb[:, 0:512],
               reads=[pb], writes=[wsbc])
    brow = sb("brow", [1, D], F32)
    q.dma("sp", brow[:, :], W["ffb"][:, :], reads=[W["ffb"]], writes=[brow])
    for half in range(2):
        bi, pb = pp.get()
        q.do("pe", lambda e, pb=pb, h=half: e.matmul(
            pb[:, 0:512], ones1[0:1, 0:128], brow[0:1, 512 * h:512 * (h + 1)],
            start=True, stop=True), reads=[ones1, brow], writes=[pb])
        q.copy("dve", bfbc[:, 512 * half:512 * (half + 1)], pb[:, 0:512],
               reads=[pb], writes=[bfbc])

    s_t = ctx.enter_context(contextlib.ExitStack())
    sbt = mksb(s_t)
    z2t = [sbt(f"z2t{r}", [128, D], F32) for r in range(2)]
    xqt = [sbt(f"xqt{r}", [128, D], F32) for r in range(2)]
    xnt = [sbt(f"xnt{r}", [128, D], F32) for r in range(2)]
    x1t = [sbt(f"x1t{r}", [128, D], F32) for r in range(2)]
    xct = sbt("xct", [128, D], F32)
    sqd = sbt("sqd", [128, 64], F32)
    mu16 = sbt("mu16", [128, 16], F32)
    v16 = sbt("v16", [128, 16], F32)
    a16 = sbt("a16", [128, 16], F32)
    sc16 = sbt("sc16", [128, 16], F32)
    rs16 = sbt("rs16", [128, 16], F32)
    h16 = sbt("h16", [128, 16], F32)
    mup = sbt("mup", [128, 1], F32)
    vp = sbt("vp", [128, 1], F32)
    ap1 = sbt("ap1", [128, 1], F32)
    scp = sbt("scp", [128, 1], F32)
    rsp = sbt("rsp", [128, 1], F32)
    hp = sbt("hp", [128, 1], F32)
    absd = sbt("absd", [128, D], F32)
    xpnb = sbt("xpnb", [128, D])
    xpT = sbt("xpT", [128, D])
    xam = sbt("xam", [128, 1], F32)
    xqf = sbt("xqf", [128, D], F32)
    xint = sbt("xint", [128, D], mybir.dt.int32)
    xqb = sbt("xqb", [128, D])
    xqTt = sbt("xqTt", [128, D])
    stile = sbt("stile", [128, D], F32)
    gq = sbt("gq", [128, D], F32)

    def heron(rst, sc, h_, n=HEFF):
        for _ in range(n):
            q.do("dve", lambda e, rst=rst, h_=h_: e.reciprocal(h_[:, :], rst[:, :]),
                 reads=[rst], writes=[h_])
            q.do("dve", lambda e, h_=h_, sc=sc: e.tensor_mul(h_[:, :], h_[:, :], sc[:, :]),
                 reads=[h_, sc], writes=[h_])
            q.do("dve", lambda e, rst=rst, h_=h_: e.tensor_add(
                rst[:, :], rst[:, :], h_[:, :]), reads=[rst, h_], writes=[rst])
            q.do("dve", lambda e, rst=rst: e.tensor_scalar_mul(rst[:, :], rst[:, :], 0.5),
                 reads=[rst], writes=[rst])

    def tile_in(i):
        r = i % 2
        if fused:
            off = (i % 2) * OWNQ + 128 * (i // 2)
            q.dma("sp", z2t[r][:, 0:512], gxq[128 * i:128 * (i + 1), :],
                  reads=[gxq], writes=[z2t[r]])
            q.dma("sp", z2t[r][:, 512:1024], gxq[S + 128 * i:S + 128 * (i + 1), :],
                  reads=[gxq], writes=[z2t[r]])
            q.dma("sp", xnt[r][:, :], gxn[off:off + 128, :], reads=[gxn], writes=[xnt[r]])
            q.dma("sp", x1t[r][:, :], gx1[off:off + 128, :], reads=[gx1], writes=[x1t[r]])
            return
        for dst, srcn in ((z2t[r], "z2b"), (xqt[r], "xqp"), (xnt[r], "xn1p"),
                          (x1t[r], "x1p")):
            q.dma("sp", dst[:, :], W[srcn][128 * i:128 * (i + 1), :],
                  reads=[W[srcn]], writes=[dst])

    tile_in(0)
    for i in range(ntile):
        r = i % 2
        if i + 1 < ntile:
            tile_in(i + 1)
        zt, qt, nt, ot = z2t[r], xqt[r], xnt[r], x1t[r]
        if not fused:
            for g in range(16):
                q.do("dve", lambda e, g=g, zt=zt: e.reduce_sum(
                    mu16[:, g:g + 1], zt[:, 64 * g:64 * (g + 1)], axis=AX),
                    reads=[zt], writes=[mu16], acc=(g > 0))
            q.do("dve", lambda e: e.tensor_scalar_mul(mu16[:, :], mu16[:, :], 1.0 / 64.0),
                 reads=[mu16], writes=[mu16])
            for g in range(16):
                q.do("dve", lambda e, g=g, zt=zt: e.tensor_scalar(
                    xct[:, 64 * g:64 * (g + 1)], zt[:, 64 * g:64 * (g + 1)],
                    mu16[:, g:g + 1], None, ALU.subtract),
                    reads=[zt, mu16], writes=[xct], acc=(g > 0))
            for g in range(16):
                q.do("dve", lambda e, g=g: e.scalar_tensor_tensor(
                    sqd[:, :], xct[:, 64 * g:64 * (g + 1)], 0.0,
                    xct[:, 64 * g:64 * (g + 1)], ALU.add, ALU.mult,
                    accum_out=v16[:, g:g + 1]),
                    reads=[xct], writes=[sqd, v16])
            for g in range(16):
                q.do("act", lambda e, g=g: e.activation(
                    absd[:, 64 * g:64 * (g + 1)], xct[:, 64 * g:64 * (g + 1)], AF.Abs,
                    accum_out=a16[:, g:g + 1]),
                    reads=[xct], writes=[absd, a16], acc=(g > 0))
            q.do("dve", lambda e: e.tensor_scalar(
                v16[:, :], v16[:, :], 1.0 / 64.0, 1e-6, ALU.mult, ALU.add),
                reads=[v16], writes=[v16])
            q.do("dve", lambda e: e.reciprocal(sc16[:, :], v16[:, :]),
                 reads=[v16], writes=[sc16])
            q.do("dve", lambda e: e.tensor_scalar(
                a16[:, :], a16[:, :], 1.0 / 64.0, 8e-4, ALU.mult, ALU.add),
                reads=[a16], writes=[a16])
            q.do("dve", lambda e: e.reciprocal(rs16[:, :], a16[:, :]),
                 reads=[a16], writes=[rs16])
            q.do("dve", lambda e: e.tensor_scalar_mul(rs16[:, :], rs16[:, :], 1.2533141),
                 reads=[rs16], writes=[rs16])
            heron(rs16, sc16, h16)
            for g in range(16):
                q.do("dve", lambda e, g=g: e.tensor_scalar(
                    xct[:, 64 * g:64 * (g + 1)], xct[:, 64 * g:64 * (g + 1)],
                    rs16[:, g:g + 1], None, ALU.mult),
                    reads=[xct, rs16], writes=[xct], acc=(g > 0))
            q.do("dve", lambda e: e.tensor_mul(xct[:, :], xct[:, :], glnb[:, :]),
                 reads=[xct, glnb], writes=[xct])
            q.do("dve", lambda e: e.tensor_add(xct[:, :], xct[:, :], blnb[:, :]),
                 reads=[xct, blnb], writes=[xct])
            q.do("dve", lambda e, qt=qt: e.tensor_add(xct[:, :], xct[:, :], qt[:, :]),
                 reads=[xct, qt], writes=[xct])
        src0 = zt if fused else xct
        q.do("dve", lambda e, src0=src0: e.reduce_sum(mup[:, :], src0[:, :], axis=AX),
             reads=[src0], writes=[mup])
        q.do("dve", lambda e: e.tensor_scalar_mul(mup[:, :], mup[:, :], 1.0 / D),
             reads=[mup], writes=[mup])
        q.do("dve", lambda e, src0=src0: e.tensor_scalar(
            xct[:, :], src0[:, :], mup[:, :], None, ALU.subtract),
            reads=[src0, mup], writes=[xct])
        q.do("dve", lambda e: e.scalar_tensor_tensor(
            absd[:, :], xct[:, :], 0.0, xct[:, :], ALU.add, ALU.mult,
            accum_out=vp[:, :]), reads=[xct], writes=[absd, vp])
        q.do("act", lambda e: e.activation(absd[:, :], xct[:, :], AF.Abs,
                                           accum_out=ap1[:, :]),
             reads=[xct], writes=[absd, ap1])
        q.do("dve", lambda e: e.tensor_scalar(
            vp[:, :], vp[:, :], 1.0 / D, 1e-6, ALU.mult, ALU.add),
            reads=[vp], writes=[vp])
        q.do("dve", lambda e: e.reciprocal(scp[:, :], vp[:, :]), reads=[vp], writes=[scp])
        q.do("dve", lambda e: e.tensor_scalar(
            ap1[:, :], ap1[:, :], 1.0 / D, 8e-4, ALU.mult, ALU.add),
            reads=[ap1], writes=[ap1])
        q.do("dve", lambda e: e.reciprocal(rsp[:, :], ap1[:, :]), reads=[ap1], writes=[rsp])
        q.do("dve", lambda e: e.tensor_scalar_mul(rsp[:, :], rsp[:, :], 1.2533141),
             reads=[rsp], writes=[rsp])
        heron(rsp, scp, hp)
        q.do("dve", lambda e: e.tensor_scalar(
            xct[:, :], xct[:, :], rsp[:, :], None, ALU.mult),
            reads=[xct, rsp], writes=[xct])
        q.do("dve", lambda e: e.tensor_mul(xct[:, :], xct[:, :], gpnb[:, :]),
             reads=[xct, gpnb], writes=[xct])
        q.do("dve", lambda e: e.tensor_add(xct[:, :], xct[:, :], bpnb[:, :]),
             reads=[xct, bpnb], writes=[xct])
        q.copy("pool", xpnb[:, :], xct[:, :], reads=[xct], writes=[xpnb])
        for f in range(8):
            sem_name = "dtx"
            q.sem(sem_name)
            if q.cnt[sem_name] > 0:
                q.wt("sp", ("eager", sem_name, q.cnt[sem_name]))
            q.do("sp", lambda e, f=f: e.dma_start_transpose(
                xpT[:, 128 * f:128 * (f + 1)], xpnb[:, 128 * f:128 * (f + 1)]),
                reads=[xpnb], writes=[xpT], inc=sem_name, n=16, is_dma=True,
                acc=(f > 0))
        for half in range(2):
            bi, pb = pp.get()
            for k in range(8):
                q.do("pe", lambda e, pb=pb, k=k, h=half: e.matmul(
                    pb[:, 0:512], xpT[:, 128 * k:128 * (k + 1)],
                    wotb[k][:, 512 * h:512 * (h + 1)], start=(k == 0), stop=(k == 7)),
                    reads=[xpT, wotb[k]], writes=[pb], acc=(k > 0))
            q.copy("dve", stile[:, 512 * half:512 * (half + 1)], pb[:, 0:512],
                   reads=[pb], writes=[stile])
        q.do("dve", lambda e, nt=nt: e.reduce_max(xam[:, :], nt[:, :], axis=AX,
                                                  apply_absolute_value=True),
             reads=[nt], writes=[xam])
        q.do("dve", lambda e: e.tensor_scalar(xam[:, :], xam[:, :], 1e-8, None, ALU.max),
             reads=[xam], writes=[xam])
        q.do("dve", lambda e: e.tensor_scalar_mul(xam[:, :], xam[:, :], 1.0 / 127.0),
             reads=[xam], writes=[xam])
        q.do("dve", lambda e: e.reciprocal(hp[:, :], xam[:, :]), reads=[xam], writes=[hp])
        q.do("dve", lambda e, nt=nt: e.tensor_scalar(
            xqf[:, :], nt[:, :], hp[:, :], None, ALU.mult),
            reads=[nt, hp], writes=[xqf])
        q.do("dve", lambda e: e.tensor_scalar(
            xqf[:, :], xqf[:, :], 12582912.0, -12582912.0, ALU.add, ALU.add),
            reads=[xqf], writes=[xqf])
        q.do("dve", lambda e: e.tensor_copy(xint[:, :], xqf[:, :]),
             reads=[xqf], writes=[xint])
        q.copy("pool", xqb[:, :], xint[:, :], reads=[xint], writes=[xqb])
        for f in range(8):
            sem_name = "dqx"
            q.sem(sem_name)
            if q.cnt[sem_name] > 0:
                q.wt("sp", ("eager", sem_name, q.cnt[sem_name]))
            q.do("sp", lambda e, f=f: e.dma_start_transpose(
                xqTt[:, 128 * f:128 * (f + 1)], xqb[:, 128 * f:128 * (f + 1)]),
                reads=[xqb], writes=[xqTt], inc=sem_name, n=16, is_dma=True,
                acc=(f > 0))
        for half in range(2):
            bi, pb = pp.get()
            for k in range(8):
                q.do("pe", lambda e, pb=pb, k=k, h=half: e.matmul(
                    pb[:, 0:512], xqTt[:, 128 * k:128 * (k + 1)],
                    wqT[k][:, 512 * h:512 * (h + 1)], start=(k == 0), stop=(k == 7)),
                    reads=[xqTt, wqT[k]], writes=[pb], acc=(k > 0))
            q.copy("dve", gq[:, 512 * half:512 * (half + 1)], pb[:, 0:512],
                   reads=[pb], writes=[gq])
        q.do("dve", lambda e: e.tensor_mul(gq[:, :], gq[:, :], wsbc[:, :]),
             reads=[gq, wsbc], writes=[gq])
        q.do("dve", lambda e: e.tensor_scalar(
            gq[:, :], gq[:, :], xam[:, :], None, ALU.mult),
            reads=[gq, xam], writes=[gq])
        q.do("dve", lambda e: e.tensor_add(gq[:, :], gq[:, :], bfbc[:, :]),
             reads=[gq, bfbc], writes=[gq])
        q.do("act", lambda e: e.activation(gq[:, :], gq[:, :], AF.Silu),
             reads=[gq], writes=[gq])
        if "d_s" in dbg_o:
            q.dma("pool", dbg_o["d_s"][128 * i:128 * (i + 1), :], stile[:, :],
                  reads=[stile], writes=[dbg_o["d_s"]])
        if "d_g" in dbg_o:
            q.dma("pool", dbg_o["d_g"][128 * i:128 * (i + 1), :], gq[:, :],
                  reads=[gq], writes=[dbg_o["d_g"]])
        if "d_xpn" in dbg_o:
            q.dma("pool", dbg_o["d_xpn"][128 * i:128 * (i + 1), :], xct[:, :],
                  reads=[xct], writes=[dbg_o["d_xpn"]])
        q.do("dve", lambda e: e.tensor_mul(stile[:, :], stile[:, :], gq[:, :]),
             reads=[stile, gq], writes=[stile])
        q.do("dve", lambda e, ot=ot: e.tensor_add(stile[:, :], stile[:, :], ot[:, :]),
             reads=[stile, ot], writes=[stile])
        q.dma("sp", x2o[128 * i:128 * (i + 1), :], stile[:, :],
              reads=[stile], writes=[x2o])

    if env is not None:
        return x2o
    q.wt("pool", q.last_tok["pool"])
    q.wt("sp", q.last_tok["sp"])
    q.build()
    ctx.close()
    return nc


def build_full(dbg=(), nstep=NSTEP):
    nc = bass.Bass("TRN2", target_bir_lowering=False)
    ctx = contextlib.ExitStack()
    q = Q(nc, ctx)
    pp = Psum(q, nc, ctx)

    s0 = contextlib.ExitStack()
    x1own = build_l0(env=(nc, s0, q, pp))
    q.barrier()
    s0.close()

    s1 = contextlib.ExitStack()
    gbufs = build_l1s(dbg=dbg, nstep=nstep, env=(nc, s1, q, pp), x1own=x1own)
    q.barrier()
    s1.close()

    s2 = contextlib.ExitStack()
    build_l1p(env=(nc, s2, q, pp), gbufs=gbufs)
    q.barrier()
    s2.close()

    q.wt("pool", q.last_tok["pool"])
    q.wt("sp", q.last_tok["sp"])
    q.build()
    ctx.close()
    return nc


def l1p_host_inputs(inputs, core, l2res, x1):
    b, hh = core % 4, core // 4
    C = np.ascontiguousarray
    T0 = TOK3 * hh
    p64 = _perm64()
    toks = slice(T0, T0 + TOK3)
    z2b = np.zeros((TOK3, D), np.float32)
    xqp = np.zeros((TOK3, D), np.float32)
    for h2 in range(2):
        rr = l2res[b + 4 * h2]
        z2b[:, 512 * h2:512 * (h2 + 1)] = rr["z2bo"][toks]
        for m in range(4):
            slab = rr["xqro"][m].reshape(128, 2 * S)
            xqcols = slab.reshape(128, 256, 32)[:, :, 0:16].reshape(128, S)
            for hl2 in range(2):
                h = 8 * h2 + 2 * m + hl2
                xqp[:, 64 * h:64 * (h + 1)] = xqcols[64 * hl2:64 * hl2 + 64, toks].T
    xn1p = C(l2res[b]["xn1o"][toks])
    x1p = C(np.asarray(x1[b], np.float32)[toks])
    glw = np.asarray(inputs["L1_attn_ttt_norm_weight"], np.float32)[:, p64].reshape(D)
    glb = np.asarray(inputs["L1_attn_ttt_norm_bias"], np.float32)[:, p64].reshape(D)
    return {
        "z2b": z2b, "xqp": xqp, "xn1p": xn1p,
        "x1p": x1p,
        "glnb": np.tile(glw[None, :], (128, 1)).astype(np.float32),
        "blnb": np.tile(glb[None, :], (128, 1)).astype(np.float32),
        **l1p_host_weights(inputs),
    }


def l1p_host_weights(inputs):
    C = np.ascontiguousarray
    p64 = _perm64()
    permfull = (64 * (np.arange(D) // 64) + p64[np.arange(D) % 64])
    gpw = np.asarray(inputs["L1_attn_post_norm_weight"], np.float32)[permfull]
    gpb = np.asarray(inputs["L1_attn_post_norm_bias"], np.float32)[permfull]
    wo = np.asarray(inputs["L1_attn_wo"], np.float32)
    return {
        "gpnb": np.tile(gpw[None, :], (128, 1)).astype(np.float32),
        "bpnb": np.tile(gpb[None, :], (128, 1)).astype(np.float32),
        "wot": C(wo[:, permfull].T),
        "ffw": C(np.asarray(inputs["L1_ffn_weight"], np.float32)),
        "ffb": C(np.asarray(inputs["L1_ffn_bias"], np.float32))[None, :],
        "ones1": np.ones((1, 128), np.float32),
    }


def full_host_inputs(inputs, core):
    m = dict(l0_host_inputs(inputs, core))
    for k, v in l1s_host_inputs(inputs, core).items():
        m["s_" + k] = v
    for k, v in l1p_host_weights(inputs).items():
        m["p_" + k] = v
    return m



def _perm32(d):
    return np.concatenate([np.arange(0, d, 2), np.arange(1, d, 2)])


def l0_host_inputs(inputs, core):
    b, hh = core % 4, core // 4
    C = np.ascontiguousarray
    x = np.asarray(inputs["L0_attn_x"], np.float32)
    own = np.arange(hh, NB, 2)
    other = np.arange(1 - hh, NB, 2)
    blocks = np.concatenate([own, other])
    xb = C(x[b].reshape(NB, 128, D)[blocks].reshape(S, D))
    perm = _perm32(DHR)
    perm_swap = np.concatenate([perm[16:], perm[:16]])
    wqr = np.asarray(inputs["L0_attn_W_QR"], np.float32).reshape(DL, NH, DHR)
    wkr = np.asarray(inputs["L0_attn_W_KR"], np.float32)
    inv = 10000.0 ** (-np.arange(0, DHR, 2, dtype=np.float64) / DHR)
    pos = (blocks[:, None] * 128 + np.arange(128)[None, :]).reshape(S).astype(np.float64)
    ang = pos[None, :] * inv[:, None]
    cosk = np.concatenate([np.cos(ang), np.cos(ang)], 0).astype(np.float32)
    sink = np.concatenate([-np.sin(ang), np.sin(ang)], 0).astype(np.float32)
    co = np.concatenate([np.cos(ang[:, :OWNQ]), np.cos(ang[:, :OWNQ])], 0).astype(np.float32)
    si = np.concatenate([-np.sin(ang[:, :OWNQ]), np.sin(ang[:, :OWNQ])], 0).astype(np.float32)
    return {
        "xb": xb,
        "wdq": C(np.asarray(inputs["L0_attn_W_DQ"], np.float32)),
        "wuq": C(np.asarray(inputs["L0_attn_W_UQ"], np.float32)),
        "wqr_a": C(wqr[:, :, perm].reshape(DL, NH * DHR)),
        "wqr_b": C(wqr[:, :, perm_swap].reshape(DL, NH * DHR)),
        "wdkv": C(np.asarray(inputs["L0_attn_W_DKV"], np.float32)),
        "wuk": C(np.asarray(inputs["L0_attn_W_UK"], np.float32)),
        "wuv": C(np.asarray(inputs["L0_attn_W_UV"], np.float32)),
        "wkr_a": C(wkr[:, perm]), "wkr_b": C(wkr[:, perm_swap]),
        "wo": C(np.asarray(inputs["L0_attn_W_O"], np.float32)),
        "u1": C(np.asarray(inputs["L0_ffn_U1"], np.float32)),
        "v1": C(np.asarray(inputs["L0_ffn_V1"], np.float32)),
        "b1r": C(np.asarray(inputs["L0_ffn_b1"], np.float32)[None, :]),
        "u2": C(np.asarray(inputs["L0_ffn_U2"], np.float32)),
        "v2": C(np.asarray(inputs["L0_ffn_V2"], np.float32)),
        "b2r": C(np.asarray(inputs["L0_ffn_b2"], np.float32)[None, :]),
        "cosq": np.tile(co, (4, 1)), "sinq": np.tile(si, (4, 1)),
        "cosk": cosk, "sink": sink,
        "trimask": np.triu(np.ones((128, 128), np.float32)),
        "sel16": np.kron(np.eye(16, dtype=np.float32), np.ones((1, 128), np.float32)),
        "m2col": np.full((128, 1), float(hh), np.float32),
        "onesr": np.ones((1, 512), np.float32),
    }


def l0_gather(inputs, results):
    x1 = np.empty((B, S, D), np.float32)
    for c in range(8):
        b, hh = c % 4, c // 4
        own = np.arange(hh, NB, 2)
        r = results[c]["x1own"].reshape(NOB, 128, D)
        for j, blk in enumerate(own):
            x1[b, 128 * blk:128 * (blk + 1), :] = r[j]
    return x1




def _perm64():
    return np.concatenate([np.arange(0, 64, 2), np.arange(1, 64, 2)])


def l1s_host_inputs(inputs, core, x1=None):
    b, hh = core % 4, core // 4
    C = np.ascontiguousarray
    heads = np.arange(8 * hh, 8 * hh + 8)
    p64 = _perm64()
    p64s = np.concatenate([p64[32:], p64[:32]])
    wq = np.asarray(inputs["L1_attn_wq"], np.float32)
    wk = np.asarray(inputs["L1_attn_wk"], np.float32)
    wv = np.asarray(inputs["L1_attn_wv"], np.float32)

    def rows(w, perm):
        idx = (64 * heads[:, None] + perm[None, :]).reshape(-1)
        return C(w[idx, :].T)

    inv = 1.0 / 10000.0 ** (np.arange(0, 64, 2, dtype=np.float64) / 64)
    pos = np.arange(512, dtype=np.float64) % 16
    ang = pos[None, :] * inv[:, None]
    cos1 = np.cos(ang)
    sin1 = np.sin(ang)
    cosl = np.tile(np.concatenate([cos1, cos1], 0), (2, 1)).astype(np.float32)
    sinl = np.tile(np.concatenate([-sin1, sin1], 0), (2, 1)).astype(np.float32)
    w1 = np.asarray(inputs["L1_attn_W1"], np.float32)[heads][:, p64, :]
    b1 = np.asarray(inputs["L1_attn_b1"], np.float32)[heads, 0, :]
    w2 = np.asarray(inputs["L1_attn_W2"], np.float32)[heads][:, :, p64]
    b2 = np.asarray(inputs["L1_attn_b2"], np.float32)[heads][:, 0, p64]
    nw = np.asarray(inputs["L1_attn_ttt_norm_weight"], np.float32)[heads][:, p64]
    nb = np.asarray(inputs["L1_attn_ttt_norm_bias"], np.float32)[heads][:, p64]
    w1c0 = np.zeros((128, 1024), np.float32)
    w2t0 = np.zeros((128, 1024), np.float32)
    kh0 = np.zeros((128, 512), np.float32)
    kh1 = np.zeros((128, 512), np.float32)
    for p in range(8):
        r0, c0 = 64 * (p % 2), 256 * (p // 2)
        w1c0[r0:r0 + 64, c0:c0 + 256] = w1[p]
        w2t0[r0:r0 + 64, c0:c0 + 256] = w2[p].T
        kh0[:, 64 * p:64 * p + 64] = w2[p][0:128, :]
        kh1[:, 64 * p:64 * p + 64] = w2[p][128:256, :]
    selwA = np.zeros((8, 128), np.float32)
    selwB = np.zeros((8, 128), np.float32)
    selzTA = np.zeros((128, 8), np.float32)
    selzTB = np.zeros((128, 8), np.float32)
    for p in range(8):
        (selwA if p % 2 == 0 else selwB)[p, 32 * (p // 2):32 * (p // 2) + 16] = 1.0
        (selzTA if p % 2 == 0 else selzTB)[32 * (p // 2):32 * (p // 2) + 16, p] = 1.0
    triu16 = np.zeros((128, 16), np.float32)
    for s_ in range(4):
        for j in range(16):
            triu16[32 * s_ + j, j:] = 1.0
    gslA = np.zeros((128, 64), np.float32)
    gslB = np.zeros((128, 64), np.float32)
    for p in range(8):
        g = (gslA if p % 2 == 0 else gslB)
        g[32 * (p // 2):32 * (p // 2) + 32, :] = nw[p]
    gbc = np.tile(nw.reshape(1, 512), (128, 1)).astype(np.float32)
    bbc = np.tile(nb.reshape(1, 512), (128, 1)).astype(np.float32)
    return {
        **({"x1b": C(x1[b].astype(np.float32))} if x1 is not None else {}),
        "wq_a": rows(wq, p64), "wq_b": rows(wq, p64s),
        "wk_a": rows(wk, p64), "wk_b": rows(wk, p64s),
        "wv_o": rows(wv, p64),
        "cosl": cosl, "sinl": sinl,
        "lrwT": C(np.asarray(inputs["L1_attn_ttt_lr_weight"], np.float32)[heads, 0, :].T),
        "lrb": C(np.asarray(inputs["L1_attn_ttt_lr_bias"], np.float32)[heads]),
        "invr16": (1.0 / np.arange(1, 17, dtype=np.float64)).astype(np.float32)[None, :],
        "lti": C(np.asarray(inputs["L1_attn_learnable_token_idx"], np.float32)[None, :]),
        "w1c0": w1c0, "b1c0": C(b1), "kh0": kh0, "kh1": kh1, "w2t0": w2t0, "b2c0": C(b2),
        "selwA": selwA, "selwB": selwB, "selzTA": selzTA, "selzTB": selzTB,
        "triu16": triu16,
        "gslotA": gslA, "gslotB": gslB, "gbc": gbc, "bbc": bbc,
        "ident": np.eye(128, dtype=np.float32),
        "onesr": np.ones((1, 512), np.float32),
        "zeror": np.zeros((1, 128), np.float32),
    }


def kernel(**inputs):
    cores = list(range(8))
    nc = build_full()
    im = [full_host_inputs(inputs, c) for c in cores]
    r = run_bass_kernel_spmd(nc, im, core_ids=cores)
    out = np.empty((B, S, D), np.float32)
    for b in range(4):
        out[b] = r.results[b]["x2o"]
    return out


def kernel_unfused(**inputs):
    cores = list(range(8))
    nc0 = build_l0()
    in0 = [l0_host_inputs(inputs, c) for c in cores]
    r0 = run_bass_kernel_spmd(nc0, in0, core_ids=cores)
    x1 = l0_gather(inputs, r0.results)
    nc1 = build_l1s()
    in1 = [l1s_host_inputs(inputs, c, x1) for c in cores]
    r1 = run_bass_kernel_spmd(nc1, in1, core_ids=cores)
    nc2 = build_l1p()
    in2 = [l1p_host_inputs(inputs, c, r1.results, x1) for c in cores]
    r2 = run_bass_kernel_spmd(nc2, in2, core_ids=cores)
    out = np.empty((B, S, D), np.float32)
    for c in cores:
        b, hh = c % 4, c // 4
        out[b, TOK3 * hh:TOK3 * (hh + 1), :] = r2.results[c]["x2o"]
    return out


# version 04
# speedup vs baseline: 1.0081x; 1.0081x over previous
import contextlib
import zlib
import numpy as np
import concourse.bass as bass
import concourse.mybir as mybir
from concourse.bass_utils import run_bass_kernel_spmd

F32 = mybir.dt.float32
BF16 = mybir.dt.bfloat16
AF = mybir.ActivationFunctionType
ALU = mybir.AluOpType
AX = mybir.AxisListType.X

D, S, NB, NOB, OWNQ, B = 1024, 4096, 32, 16, 2048, 4
NH, DH, DHR, DL, DQK = 16, 64, 32, 256, 96
INVSQ = float(1.0 / np.sqrt(np.float64(DQK)))
EXPLO = float(np.exp(np.float64(-80.0) * INVSQ))
EXPHI = float(np.exp(np.float64(80.0) * INVSQ))

ENGS = ("pe", "dve", "act", "pool", "sp")
_ENG_ATTR = {"pe": "tensor", "dve": "vector", "act": "scalar", "pool": "gpsimd", "sp": "sync"}
_SEM_OF = {"pe": "spe", "dve": "sdv", "act": "sac", "pool": "spl", "sp": "ssp"}


class Q:

    def __init__(self, nc, ctx):
        self.nc, self.ctx = nc, ctx
        self.ops = {e: [] for e in ENGS}
        self.meta = {e: [] for e in ENGS}
        self.sems, self.cnt = {}, {}
        self.last_writer = {}
        self.readers = {}
        self.last_tok = {e: None for e in ENGS}

    def sem(self, name):
        if name not in self.sems:
            self.sems[name] = self.ctx.enter_context(self.nc.semaphore(name))
            self.cnt[name] = 0
        return self.sems[name]

    @staticmethod
    def _tid(t):
        h = getattr(t, "tensor", None)
        if h is not None:
            return getattr(h, "name", id(h))
        return getattr(t, "name", id(t))

    def do(self, eng, fn, reads=(), writes=(), inc=None, n=1, is_dma=False, acc=False):
        for t in reads:
            self.wt(eng, self.last_writer.get(self._tid(t)))
        if not acc:
            for t in writes:
                ti = self._tid(t)
                self.wt(eng, self.last_writer.get(ti))
                for rt in self.readers.get(ti, ()):
                    self.wt(eng, rt)
        if inc is not None:
            h = self.sem(inc)
            self.cnt[inc] += n
            tok = ("eager", inc, self.cnt[inc])
            self.ops[eng].append(lambda e, fn=fn, h=h, n=n: fn(e).then_inc(h, n))
            self.meta[eng].append(("dma", inc, self.cnt[inc]) if is_dma else ("op", None))
        else:
            sem_name = _SEM_OF[eng]
            h = self.sem(sem_name)
            cell = {"sem": sem_name, "on": False, "v": None}
            tok = ("lazy", cell, eng)

            def run(e, fn=fn, h=h, cell=cell):
                ins = fn(e)
                if cell["on"]:
                    ins.then_inc(h, 1)
                return ins
            self.ops[eng].append(run)
            self.meta[eng].append(("op", cell))
        for t in reads:
            self.readers.setdefault(self._tid(t), []).append(tok)
        for t in writes:
            ti = self._tid(t)
            self.last_writer[ti] = tok
            self.readers[ti] = []
        self.last_tok[eng] = tok
        return tok

    def dma(self, eng, out_ap, in_ap, inc=None, reads=(), writes=()):
        dst = writes[0] if writes else out_ap
        grp = zlib.crc32(str(self._tid(dst)).encode()) % 4
        sem_name = inc if inc is not None else ("d_%s%d" % (eng, grp))
        self.sem(sem_name)
        if self.cnt[sem_name] > 0:
            self.wt(eng, ("eager", sem_name, self.cnt[sem_name]))
        return self.do(eng, lambda e, o=out_ap, i=in_ap: e.dma_start(o, i),
                       reads=reads, writes=writes, inc=sem_name, n=16, is_dma=True)

    def wt(self, eng, tok):
        if tok is None:
            return
        if tok[0] == "lazy":
            tok[1]["on"] = True
            wcell = {"sem": tok[1]["sem"], "pc": tok[1], "v": None, "skip": False}
        else:
            wcell = {"sem": tok[1], "pc": None, "v": tok[2], "skip": False}
        h = self.sem(wcell["sem"])

        def run(e, h=h, c=wcell):
            if not c["skip"]:
                return e.wait_ge(h, c["v"])

        class _W:
            def then_inc(self, *a):
                return self
        self.ops[eng].append(lambda e, h=h, c=wcell: (run(e)))
        self.meta[eng].append(("wait", wcell))

    def wts(self, engs, toks):
        for e in engs:
            for t in toks:
                self.wt(e, t)

    def barrier(self):
        toks = {id(t): t for t in self.last_tok.values() if t is not None}
        for t in self.last_writer.values():
            toks[id(t)] = t
        for lst in self.readers.values():
            for t in lst:
                toks[id(t)] = t
        for e in ENGS:
            for t in toks.values():
                self.wt(e, t)

    def copy(self, eng, dst, src, reads=(), writes=()):
        r = list(reads) or [src]
        w = list(writes) or [dst]
        if eng == "act":
            return self.do(eng, lambda e, d=dst, s=src: e.copy(d, s), reads=r, writes=w)
        return self.do(eng, lambda e, d=dst, s=src: e.tensor_copy(d, s), reads=r, writes=w)

    def _finalize(self):
        for eng in ENGS:
            for m in self.meta[eng]:
                if m[0] == "op" and m[1] is not None and m[1]["on"]:
                    c = m[1]
                    self.cnt[c["sem"]] += 1
                    c["v"] = self.cnt[c["sem"]]
        for eng in ENGS:
            for m in self.meta[eng]:
                if m[0] == "wait" and m[1]["pc"] is not None:
                    m[1]["v"] = m[1]["pc"]["v"]
        for eng in ENGS:
            hi = {}
            for m in self.meta[eng]:
                if m[0] == "wait":
                    c = m[1]
                    if c["v"] is None or hi.get(c["sem"], -1) >= c["v"]:
                        c["skip"] = True
                    else:
                        hi[c["sem"]] = c["v"]
        burst_end = {}
        for eng in ENGS:
            run = {}
            for m in self.meta[eng] + [("op", None)]:
                if m[0] == "dma":
                    _, s_, v_ = m
                    run.setdefault(s_, []).append(v_)
                else:
                    for s_, vals in run.items():
                        for v_ in vals:
                            burst_end.setdefault(s_, {})[v_] = vals[-1]
                    run = {}
        for eng in ENGS:
            for m in self.meta[eng]:
                if m[0] == "wait" and not m[1]["skip"]:
                    c = m[1]
                    be = burst_end.get(c["sem"], {}).get(c["v"])
                    if be is not None and be > c["v"]:
                        c["v"] = be

    def build(self):
        self._finalize()
        with self.nc.allow_low_precision(reason="intentional bf16 staging"), \
             self.nc.Block() as block:
            for e in ENGS:
                if not self.ops[e]:
                    continue

                def mk(ops):
                    def body(eng):
                        for f in ops:
                            f(eng)
                    return body
                getattr(block, _ENG_ATTR[e])(mk(self.ops[e]))


class Psum:
    def __init__(self, q, nc, ctx):
        self.q = q
        self.banks = [ctx.enter_context(nc.psum_tensor(f"ps{i}", [128, 512], F32))
                      for i in range(8)]
        self.lastc = [None] * 8
        self.rr = 0
        self.held = set()

    def get(self, idx=None):
        if idx is None:
            for _ in range(16):
                i = self.rr % 8
                self.rr += 1
                if i not in self.held:
                    idx = i
                    break
        self.q.wt("pe", self.lastc[idx])
        return idx, self.banks[idx]

    def hold(self, idx):
        self.held.add(idx)

    def release(self, idx, consumer_tok):
        self.lastc[idx] = consumer_tok
        self.held.discard(idx)


def cdiv(a, b):
    return (a + b - 1) // b


L0_IN = [
    ("xb", [S, D]), ("wdq", [D, DL]), ("wuq", [DL, NH * DH]),
    ("wqr_a", [DL, NH * DHR]), ("wqr_b", [DL, NH * DHR]), ("wdkv", [D, DL]),
    ("wuk", [DL, NH * DH]), ("wuv", [DL, NH * DH]), ("wkr_a", [D, DHR]),
    ("wkr_b", [D, DHR]), ("wo", [NH * DH, D]), ("u1", [D, 256]), ("v1", [256, 4096]),
    ("b1r", [1, 4096]), ("u2", [4096, 256]), ("v2", [256, D]), ("b2r", [1, D]),
    ("cosq", [128, OWNQ]), ("sinq", [128, OWNQ]), ("cosk", [DHR, S]), ("sink", [DHR, S]),
    ("trimask", [128, 128]), ("m2col", [128, 1]), ("onesr", [1, 512]),
    ("sel16", [16, 16 * 128]),
]


def build_l0(dbg=(), upto=99, env=None):
    if env is None:
        nc = bass.Bass("TRN2", target_bir_lowering=False)
        ctx = contextlib.ExitStack()
        q = Q(nc, ctx)
        pp = Psum(q, nc, ctx)
    else:
        nc, ctx, q, pp = env
    ps = pp.banks

    W = {n: nc.dram_tensor(n, list(s), F32, kind="ExternalInput") for n, s in L0_IN}
    xb = W["xb"]
    x1own = nc.dram_tensor("x1own", [OWNQ, D], F32,
                           kind="ExternalOutput" if env is None else "Internal")
    zscratch = nc.dram_tensor("zscratch", [16, 512], F32)
    dbg_o = {n: nc.dram_tensor(n, list(s), F32, kind="ExternalOutput") for n, s in dbg}

    def mksb(c):
        return lambda name, shape, dt=BF16: c.enter_context(
            nc.sbuf_tensor(name, list(shape), dt))

    sb = mksb(ctx)
    cqT = [sb(f"cqT{k}", [128, OWNQ]) for k in range(2)]
    ckvT = [sb(f"ckvT{k}", [128, S]) for k in range(2)]
    krT = sb("krT", [DHR, S])
    pT = [sb(f"pT{k}", [128, OWNQ]) for k in range(2)]
    trim = sb("trim", [128, 128])
    m2c = sb("m2c", [128, 1], F32)
    onesc = sb("onesc", [1, 512])
    sel16 = sb("sel16t", [16, 16 * 128])
    WT = {}

    def alloc_w(c, names):
        s_ = mksb(c)
        for n in names:
            R, C = W[n].shape
            WT[n] = s_("w_" + n, [1 if R == 1 else 128, cdiv(R, 128) * C])

    def cast_weights(names):
        for n in names:
            R, C = W[n].shape
            for k in range(cdiv(R, 128)):
                r = min(128, R - 128 * k)
                q.dma("pool", WT[n][:r, k * C:(k + 1) * C],
                      W[n][128 * k:128 * k + r, :],
                      reads=[W[n]], writes=[WT[n]])

    alloc_w(ctx, ["wuq", "wqr_a", "wqr_b", "wuk", "wuv"])
    s1 = ctx.enter_context(contextlib.ExitStack())
    sb1 = mksb(s1)
    xnT = [sb1(f"xnT{k}", [128, S]) for k in range(8)]
    coskt, sinkt = sb1("coskt", [DHR, S], F32), sb1("sinkt", [DHR, S], F32)
    alloc_w(s1, ["wdq", "wdkv", "wkr_a", "wkr_b", "u1"])

    for dst, srch in ((trim, W["trimask"]), (onesc, W["onesr"]), (sel16, W["sel16"])):
        q.dma("pool", dst[:, :], srch[:, :], reads=[srch], writes=[dst])
    q.dma("sp", m2c[:, :], W["m2col"][:, :], reads=[W["m2col"]], writes=[m2c])
    q.dma("sp", coskt[:, :], W["cosk"][:, :], reads=[W["cosk"]], writes=[coskt])
    q.dma("sp", sinkt[:, :], W["sink"][:, :], reads=[W["sink"]], writes=[sinkt])
    cast_weights(["wuq", "wqr_a", "wqr_b", "wuk", "wuv", "wdq", "wdkv", "wkr_a",
                  "wkr_b", "u1"])

    s_ln = contextlib.ExitStack()
    sbl = mksb(s_ln)
    xrow = [sbl(f"xrow{i}", [128, D], F32) for i in range(2)]
    sqs = [sbl(f"sqs{i}", [128, D]) for i in range(2)]
    mus = sbl("mus", [128, NB], F32)
    s2s = sbl("s2s", [128, NB], F32)
    vars_ = sbl("vars", [128, NB], F32)
    rstds = sbl("rstds", [128, NB], F32)
    xnrow = [sbl(f"xnrow{i}", [128, D]) for i in range(2)]

    for t in range(NB):
        xt = xrow[t % 2]
        q.dma("sp", xt[:, :], xb[128 * t:128 * (t + 1), :], reads=[xb], writes=[xt])
        q.do("dve", lambda e, xt=xt, t=t: e.reduce_sum(mus[:, t:t + 1], xt[:, :], axis=AX),
             reads=[xt], writes=[mus], acc=(t > 0))
        q.do("act", lambda e, xt=xt, t=t: e.activation(
            sqs[t % 2][:, :], xt[:, :], AF.Square, accum_out=s2s[:, t:t + 1]),
            reads=[xt], writes=[sqs[t % 2], s2s], acc=(t > 1))
    q.do("dve", lambda e: e.tensor_scalar_mul(mus[:, :], mus[:, :], 1.0 / D),
         reads=[mus], writes=[mus])
    q.do("dve", lambda e: e.tensor_scalar_mul(s2s[:, :], s2s[:, :], 1.0 / D),
         reads=[s2s], writes=[s2s])
    q.do("dve", lambda e: e.scalar_tensor_tensor(vars_[:, :], mus[:, :], 0.0, mus[:, :],
                                                 ALU.add, ALU.mult),
         reads=[mus], writes=[vars_])
    q.do("dve", lambda e: e.tensor_sub(vars_[:, :], s2s[:, :], vars_[:, :]),
         reads=[s2s, vars_], writes=[vars_])
    q.do("dve", lambda e: e.tensor_scalar_add(vars_[:, :], vars_[:, :], 1e-5),
         reads=[vars_], writes=[vars_])
    q.do("act", lambda e: e.activation(rstds[:, :], vars_[:, :], AF.Sqrt),
         reads=[vars_], writes=[rstds])
    q.do("dve", lambda e: e.reciprocal(rstds[:, :], rstds[:, :]),
         reads=[rstds], writes=[rstds])

    for t in range(NB):
        xt, xo = xrow[t % 2], xnrow[t % 2]
        q.dma("sp", xt[:, :], xb[128 * t:128 * (t + 1), :], reads=[xb], writes=[xt])
        q.do("dve", lambda e, xt=xt, xo=xo, t=t: e.tensor_scalar(
            xo[:, :], xt[:, :], mus[:, t:t + 1], rstds[:, t:t + 1], ALU.subtract, ALU.mult),
            reads=[xt, mus, rstds], writes=[xo])
        for k in range(8):
            sem_name = "dt%d" % k
            q.sem(sem_name)
            if q.cnt[sem_name] > 0:
                q.wt("sp", ("eager", sem_name, q.cnt[sem_name]))
            q.do("sp", lambda e, xo=xo, t=t, k=k: e.dma_start_transpose(
                xnT[k][:, 128 * t:128 * (t + 1)], xo[:, 128 * k:128 * (k + 1)]),
                reads=[xo], writes=[xnT[k]], inc=sem_name, n=16, is_dma=True)

    evac_rr = [0]

    def evac(dst, srcb, eng=None, extra_reads=(), dwr=None):
        if eng is None:
            eng = ("dve", "act")[evac_rr[0] % 2]
            evac_rr[0] += 1
        return q.copy(eng, dst, srcb, reads=[srcb] + list(extra_reads),
                      writes=[dwr if dwr is not None else dst])

    def proj(dsts, wname, Kc, Mtot, N, src, CW=None, wcol0=0):
        CW = CW if CW is not None else WT[wname].shape[1] // Kc
        for mi in range(cdiv(Mtot, 128)):
            m = min(128, Mtot - 128 * mi)
            for nj in range(cdiv(N, 512)):
                nn = min(512, N - 512 * nj)
                bi, pb = pp.get()
                for k in range(Kc):
                    q.do("pe", lambda e, pb=pb, m=m, nn=nn, k=k, mi=mi, nj=nj, CW=CW,
                         wname=wname, wcol0=wcol0: e.matmul(
                             pb[:m, :nn],
                             WT[wname][:, k * CW + wcol0 + 128 * mi:
                                       k * CW + wcol0 + 128 * mi + m],
                             src[k][:, 512 * nj:512 * nj + nn],
                             start=(k == 0), stop=(k == Kc - 1)),
                         reads=[WT[wname], src[k]], writes=[pb], acc=(k > 0))
                evac(dsts[mi][:m, 512 * nj:512 * nj + nn], pb[:m, :nn], dwr=dsts[mi])
        return

    proj(cqT, "wdq", 8, DL, OWNQ, xnT)
    proj(ckvT, "wdkv", 8, DL, S, xnT)
    proj(pT, "u1", 8, 256, OWNQ, xnT)

    s_rope2 = contextlib.ExitStack()
    ropek = [mksb(s_rope2)(f"ropek{i}", [128, 512], F32) for i in range(2)]
    for nj in range(8):
        bia, pa = pp.get()
        bib, pb_ = pp.get()
        for k in range(8):
            q.do("pe", lambda e, k=k, nj=nj, pa=pa: e.matmul(
                pa[:DHR, :512], WT["wkr_a"][:, k * DHR:(k + 1) * DHR],
                xnT[k][:, 512 * nj:512 * (nj + 1)], start=(k == 0), stop=(k == 7)),
                reads=[WT["wkr_a"], xnT[k]], writes=[pa], acc=(k > 0))
        for k in range(8):
            q.do("pe", lambda e, k=k, nj=nj, pb_=pb_: e.matmul(
                pb_[:DHR, :512], WT["wkr_b"][:, k * DHR:(k + 1) * DHR],
                xnT[k][:, 512 * nj:512 * (nj + 1)], start=(k == 0), stop=(k == 7)),
                reads=[WT["wkr_b"], xnT[k]], writes=[pb_], acc=(k > 0))
        q.do("dve", lambda e, nj=nj, pa=pa, r0=ropek[0]: e.tensor_mul(
            r0[:DHR, :512], pa[:DHR, :512], coskt[:, 512 * nj:512 * (nj + 1)]),
            reads=[pa, coskt], writes=[ropek[0]])
        q.do("dve", lambda e, nj=nj, pb_=pb_, r1=ropek[1]: e.tensor_mul(
            r1[:DHR, :512], pb_[:DHR, :512], sinkt[:, 512 * nj:512 * (nj + 1)]),
            reads=[pb_, sinkt], writes=[ropek[1]])
        q.do("dve", lambda e, nj=nj, r0=ropek[0], r1=ropek[1]: e.tensor_add(
            krT[:DHR, 512 * nj:512 * (nj + 1)], r0[:DHR, :512], r1[:DHR, :512]),
            reads=[ropek[0], ropek[1]], writes=[krT])

    if upto <= 2:
        t_out = []
        dbgmap = {"d_xnT0": (xnT[0], 128, 512), "d_cqT0": (cqT[0], 128, 512),
                  "d_pT0": (pT[0], 128, 512), "d_krT": (krT, DHR, 512),
                  "d_ckvT0": (ckvT[0], 128, 512)}
        for nname, (tt, r, cc) in dbgmap.items():
            if nname in dbg_o:
                t_out.append(q.dma("pool", dbg_o[nname][:, :], tt[:r, 0:cc], reads=[tt], writes=[dbg_o[nname]]))
        q.wts(["pool"], t_out)
        q.build()
        s_rope2.close()
        s_ln.close()
        ctx.close()
        return nc

    q.barrier()
    s_rope2.close()
    s_ln.close()
    s1.close()

    attnT = [sb(f"attnT{k}", [128, OWNQ]) for k in range(8)]
    s2 = ctx.enter_context(contextlib.ExitStack())
    sb2 = mksb(s2)
    QT = [sb2(f"QT{h}", [DQK, OWNQ]) for h in range(4)]
    KT = [sb2(f"KT{h}", [DQK, S]) for h in range(4)]
    VT = [sb2(f"VT{i}", [128, 4 * 65]) for i in range(NB)]
    expt = [sb2(f"expt{i}", [128, 512]) for i in range(4)]
    ropet = [sb2(f"ropetw{i}", [128, 512], F32) for i in range(2)]
    cosqt, sinqt = sb2("cosqt", [128, OWNQ]), sb2("sinqt", [128, OWNQ])
    rowsums = sb2("rowsums", [16, 512], F32)
    zcol = sb2("zcol", [128, 2048], F32)
    rowinv = sb2("rowinv", [16, 512])
    invbs = sb2("invbs", [128, 512])
    q.dma("pool", cosqt[:, :], W["cosq"][:, :], reads=[W["cosq"]], writes=[cosqt])
    q.dma("pool", sinqt[:, :], W["sinq"][:, :], reads=[W["sinq"]], writes=[sinqt])

    NWAVE = 1 if upto <= 5 else 4
    for wv in range(NWAVE):
        for mi in range(2):
            for nj in range(4):
                bi, pb = pp.get()
                for k in range(2):
                    q.do("pe", lambda e, pb=pb, k=k, mi=mi, nj=nj, wv=wv: e.matmul(
                        pb[:, :512],
                        WT["wuq"][:, k * NH * DH + 4 * wv * DH + 128 * mi:
                                  k * NH * DH + 4 * wv * DH + 128 * mi + 128],
                        cqT[k][:, 512 * nj:512 * (nj + 1)], start=(k == 0), stop=(k == 1)),
                        reads=[WT["wuq"], cqT[k]], writes=[pb], acc=(k > 0))
                evac(QT[2 * mi][0:64, 512 * nj:512 * (nj + 1)], pb[0:64, :512],
                     dwr=QT[2 * mi])
                evac(QT[2 * mi + 1][0:64, 512 * nj:512 * (nj + 1)], pb[64:128, :512],
                     dwr=QT[2 * mi + 1])
        for nj in range(4):
            bia, pa = pp.get()
            bib, pb_ = pp.get()
            for k in range(2):
                q.do("pe", lambda e, pa=pa, k=k, nj=nj, wv=wv: e.matmul(
                    pa[:, :512],
                    WT["wqr_a"][:, k * NH * DHR + 4 * wv * DHR:
                                k * NH * DHR + 4 * wv * DHR + 128],
                    cqT[k][:, 512 * nj:512 * (nj + 1)], start=(k == 0), stop=(k == 1)),
                    reads=[WT["wqr_a"], cqT[k]], writes=[pa], acc=(k > 0))
            for k in range(2):
                q.do("pe", lambda e, pb_=pb_, k=k, nj=nj, wv=wv: e.matmul(
                    pb_[:, :512],
                    WT["wqr_b"][:, k * NH * DHR + 4 * wv * DHR:
                                k * NH * DHR + 4 * wv * DHR + 128],
                    cqT[k][:, 512 * nj:512 * (nj + 1)], start=(k == 0), stop=(k == 1)),
                    reads=[WT["wqr_b"], cqT[k]], writes=[pb_], acc=(k > 0))
            q.do("dve", lambda e, pa=pa, nj=nj, r0=ropet[0]: e.tensor_mul(
                r0[:, :512], pa[:, :512], cosqt[:, 512 * nj:512 * (nj + 1)]),
                reads=[pa, cosqt], writes=[ropet[0]])
            q.do("dve", lambda e, pb_=pb_, nj=nj, r1=ropet[1]: e.tensor_mul(
                r1[:, :512], pb_[:, :512], sinqt[:, 512 * nj:512 * (nj + 1)]),
                reads=[pb_, sinqt], writes=[ropet[1]])
            q.do("dve", lambda e, r0=ropet[0], r1=ropet[1]: e.tensor_add(
                r0[:, :512], r0[:, :512], r1[:, :512]),
                reads=[ropet[0], ropet[1]], writes=[ropet[0]])
            for hi in range(4):
                q.dma("pool", QT[hi][64:96, 512 * nj:512 * (nj + 1)],
                      ropet[0][32 * hi:32 * hi + 32, :512],
                      reads=[ropet[0]], writes=[QT[hi]])
        for mi in range(2):
            for nj in range(8):
                bi, pb = pp.get()
                for k in range(2):
                    q.do("pe", lambda e, pb=pb, k=k, mi=mi, nj=nj, wv=wv: e.matmul(
                        pb[:, :512],
                        WT["wuk"][:, k * NH * DH + 4 * wv * DH + 128 * mi:
                                  k * NH * DH + 4 * wv * DH + 128 * mi + 128],
                        ckvT[k][:, 512 * nj:512 * (nj + 1)], start=(k == 0), stop=(k == 1)),
                        reads=[WT["wuk"], ckvT[k]], writes=[pb], acc=(k > 0))
                evac(KT[2 * mi][0:64, 512 * nj:512 * (nj + 1)], pb[0:64, :512],
                     dwr=KT[2 * mi])
                evac(KT[2 * mi + 1][0:64, 512 * nj:512 * (nj + 1)], pb[64:128, :512],
                     dwr=KT[2 * mi + 1])
        for hi in range(4):
            q.dma("pool", KT[hi][64:96, :], krT[:DHR, :], reads=[krT], writes=[KT[hi]])
        for i in range(NB):
            q.do("pool", lambda e, i=i: e.memset(VT[i][:, :], 1.0),
                 reads=[], writes=[VT[i]])
        for i in range(NB):
            bi, pb = pp.get()
            for k in range(2):
                q.do("pe", lambda e, pb=pb, k=k, i=i, wv=wv: e.matmul(
                    pb[:, :256], ckvT[k][:, 128 * i:128 * (i + 1)],
                    WT["wuv"][:, k * NH * DH + 4 * wv * DH:k * NH * DH + 4 * wv * DH + 256],
                    start=(k == 0), stop=(k == 1)),
                    reads=[ckvT[k], WT["wuv"]], writes=[pb], acc=(k > 0))
            dst = VT[i][:, :].rearrange("p (h c) -> p h c", c=65)[:, :, 0:64]
            srcv = pb[:, :256].rearrange("p (h c) -> p h c", h=4)
            q.copy("dve", dst, srcv, reads=[pb], writes=[VT[i]])

        if upto <= 3:
            break
        for hi in range(4 if upto >= 5 else 1):
            h = 4 * wv + hi
            slab, row0 = attnT[h // 2], 64 * (h % 2)
            avb = [pp.banks[g] for g in range(4)]
            exp_cnt = [0]
            for c in range(NB):
                j0 = c if c < NOB else c - NOB
                qs = 128 * j0
                for g in range(j0 // 4, 4):
                    g0 = max(qs, 512 * g)
                    nn = 512 * (g + 1) - g0
                    off = g0 % 512
                    sp2 = pp.banks[4 + (exp_cnt[0] % 2)]
                    eb = expt[exp_cnt[0] % 4]
                    q.do("pe", lambda e, sp2=sp2, c=c, g0=g0, nn=nn, hi=hi: e.matmul(
                        sp2[:, :nn], KT[hi][:, 128 * c:128 * (c + 1)],
                        QT[hi][:, g0:g0 + nn], start=True, stop=True),
                        reads=[KT[hi], QT[hi]], writes=[sp2])
                    q.do("act", lambda e, eb=eb, sp2=sp2, nn=nn: e.activation(
                        eb[:, :nn], sp2[:, :nn], AF.Exp, scale=INVSQ),
                        reads=[sp2], writes=[eb])
                    q.do("dve", lambda e, eb=eb, nn=nn: e.tensor_scalar(
                        eb[:, :nn], eb[:, :nn], EXPLO, EXPHI, ALU.max, ALU.min),
                        reads=[eb], writes=[eb])
                    if c < NOB and g0 == qs:
                        q.do("dve", lambda e, eb=eb: e.tensor_mul(
                            eb[:, 0:128], eb[:, 0:128], trim[:, :]),
                            reads=[eb, trim], writes=[eb])
                    if c >= NOB and g0 == qs:
                        q.do("dve", lambda e, eb=eb: e.tensor_scalar(
                            eb[:, 0:128], eb[:, 0:128], m2c[:, :], None, ALU.mult),
                            reads=[eb, m2c], writes=[eb])
                    q.do("pe", lambda e, av=avb[g], hi=hi, c=c, eb=eb, off=off, nn=nn,
                         st=(c == 0), sp_=(c == NOB + 4 * g + 3): e.matmul(
                             av[0:65, off:off + nn], VT[c][:, 65 * hi:65 * hi + 65],
                             eb[:, :nn], start=st, stop=sp_),
                         reads=[VT[c], eb], writes=[avb[g]], acc=(c != 0))
                    exp_cnt[0] += 1
            for g in range(4):
                q.copy("dve", zcol[32 * hi:32 * hi + 1, 512 * g:512 * g + 512],
                       avb[g][64:65, :512], reads=[avb[g]], writes=[zcol])
                evac(slab[row0:row0 + 64, 512 * g:512 * (g + 1)], avb[g][0:64, :512],
                     eng="dve", dwr=slab)
        if upto <= 4:
            break
        for hi4 in range(4):
            q.dma("sp", zscratch[4 * hi4:4 * hi4 + 4, :].rearrange("a b -> (a b)").unsqueeze(0),
                  zcol[32 * hi4:32 * hi4 + 1, :], reads=[zcol], writes=[zscratch])
        q.dma("sp", rowsums[:, :], zscratch[:, :], reads=[zscratch], writes=[rowsums])
        q.do("dve", lambda e: e.reciprocal(rowinv[:, :], rowsums[:, :]),
             reads=[rowsums], writes=[rowinv])
        for hi in range(4):
            h = 4 * wv + hi
            slab, row0 = attnT[h // 2], 64 * (h % 2)
            for g in range(4):
                bi, pb = pp.get()
                q.do("pe", lambda e, pb=pb, hi=hi, g=g: e.matmul(
                    pb[0:128, :512], sel16[0:16, 128 * (4 * hi + g):128 * (4 * hi + g) + 128],
                    rowinv[0:16, :512], start=True, stop=True),
                    reads=[sel16, rowinv], writes=[pb])
                q.copy("dve", invbs[:, :], pb[0:128, :512], reads=[pb], writes=[invbs])
                q.do("dve", lambda e, slab=slab, row0=row0, g=g: e.tensor_mul(
                    slab[row0:row0 + 64, 512 * g:512 * (g + 1)],
                    slab[row0:row0 + 64, 512 * g:512 * (g + 1)],
                    invbs[row0:row0 + 64, :]),
                    reads=[slab, invbs], writes=[slab])

    if upto <= 6:
        t_out = []
        dbgmap = {"d_attnT0": (attnT[0], 128, 2048), "d_QT0": (QT[0], DQK, 512),
                  "d_KT0": (KT[0], DQK, 512), "d_VT0": (VT[0], 128, 260)}
        for nname, (tt, r, cc) in dbgmap.items():
            if nname in dbg_o:
                t_out.append(q.dma("pool", dbg_o[nname][:, :], tt[:r, 0:cc], reads=[tt], writes=[dbg_o[nname]]))
        q.wts(["pool"], t_out)
        q.build()
        ctx.close()
        return nc

    q.barrier()
    s2.close()
    s3 = ctx.enter_context(contextlib.ExitStack())
    sb3 = mksb(s3)
    alloc_w(s3, ["wo", "v1", "b1r", "u2", "v2", "b2r"])
    cast_weights(["wo", "v1", "b1r", "u2", "v2", "b2r"])
    hT = [sb3(f"hT{i}", [128, 512]) for i in range(2)]
    q2T = [sb3(f"q2T{k}", [128, OWNQ]) for k in range(2)]
    gtile = [sb3(f"gtile{i}", [128, D]) for i in range(2)]
    sgt = sb3("sgt", [128, D])
    x1r = [sb3(f"x1r{i}", [128, D], F32) for i in range(2)]

    for tg in range(4):
        pq0, pq1 = pp.banks[6], pp.banks[7]
        for f in range(32):
            bi, pb = pp.get(f % 3)
            ht = hT[f % 2]
            for k in range(2):
                q.do("pe", lambda e, pb=pb, k=k, f=f, tg=tg: e.matmul(
                    pb[:, :512], WT["v1"][:, k * 4096 + 128 * f:k * 4096 + 128 * f + 128],
                    pT[k][:, 512 * tg:512 * (tg + 1)], start=(k == 0), stop=False),
                    reads=[WT["v1"], pT[k]], writes=[pb], acc=(k > 0))
            q.do("pe", lambda e, pb=pb, f=f: e.matmul(
                pb[:, :512], WT["b1r"][0:1, 128 * f:128 * f + 128], onesc[0:1, 0:512],
                start=False, stop=True),
                reads=[WT["b1r"], onesc], writes=[pb], acc=True)
            q.do("act", lambda e, pb=pb, ht=ht: e.activation(
                ht[:, :512], pb[:, :512], AF.Gelu_apprx_tanh),
                reads=[pb], writes=[ht])
            for m in range(2):
                q.do("pe", lambda e, m=m, f=f, tg=tg, ht=ht, pq=(pq0, pq1)[m]: e.matmul(
                    pq[:, :512], WT["u2"][:, 256 * f + 128 * m:256 * f + 128 * m + 128],
                    ht[:, :512], start=(f == 0), stop=(f == 31)),
                    reads=[WT["u2"], ht], writes=[(pq0, pq1)[m]], acc=(f > 0))
        evac(q2T[0][:, 512 * tg:512 * (tg + 1)], pq0[:, :512], eng="dve", dwr=q2T[0])
        evac(q2T[1][:, 512 * tg:512 * (tg + 1)], pq1[:, :512], eng="act", dwr=q2T[1])

    for i in range(NOB):
        xt = x1r[i % 2]
        q.dma("sp", xt[:, :], xb[128 * i:128 * (i + 1), :], reads=[xb], writes=[xt])
        gt = gtile[i % 2]
        for half in range(2):
            bi, pb = pp.get()
            for k in range(2):
                q.do("pe", lambda e, pb=pb, k=k, i=i, half=half: e.matmul(
                    pb[:, :512], q2T[k][:, 128 * i:128 * (i + 1)],
                    WT["v2"][:, k * D + 512 * half:k * D + 512 * half + 512],
                    start=(k == 0), stop=False),
                    reads=[q2T[k], WT["v2"]], writes=[pb], acc=(k > 0))
            q.do("pe", lambda e, pb=pb, half=half: e.matmul(
                pb[:, :512], onesc[0:1, 0:128], WT["b2r"][0:1, 512 * half:512 * half + 512],
                start=False, stop=True),
                reads=[onesc, WT["b2r"]], writes=[pb], acc=True)
            q.do("act", lambda e, pb=pb, gt=gt, half=half: e.activation(
                gt[:, 512 * half:512 * half + 512], pb[:, :512], AF.Silu),
                reads=[pb], writes=[gt] if half == 1 else [gt])
            bi2, pb2 = pp.get()
            for k in range(8):
                q.do("pe", lambda e, pb2=pb2, k=k, i=i, half=half: e.matmul(
                    pb2[:, :512], attnT[k][:, 128 * i:128 * (i + 1)],
                    WT["wo"][:, k * D + 512 * half:k * D + 512 * half + 512],
                    start=(k == 0), stop=(k == 7)),
                    reads=[attnT[k], WT["wo"]], writes=[pb2], acc=(k > 0))
            q.do("dve", lambda e, pb2=pb2, gt=gt, half=half: e.tensor_mul(
                sgt[:, 512 * half:512 * half + 512], pb2[:, :512],
                gt[:, 512 * half:512 * half + 512]),
                reads=[pb2, gt], writes=[sgt])
            q.do("dve", lambda e, xt=xt, half=half: e.tensor_add(
                xt[:, 512 * half:512 * half + 512], xt[:, 512 * half:512 * half + 512],
                sgt[:, 512 * half:512 * half + 512]),
                reads=[xt, sgt], writes=[xt])
        tout = q.dma("sp", x1own[128 * i:128 * (i + 1), :], xt[:, :],
                     reads=[xt], writes=[x1own])

    if env is not None:
        return x1own
    q.wt("sp", q.last_tok["sp"])
    q.build()
    ctx.close()
    return nc




L1S_IN = [
    ("x1b", [S, D]),
    ("wq_a", [D, 512]), ("wq_b", [D, 512]), ("wk_a", [D, 512]), ("wk_b", [D, 512]),
    ("wv_o", [D, 512]), ("cosl", [128, 512]), ("sinl", [128, 512]),
    ("lrwT", [D, 8]), ("lrb", [8, 1]), ("invr16", [1, 16]), ("lti", [1, 16]),
    ("w1c0", [128, 1024]), ("b1c0", [8, 256]),
    ("kh0", [128, 512]), ("kh1", [128, 512]), ("w2t0", [128, 1024]), ("b2c0", [8, 64]),
    ("selwA", [8, 128]), ("selwB", [8, 128]), ("selzTA", [128, 8]), ("selzTB", [128, 8]),
    ("triu16", [128, 16]), ("gslotA", [128, 64]), ("gslotB", [128, 64]),
    ("gbc", [128, 512]), ("bbc", [128, 512]), ("ident", [128, 128]),
    ("onesr", [1, 512]), ("zeror", [1, 128]),
]

NSTEP = 256
HEFF = 4
SKEW = 3


def build_l1s(dbg=(), nstep=NSTEP, env=None, x1own=None):
    fused = env is not None
    pfx = "s_" if fused else ""
    if env is None:
        nc = bass.Bass("TRN2", target_bir_lowering=False)
        ctx = contextlib.ExitStack()
        q = Q(nc, ctx)
        pp = Psum(q, nc, ctx)
    else:
        nc, ctx, q, pp = env
    ps = pp.banks

    W = {n: nc.dram_tensor(pfx + n, list(s), F32, kind="ExternalInput")
         for n, s in L1S_IN if not (fused and n == "x1b")}
    if not fused:
        x1b = W["x1b"]
        xn1o = nc.dram_tensor("xn1o", [S, D], F32, kind="ExternalOutput")
        xqro = nc.dram_tensor("xqro", [4, 128, 2 * S], F32, kind="ExternalOutput")
    else:
        xn_own = nc.dram_tensor("xn_own", [OWNQ, D], F32)
        gx1 = nc.dram_tensor("gx1", [2 * OWNQ, D], F32)
        gxn = nc.dram_tensor("gxn", [2 * OWNQ, D], F32)
        xqwo = nc.dram_tensor("xqwo", [S, 512], F32)
        gxq = nc.dram_tensor("gxq", [2 * S, 512], F32)
    z2bo = nc.dram_tensor(pfx + "z2bo", [S, 512], F32,
                          kind="Internal" if fused else "ExternalOutput")
    tgt2_d = nc.dram_tensor(pfx + "tgt2_d", [S, 512], BF16)
    xkpp_d = nc.dram_tensor(pfx + "xkpp_d", [S, 512], BF16)
    lr_d = nc.dram_tensor(pfx + "lr_d", [8, S], F32)
    dbg_o = {n: nc.dram_tensor(n, list(s), F32, kind="ExternalOutput") for n, s in dbg}

    def mksb(c):
        return lambda name, shape, dt=BF16: c.enter_context(
            nc.sbuf_tensor(pfx + name, list(shape), dt))

    sb = mksb(ctx)
    xqkT = [sb(f"xqkT{m}", [128, 2 * S]) for m in range(4)]
    lrS = sb("lrS", [8, S])
    toki = sb("toki", [1, 16], F32)
    t15c = sb("t15c", [128, 1], F32)
    tokbc = sb("tokbc", [128, 16], F32)
    p2A = sb("p2A", [128, 16], F32)
    p2B = sb("p2B", [128, 16], F32)
    idt = sb("idt", [128, 128])
    onesc = sb("onesc", [1, 512])
    ones32 = sb("ones32", [1, 128], F32)
    zeroc = sb("zeroc", [1, 128])
    CON = {}
    for n in ("selwA", "selwB", "selzTA", "selzTB", "triu16", "gslotA", "gslotB",
              "gbc", "bbc"):
        CON[n] = sb("c_" + n, list(W[n].shape),
                    BF16 if n.startswith("sel") else F32)
    WT = {}

    def alloc_w(c, names):
        s_ = mksb(c)
        for n in names:
            R, C = W[n].shape
            WT[n] = s_("w_" + n, [1 if R == 1 else 128, cdiv(R, 128) * C])

    def cast_weights(names):
        for n in names:
            R, C = W[n].shape
            for k in range(cdiv(R, 128)):
                r = min(128, R - 128 * k)
                q.dma("pool", WT[n][:r, k * C:(k + 1) * C],
                      W[n][128 * k:128 * k + r, :], reads=[W[n]], writes=[WT[n]])

    for n, dst in (("ident", idt), ("onesr", onesc), ("zeror", zeroc)):
        q.dma("pool", dst[:, :], W[n][:, :], reads=[W[n]], writes=[dst])
    q.dma("sp", ones32[:, :], W["onesr"][0:1, 0:128], reads=[W["onesr"]], writes=[ones32])
    for n in CON:
        q.dma("pool", CON[n][:, :], W[n][:, :], reads=[W[n]], writes=[CON[n]])

    ivr = sb("ivr", [1, 16], F32)
    lti_t = sb("lti_t", [1, 16], F32)
    q.dma("sp", ivr[:, :], W["invr16"][:, :], reads=[W["invr16"]], writes=[ivr])
    q.dma("sp", lti_t[:, :], W["lti"][:, :], reads=[W["lti"]], writes=[lti_t])
    q.do("dve", lambda e: e.tensor_add(toki[:, :], ivr[:, :], lti_t[:, :]),
         reads=[ivr, lti_t], writes=[toki])
    q.do("dve", lambda e: e.tensor_relu(toki[:, :], toki[:, :]), reads=[toki], writes=[toki])
    bi0, pb0 = pp.get()
    q.do("pe", lambda e, pb=pb0: e.matmul(pb[0:128, 0:16], ones32[0:1, 0:128],
                                          toki[0:1, 0:16], start=True, stop=True),
         reads=[ones32, toki], writes=[pb0])
    q.copy("dve", tokbc[:, :], pb0[0:128, 0:16], reads=[pb0], writes=[tokbc])
    q.do("dve", lambda e, pb=pb0: e.tensor_scalar_mul(
        t15c[:, :], pb[0:128, 15:16], 1.0 / 64.0), reads=[pb0], writes=[t15c])
    for p2 in (p2A, p2B):
        q.do("dve", lambda e, p2=p2: e.tensor_mul(p2[:, :], CON["triu16"][:, :], tokbc[:, :]),
             reads=[CON["triu16"], tokbc], writes=[p2])
        q.do("dve", lambda e, p2=p2: e.tensor_scalar_mul(p2[:, :], p2[:, :], -1.0 / 64.0),
             reads=[p2], writes=[p2])

    s1 = ctx.enter_context(contextlib.ExitStack())
    sb1 = mksb(s1)
    xnT = [sb1(f"xnT{k}", [128, S]) for k in range(8)]
    coslt, sinlt = sb1("coslt", [128, 512]), sb1("sinlt", [128, 512])
    lrbt = sb1("lrbt", [8, 1], F32)
    alloc_w(s1, ["wq_a", "wq_b", "wk_a", "wk_b", "wv_o", "lrwT"])
    cast_weights(["wq_a", "wq_b", "wk_a", "wk_b", "wv_o", "lrwT"])
    q.dma("pool", coslt[:, :], W["cosl"][:, :], reads=[W["cosl"]], writes=[coslt])
    q.dma("pool", sinlt[:, :], W["sinl"][:, :], reads=[W["sinl"]], writes=[sinlt])

    s_ln = contextlib.ExitStack()
    sbl = mksb(s_ln)
    xrow = [sbl(f"xrow{i}", [128, D], F32) for i in range(2)]
    mus = sbl("mus", [128, NB], F32)
    s2s = sbl("s2s", [128, NB], F32)
    vars_ = sbl("vars", [128, NB], F32)
    rstds = sbl("rstds", [128, NB], F32)
    xnrow = [sbl(f"xnrow{i}", [128, D]) for i in range(2)]
    xnrow32 = [sbl("xnrow32", [128, D], F32)] * 2

    nln = 16 if fused else NB
    xsrc = x1own if fused else x1b
    for t in range(nln):
        xt = xrow[t % 2]
        q.dma("sp", xt[:, :], xsrc[128 * t:128 * (t + 1), :], reads=[xsrc], writes=[xt])
        q.do("dve", lambda e, xt=xt, t=t: e.reduce_sum(mus[:, t:t + 1], xt[:, :], axis=AX),
             reads=[xt], writes=[mus], acc=(t > 0))
        q.do("act", lambda e, xt=xt, t=t: e.activation(
            xnrow[t % 2][:, :], xt[:, :], AF.Square, accum_out=s2s[:, t:t + 1]),
            reads=[xt], writes=[xnrow[t % 2], s2s], acc=(t > 1))
    q.do("dve", lambda e: e.tensor_scalar_mul(mus[:, 0:nln], mus[:, 0:nln], 1.0 / D),
         reads=[mus], writes=[mus])
    q.do("dve", lambda e: e.tensor_scalar_mul(s2s[:, 0:nln], s2s[:, 0:nln], 1.0 / D),
         reads=[s2s], writes=[s2s])
    q.do("dve", lambda e: e.scalar_tensor_tensor(vars_[:, 0:nln], mus[:, 0:nln], 0.0,
                                                 mus[:, 0:nln], ALU.add, ALU.mult),
         reads=[mus], writes=[vars_])
    q.do("dve", lambda e: e.tensor_sub(vars_[:, 0:nln], s2s[:, 0:nln], vars_[:, 0:nln]),
         reads=[s2s, vars_], writes=[vars_])
    q.do("dve", lambda e: e.tensor_scalar_add(vars_[:, 0:nln], vars_[:, 0:nln], 1e-5),
         reads=[vars_], writes=[vars_])
    q.do("act", lambda e: e.activation(rstds[:, 0:nln], vars_[:, 0:nln], AF.Sqrt),
         reads=[vars_], writes=[rstds])
    q.do("dve", lambda e: e.reciprocal(rstds[:, 0:nln], rstds[:, 0:nln]),
         reads=[rstds], writes=[rstds])
    for t in range(nln):
        xt, xo, xo32 = xrow[t % 2], xnrow[t % 2], xnrow32[t % 2]
        q.dma("sp", xt[:, :], xsrc[128 * t:128 * (t + 1), :], reads=[xsrc], writes=[xt])
        q.do("dve", lambda e, xt=xt, xo32=xo32, t=t: e.tensor_scalar(
            xo32[:, :], xt[:, :], mus[:, t:t + 1], rstds[:, t:t + 1], ALU.subtract, ALU.mult),
            reads=[xt, mus, rstds], writes=[xo32])
        if fused:
            q.dma("sp", xn_own[128 * t:128 * (t + 1), :], xo32[:, :],
                  reads=[xo32], writes=[xn_own])
            continue
        q.do("dve", lambda e, xt=xt, xo=xo, t=t: e.tensor_scalar(
            xo[:, :], xt[:, :], mus[:, t:t + 1], rstds[:, t:t + 1], ALU.subtract, ALU.mult),
            reads=[xt, mus, rstds], writes=[xo])
        q.dma("sp", xn1o[128 * t:128 * (t + 1), :], xo32[:, :], reads=[xo32], writes=[xn1o])
        for k in range(8):
            sem_name = "dt%d" % k
            q.sem(sem_name)
            if q.cnt[sem_name] > 0:
                q.wt("sp", ("eager", sem_name, q.cnt[sem_name]))
            q.do("sp", lambda e, xo=xo, t=t, k=k: e.dma_start_transpose(
                xnT[k][:, 128 * t:128 * (t + 1)], xo[:, 128 * k:128 * (k + 1)]),
                reads=[xo], writes=[xnT[k]], inc=sem_name, n=16, is_dma=True)
    if fused:
        if nstep < NSTEP:
            s_zz = contextlib.ExitStack()
            zz0 = mksb(s_zz)("zz0", [128, 512], F32)
            q.do("pool", lambda e: e.memset(zz0[:, :], 0.0), reads=[], writes=[zz0])
            for t in range(NB):
                q.dma("pool", z2bo[128 * t:128 * (t + 1), :], zz0[:, :],
                      reads=[zz0], writes=[z2bo])
            q.barrier()
            s_zz.close()
        q.do("pool", lambda e: e.collective_compute(
            "AllGather", ALU.bypass, [[0, 4], [1, 5], [2, 6], [3, 7]],
            [x1own[:, :]], [gx1[:, :]]), reads=[x1own], writes=[gx1])
        q.do("pool", lambda e: e.collective_compute(
            "AllGather", ALU.bypass, [[0, 4], [1, 5], [2, 6], [3, 7]],
            [xn_own[:, :]], [gxn[:, :]]), reads=[xn_own], writes=[gxn])
        for t in range(NB):
            xo = xnrow[t % 2]
            src_off = (t % 2) * OWNQ + 128 * (t // 2)
            q.dma("pool", xo[:, :], gxn[src_off:src_off + 128, :],
                  reads=[gxn], writes=[xo])
            for k in range(8):
                sem_name = "dt%d" % k
                q.sem(sem_name)
                if q.cnt[sem_name] > 0:
                    q.wt("sp", ("eager", sem_name, q.cnt[sem_name]))
                q.do("sp", lambda e, xo=xo, t=t, k=k: e.dma_start_transpose(
                    xnT[k][:, 128 * t:128 * (t + 1)], xo[:, 128 * k:128 * (k + 1)]),
                    reads=[xo], writes=[xnT[k]], inc=sem_name, n=16, is_dma=True)

    s_pro = contextlib.ExitStack()
    ropep = [mksb(s_pro)(f"ropep{i}", [128, 512]) for i in range(2)]
    for m in range(4):
        for (wn_a, wn_b, base) in (("wq_a", "wq_b", 0), ("wk_a", "wk_b", 16)):
            for nj in range(8):
                bia, pa = pp.get()
                bib, pb_ = pp.get()
                for k in range(8):
                    q.do("pe", lambda e, pa=pa, k=k, nj=nj, m=m, wn=wn_a: e.matmul(
                        pa[:, :512], WT[wn][:, k * 512 + 128 * m:k * 512 + 128 * m + 128],
                        xnT[k][:, 512 * nj:512 * (nj + 1)], start=(k == 0), stop=(k == 7)),
                        reads=[WT[wn_a], xnT[k]], writes=[pa], acc=(k > 0))
                for k in range(8):
                    q.do("pe", lambda e, pb_=pb_, k=k, nj=nj, m=m, wn=wn_b: e.matmul(
                        pb_[:, :512], WT[wn][:, k * 512 + 128 * m:k * 512 + 128 * m + 128],
                        xnT[k][:, 512 * nj:512 * (nj + 1)], start=(k == 0), stop=(k == 7)),
                        reads=[WT[wn_b], xnT[k]], writes=[pb_], acc=(k > 0))
                q.do("dve", lambda e, pa=pa, r0=ropep[0]: e.tensor_mul(
                    r0[:, :512], pa[:, :512], coslt[:, 0:512]),
                    reads=[pa, coslt], writes=[ropep[0]])
                q.do("dve", lambda e, pb_=pb_, r1=ropep[1]: e.tensor_mul(
                    r1[:, :512], pb_[:, :512], sinlt[:, 0:512]),
                    reads=[pb_, sinlt], writes=[ropep[1]])
                dst = bass.AP(xqkT[m], 32 * 32 * nj + base, [[2 * S, 128], [32, 32], [1, 16]])
                q.do("dve", lambda e, dst=dst, r0=ropep[0], r1=ropep[1]: e.tensor_add(
                    dst, r0[:, :512].rearrange("p (c i) -> p c i", i=16),
                    r1[:, :512].rearrange("p (c i) -> p c i", i=16)),
                    reads=[ropep[0], ropep[1]], writes=[xqkT[m]])
    if not fused:
        for m in range(4):
            q.dma("pool", xqro[m, :, :], xqkT[m][:, :], reads=[xqkT[m]], writes=[xqro])

    q.dma("sp", lrbt[:, :], W["lrb"][:, :], reads=[W["lrb"]], writes=[lrbt])
    for nj in range(8):
        bi, pb = pp.get()
        for k in range(8):
            q.do("pe", lambda e, pb=pb, k=k, nj=nj: e.matmul(
                pb[0:8, :512], WT["lrwT"][:, k * 8:(k + 1) * 8],
                xnT[k][:, 512 * nj:512 * (nj + 1)], start=(k == 0), stop=(k == 7)),
                reads=[WT["lrwT"], xnT[k]], writes=[pb], acc=(k > 0))
        q.do("act", lambda e, pb=pb, nj=nj: e.activation(
            lrS[0:8, 512 * nj:512 * (nj + 1)], pb[0:8, :512], AF.Sigmoid,
            bias=lrbt[:, :], scale=1.0), reads=[pb, lrbt], writes=[lrS])
    q.dma("pool", lr_d[:, :], lrS[:, :], reads=[lrS], writes=[lr_d])

    q.barrier()
    s_pro.close()
    s_bulk = contextlib.ExitStack()
    sbk = mksb(s_bulk)
    xkro = sbk("xkro", [128, 512])
    xkst = sbk("xkst", [128, 128])
    lrTt = sbk("lrTt", [128, 8], F32)
    bgt = sbk("bgt", [128, 512])
    ttile = sbk("ttile", [128, 512])
    tg2 = sbk("tg2", [128, 512])
    xpp = sbk("xpp", [128, 512])
    q.do("dve", lambda e: e.tensor_mul(bgt[:, :], CON["bbc"][:, :], CON["gbc"][:, :]),
         reads=[CON["bbc"], CON["gbc"]], writes=[bgt])
    for i in range(NB):
        biv, pbv = pp.get()
        for k in range(8):
            q.do("pe", lambda e, pb=pbv, k=k, i=i: e.matmul(
                pb[:, :512], xnT[k][:, 128 * i:128 * (i + 1)],
                WT["wv_o"][:, k * 512:(k + 1) * 512], start=(k == 0), stop=(k == 7)),
                reads=[xnT[k], WT["wv_o"]], writes=[pbv], acc=(k > 0))
        for m in range(4):
            bit, pbt = pp.get()
            inview = bass.AP(xqkT[m], 256 * i + 16,
                             [[2 * S, 128], [32, 8], [1, 16]])
            q.do("dve", lambda e, iv=inview: e.tensor_copy(xkst[:, :], iv),
                 reads=[xqkT[m]], writes=[xkst])
            q.do("pe", lambda e, pbt=pbt: e.transpose(
                pbt.bitcast(BF16)[:, 0:128], xkst[:, :], idt[:, :]),
                reads=[xkst, idt], writes=[pbt])
            q.copy("dve", xkro[:, 128 * m:128 * (m + 1)], pbt.bitcast(BF16)[:, 0:128],
                   reads=[pbt], writes=[xkro])
        bit, pbt = pp.get()
        q.do("pe", lambda e, pbt=pbt, i=i: e.transpose(
            pbt[0:8, 0:128].bitcast(F32) if False else pbt[0:128, 0:8],
            lrS[0:8, 128 * i:128 * (i + 1)], idt[:, :]),
            reads=[lrS, idt], writes=[pbt]) if False else None
        bil, pbl = pp.get()
        q.do("pe", lambda e, pbl=pbl, i=i: e.transpose(
            pbl.bitcast(BF16)[:, 0:8], lrS[0:8, 128 * i:128 * (i + 1)], idt[0:8, 0:8]),
            reads=[lrS, idt], writes=[pbl])
        q.copy("dve", lrTt[:, :], pbl.bitcast(BF16)[:, 0:8], reads=[pbl], writes=[lrTt])
        q.do("dve", lambda e, pb=pbv: e.tensor_sub(ttile[:, :], xkro[:, :], pb[:, :512]),
             reads=[xkro, pbv], writes=[ttile])
        q.do("dve", lambda e: e.tensor_mul(ttile[:, :], ttile[:, :], CON["gbc"][:, :]),
             reads=[ttile, CON["gbc"]], writes=[ttile])
        q.do("dve", lambda e: e.tensor_add(tg2[:, :], ttile[:, :], bgt[:, :]),
             reads=[ttile, bgt], writes=[tg2])
        q.dma("sp", tgt2_d[128 * i:128 * (i + 1), :], tg2[:, :], reads=[tg2], writes=[tgt2_d])
        for p in range(8):
            q.do("dve", lambda e, p=p: e.tensor_scalar(
                xpp[:, 64 * p:64 * (p + 1)], xkro[:, 64 * p:64 * (p + 1)],
                lrTt[:, p:p + 1], t15c[:, :], ALU.mult, ALU.mult),
                reads=[xkro, lrTt, t15c], writes=[xpp], acc=(p > 0))
        q.dma("sp", xkpp_d[128 * i:128 * (i + 1), :], xpp[:, :], reads=[xpp], writes=[xkpp_d])

    q.barrier()
    s_bulk.close()
    s_ln.close()
    s1.close()

    s2 = ctx.enter_context(contextlib.ExitStack())
    sb2 = mksb(s2)
    w1m = sb2("w1m", [128, 1024], F32)
    w1b = sb2("w1b", [128, 1024])
    khm = [sb2(f"khm{h}", [128, 512], F32) for h in range(2)]
    khb = [sb2(f"khb{h}", [128, 512]) for h in range(2)]
    w2tm = sb2("w2tm", [128, 1024], F32)
    w2tb = sb2("w2tb", [128, 1024])
    b1m = sb2("b1m", [8, 256], F32)
    b1bb = sb2("b1bb", [8, 512])
    b2m = sb2("b2m", [8, 64], F32)
    b2bb = sb2("b2bb", [8, 64])
    for dst, srcn in ((w1m, "w1c0"), (khm[0], "kh0"), (khm[1], "kh1"), (w2tm, "w2t0"),
                      (b1m, "b1c0"), (b2m, "b2c0")):
        q.dma("sp", dst[:, :], W[srcn][:, :], reads=[W[srcn]], writes=[dst])
    for dst, srcn in ((w1b, "w1c0"), (khb[0], "kh0"), (khb[1], "kh1"), (w2tb, "w2t0"),
                      (b2bb, "b2c0")):
        q.dma("pool", dst[:, :], W[srcn][:, :], reads=[W[srcn]], writes=[dst])
    q.dma("pool", b1bb[:, 0:256], W["b1c0"][:, :], reads=[W["b1c0"]], writes=[b1bb])
    q.dma("pool", b1bb[:, 256:512], W["b1c0"][:, :], reads=[W["b1c0"]], writes=[b1bb])
    tgs = [[sb2(f"tgs{P}_{r}", [128, 64]) for r in range(2)] for P in range(2)]
    xps = [[sb2(f"xps{P}_{r}", [128, 64]) for r in range(2)] for P in range(2)]
    lrc = [[sb2(f"lrc{P}_{r}", [128, 1], F32) for r in range(2)] for P in range(2)]
    z1sb = [sb2(f"z1sb{P}", [128, 256]) for P in range(2)]
    gu = [sb2(f"gu{P}", [128, 256]) for P in range(2)]
    gt_ = [sb2(f"gt{P}", [128, 256]) for P in range(2)]
    gw = [sb2(f"gw{P}", [128, 256]) for P in range(2)]
    x2sb = [sb2(f"x2sb{P}", [128, 256]) for P in range(2)]
    x2T = [sb2(f"x2T{P}", [128, 256]) for P in range(2)]
    z1bb = [sb2(f"z1bb{P}", [128, 256]) for P in range(2)]
    gub = [sb2(f"gub{P}", [128, 256]) for P in range(2)]
    gtb = [sb2(f"gtb{P}", [128, 256]) for P in range(2)]
    gwb = [sb2(f"gwb{P}", [128, 256]) for P in range(2)]
    x2bb = [sb2(f"x2bb{P}", [128, 256]) for P in range(2)]
    x2bT = [sb2(f"x2bT{P}", [128, 256]) for P in range(2)]
    gz2 = [sb2(f"gz2{P}", [128, 64]) for P in range(2)]
    gz2pp = [sb2(f"gz2pp{P}", [128, 64]) for P in range(2)]
    gz2T = sb2("gz2T", [128, 128])
    gz1 = [sb2(f"gz1{P}", [128, 256]) for P in range(2)]
    gz1pp = [sb2(f"gz1pp{P}", [128, 256]) for P in range(2)]
    statm = [sb2(f"statm{P}", [128, 16]) for P in range(2)]
    statm2 = [sb2(f"statm2{P}", [128, 16]) for P in range(2)]
    xcf = [sb2(f"xcf{P}", [128, 64], F32) for P in range(2)]
    xhf = [sb2(f"xhf{P}", [128, 64], F32) for P in range(2)]
    gof = [sb2(f"gof{P}", [128, 64], F32) for P in range(2)]
    scrf = [sb2(f"scrf{P}", [128, 64], F32) for P in range(2)]
    g2s = [sb2(f"g2s{P}", [128, 64], F32) for P in range(2)]
    mcol = [sb2(f"mcol{P}", [128, 1], F32) for P in range(2)]
    vcol = [sb2(f"vcol{P}", [128, 1], F32) for P in range(2)]
    scol = [sb2(f"scol{P}", [128, 1], F32) for P in range(2)]
    rstc = [sb2(f"rstc{P}", [128, 1], F32) for P in range(2)]
    hcol = [sb2(f"hcol{P}", [128, 1], F32) for P in range(2)]
    acol = [sb2(f"acol{P}", [128, 1], F32) for P in range(2)]
    gsum = [sb2(f"gsum{P}", [128, 1], F32) for P in range(2)]
    gxs = [sb2(f"gxs{P}", [128, 1], F32) for P in range(2)]
    z2es = [sb2(f"z2es{P}", [128, 64], F32) for P in range(2)]

    P2 = {0: p2A, 1: p2B}
    GS = {0: CON["gslotA"], 1: CON["gslotB"]}
    SELZT = {0: CON["selzTA"], 1: CON["selzTB"]}
    SELW = {0: CON["selwA"], 1: CON["selwB"]}
    for P in range(2):
        for rr_ in range(2):
            for tile in (tgs[P][rr_], xps[P][rr_], lrc[P][rr_]):
                q.do("pool", lambda e, tl=tile: e.memset(tl[:, :], 0.0),
                     reads=[], writes=[tile])
    for P in range(2):
        q.do("dve", lambda e, P=P: e.tensor_mul(g2s[P][:, :], GS[P][:, :], GS[P][:, :]),
             reads=[GS[P]], writes=[g2s[P]])

    def _slot_dma(dst_tile, src_d, base_off, jstride, width, t, P, semb):
        sem_name = "%s%d" % (semb, P)
        q.sem(sem_name)
        for j in range(4):
            if q.cnt[sem_name] > 0:
                q.wt("sp", ("eager", sem_name, q.cnt[sem_name]))
            tok = q.do("sp", lambda e, j=j, dst=dst_tile, src=src_d, o=base_off, js=jstride,
                       w=width: e.dma_start(
                           dst[32 * j:32 * j + 16, 0:w],
                           bass.AP(src, o + j * js, [[512 if w == 64 else 1, 16], [1, w]])),
                       reads=[src_d], writes=[dst_tile], inc=sem_name,
                       n=16, is_dma=True, acc=(j > 0))

    def stream_loads(t):
        r = t % 2
        for P in range(2):
            off = 16 * t * 512 + 64 * P
            _slot_dma(tgs[P][r], tgt2_d, off, 128, 64, t, P, "dg")
            _slot_dma(xps[P][r], xkpp_d, off, 128, 64, t, P, "dx")
            _slot_dma(lrc[P][r], lr_d, P * S + 16 * t, 2 * S, 1, t, P, "dl")

    def gelu(P, zin, u, tt, w, xout):
        q.do("dve", lambda e, z=zin, u=u: e.tensor_mul(u[:, :], z[:, :], z[:, :]),
             reads=[zin], writes=[u])
        q.do("dve", lambda e, z=zin, u=u, w=w: e.scalar_tensor_tensor(
            w[:, :], u[:, :], 0.044715, z[:, :], ALU.mult, ALU.mult),
            reads=[u, zin], writes=[w])
        q.do("dve", lambda e, z=zin, w=w: e.tensor_add(w[:, :], w[:, :], z[:, :]),
             reads=[w, zin], writes=[w])
        q.do("act", lambda e, w=w, tt=tt: e.activation(
            tt[:, :], w[:, :], AF.Tanh, scale=0.79788456), reads=[w], writes=[tt])
        q.do("dve", lambda e, w=w, tt=tt: e.tensor_scalar(
            w[:, :], tt[:, :], 0.5, 0.5, ALU.mult, ALU.add), reads=[tt], writes=[w])
        q.do("dve", lambda e, z=zin, w=w, xo=xout: e.tensor_mul(xo[:, :], w[:, :], z[:, :]),
             reads=[w, zin], writes=[xout])

    def gelu_bwd(P, zin, u, tt, w, bout):
        q.do("dve", lambda e, tt=tt, w=w: e.tensor_mul(w[:, :], tt[:, :], tt[:, :]),
             reads=[tt], writes=[w])
        q.do("dve", lambda e, w=w: e.tensor_scalar(
            w[:, :], w[:, :], -1.0, 1.0, ALU.mult, ALU.add), reads=[w], writes=[w])
        q.do("dve", lambda e, u=u, b=bout: e.tensor_scalar(
            b[:, :], u[:, :], 0.05351611215, 0.39894228, ALU.mult, ALU.add),
            reads=[u], writes=[bout])
        q.do("dve", lambda e, w=w, b=bout: e.tensor_mul(b[:, :], b[:, :], w[:, :]),
             reads=[bout, w], writes=[bout])
        q.do("dve", lambda e, z=zin, b=bout: e.tensor_mul(b[:, :], b[:, :], z[:, :]),
             reads=[bout, zin], writes=[bout])
        q.do("dve", lambda e, tt=tt, w=w: e.tensor_scalar(
            w[:, :], tt[:, :], 0.5, 0.5, ALU.mult, ALU.add), reads=[tt], writes=[w])
        q.do("dve", lambda e, w=w, b=bout: e.tensor_add(b[:, :], b[:, :], w[:, :]),
             reads=[bout, w], writes=[bout])

    stream_loads(0)
    for t in range(nstep):
        r = t % 2
        if t + 1 < nstep:
            stream_loads(t + 1)
        bz1 = {0: ps[0], 1: ps[1]}
        bC = {0: ps[2], 1: ps[3]}
        bz2 = {0: ps[4], 1: ps[5]}
        bg = bmix = ps[7]
        def st_e1(P):
            q.do("pe", lambda e, P=P: e.matmul(
                bz1[P][:, 0:512], SELW[P][:, :], b1bb[:, :], start=True, stop=False,
                skip_group_check=True),
                reads=[SELW[P], b1bb], writes=[bz1[P]])
            for j in range(4):
                p = 2 * j + P
                q.do("pe", lambda e, P=P, j=j, p=p, t=t: e.matmul(
                    bz1[P][32 * j:32 * j + 16, 0:256],
                    xqkT[p // 2][64 * (p % 2):64 * (p % 2) + 64,
                                 32 * t + 16:32 * t + 32],
                    w1b[64 * P:64 * P + 64, 256 * j:256 * (j + 1)],
                    start=False, stop=False, tile_position=(64 * P, 32 * j),
                    skip_group_check=True),
                    reads=[xqkT[p // 2], w1b], writes=[bz1[P]], acc=True)
                q.do("pe", lambda e, P=P, j=j, p=p, t=t: e.matmul(
                    bz1[P][32 * j:32 * j + 16, 256:512],
                    xqkT[p // 2][64 * (p % 2):64 * (p % 2) + 64,
                                 32 * t:32 * t + 16],
                    w1b[64 * P:64 * P + 64, 256 * j:256 * (j + 1)],
                    start=False, stop=(j == 3), tile_position=(64 * P, 32 * j),
                    skip_group_check=True),
                    reads=[xqkT[p // 2], w1b], writes=[bz1[P]], acc=True)
        def st_attn1(P):
            q.do("pe", lambda e, P=P: e.matmul(
                bC[P][:, 0:16], onesc[0:1, 0:128], onesc[0:1, 0:16], start=True, stop=False,
                skip_group_check=True),
                reads=[onesc], writes=[bC[P]])
            for j in range(4):
                p = 2 * j + P
                q.do("pe", lambda e, P=P, j=j, p=p, t=t: e.matmul(
                    bC[P][32 * j:32 * j + 16, 0:16],
                    xqkT[p // 2][64 * (p % 2):64 * (p % 2) + 64,
                                 32 * t + 16:32 * t + 32],
                    xqkT[p // 2][64 * (p % 2):64 * (p % 2) + 64, 32 * t:32 * t + 16],
                    start=False, stop=(j == 3), tile_position=(64 * P, 32 * j),
                    skip_group_check=True),
                    reads=[xqkT[p // 2]], writes=[bC[P]], acc=True)
        def st_gelu1(P):
            q.copy("dve", z1sb[P][:, :], bz1[P][:, 0:256], reads=[bz1[P]], writes=[z1sb[P]])
            gelu(P, z1sb[P], gu[P], gt_[P], gw[P], x2sb[P])
            for h in range(2):
                bi6, pb6 = pp.get(6)
                q.do("pe", lambda e, P=P, h=h, pb6=pb6: e.transpose(
                    pb6.bitcast(BF16)[:, 0:128], x2sb[P][:, 128 * h:128 * (h + 1)],
                    idt[:, :]), reads=[x2sb[P], idt], writes=[pb6])
                q.copy("dve", x2T[P][:, 128 * h:128 * (h + 1)], pb6.bitcast(BF16)[:, 0:128],
                       reads=[pb6], writes=[x2T[P]])
        def st_e2a(P):
            q.do("pe", lambda e, P=P: e.matmul(
                bz2[P][:, 0:64], SELW[P][:, :], b2bb[:, :], start=True, stop=False,
                skip_group_check=True),
                reads=[SELW[P], b2bb], writes=[bz2[P]])
            for j in range(4):
                for h in range(2):
                    q.do("pe", lambda e, P=P, j=j, h=h: e.matmul(
                        bz2[P][32 * j:32 * j + 16, 0:64],
                        x2T[P][:, 128 * h + 32 * j:128 * h + 32 * j + 16],
                        khb[h][:, 64 * (2 * j + P):64 * (2 * j + P) + 64],
                        start=False, stop=(j == 3 and h == 1), tile_position=(0, 32 * j),
                        skip_group_check=True),
                        reads=[x2T[P], khb[h]], writes=[bz2[P]], acc=True)
        def st_ln(P):
            q.do("dve", lambda e, P=P: e.reduce_sum(mcol[P][:, :], bz2[P][:, 0:64], axis=AX),
                 reads=[bz2[P]], writes=[mcol[P]])
            q.do("dve", lambda e, P=P: e.tensor_scalar_mul(
                mcol[P][:, :], mcol[P][:, :], 1.0 / 64.0), reads=[mcol[P]], writes=[mcol[P]])
            q.do("dve", lambda e, P=P: e.tensor_scalar(
                xcf[P][:, :], bz2[P][:, 0:64], mcol[P][:, :], None, ALU.subtract),
                reads=[bz2[P], mcol[P]], writes=[xcf[P]])
            q.do("dve", lambda e, P=P: e.scalar_tensor_tensor(
                scrf[P][:, :], xcf[P][:, :], 0.0, xcf[P][:, :], ALU.add, ALU.mult,
                accum_out=vcol[P][:, :]), reads=[xcf[P]], writes=[scrf[P], vcol[P]])
            q.do("dve", lambda e, P=P: e.tensor_scalar(
                vcol[P][:, :], vcol[P][:, :], 1.0 / 64.0, 1e-6, ALU.mult, ALU.add),
                reads=[vcol[P]], writes=[vcol[P]])
            q.do("dve", lambda e, P=P: e.reciprocal(scol[P][:, :], vcol[P][:, :]),
                 reads=[vcol[P]], writes=[scol[P]])
            q.do("act", lambda e, P=P: e.activation(
                scrf[P][:, :], xcf[P][:, :], AF.Abs, accum_out=acol[P][:, :]),
                reads=[xcf[P]], writes=[scrf[P], acol[P]])
            q.do("dve", lambda e, P=P: e.tensor_scalar(
                acol[P][:, :], acol[P][:, :], 1.0 / 64.0, 8e-4, ALU.mult, ALU.add),
                reads=[acol[P]], writes=[acol[P]])
            q.do("dve", lambda e, P=P: e.reciprocal(rstc[P][:, :], acol[P][:, :]),
                 reads=[acol[P]], writes=[rstc[P]])
            q.do("dve", lambda e, P=P: e.tensor_scalar_mul(
                rstc[P][:, :], rstc[P][:, :], 1.2533141), reads=[rstc[P]], writes=[rstc[P]])
            for it in range(HEFF):
                q.do("dve", lambda e, P=P: e.reciprocal(hcol[P][:, :], rstc[P][:, :]),
                     reads=[rstc[P]], writes=[hcol[P]])
                q.do("dve", lambda e, P=P: e.tensor_mul(
                    hcol[P][:, :], hcol[P][:, :], scol[P][:, :]),
                    reads=[hcol[P], scol[P]], writes=[hcol[P]])
                q.do("dve", lambda e, P=P: e.tensor_add(
                    rstc[P][:, :], rstc[P][:, :], hcol[P][:, :]),
                    reads=[rstc[P], hcol[P]], writes=[rstc[P]])
                q.do("dve", lambda e, P=P: e.tensor_scalar_mul(
                    rstc[P][:, :], rstc[P][:, :], 0.5), reads=[rstc[P]], writes=[rstc[P]])
            q.do("dve", lambda e, P=P: e.tensor_scalar(
                xhf[P][:, :], xcf[P][:, :], rstc[P][:, :], None, ALU.mult),
                reads=[xcf[P], rstc[P]], writes=[xhf[P]])
            q.do("dve", lambda e, P=P: e.tensor_mul(gof[P][:, :], g2s[P][:, :], xhf[P][:, :]),
                 reads=[g2s[P], xhf[P]], writes=[gof[P]])
            q.do("dve", lambda e, P=P, r=r: e.scalar_tensor_tensor(
                gof[P][:, :], gof[P][:, :], 1.0, tgs[P][r][:, :], ALU.mult, ALU.add,
                accum_out=gsum[P][:, :]),
                reads=[gof[P], tgs[P][r]], writes=[gof[P], gsum[P]])
            q.do("dve", lambda e, P=P: e.scalar_tensor_tensor(
                scrf[P][:, :], gof[P][:, :], 1.0, xhf[P][:, :], ALU.mult, ALU.mult,
                accum_out=gxs[P][:, :]),
                reads=[gof[P], xhf[P]], writes=[scrf[P], gxs[P]])
            q.do("dve", lambda e, P=P: e.tensor_scalar(
                scrf[P][:, :], gof[P][:, :], 64.0, gsum[P][:, :], ALU.mult, ALU.subtract),
                reads=[gof[P], gsum[P]], writes=[scrf[P]])
            q.do("dve", lambda e, P=P: e.tensor_scalar(
                xhf[P][:, :], xhf[P][:, :], gxs[P][:, :], None, ALU.mult),
                reads=[xhf[P], gxs[P]], writes=[xhf[P]])
            q.do("dve", lambda e, P=P: e.tensor_sub(scrf[P][:, :], scrf[P][:, :], xhf[P][:, :]),
                 reads=[scrf[P], xhf[P]], writes=[scrf[P]])
            q.do("dve", lambda e, P=P: e.tensor_scalar(
                gz2[P][:, :], scrf[P][:, :], rstc[P][:, :], 1.0 / 64.0, ALU.mult, ALU.mult),
                reads=[scrf[P], rstc[P]], writes=[gz2[P]])
            q.do("dve", lambda e, P=P, r=r: e.tensor_scalar(
                gz2pp[P][:, :], gz2[P][:, :], lrc[P][r][:, :], t15c[:, :], ALU.mult, ALU.mult),
                reads=[gz2[P], lrc[P][r], t15c], writes=[gz2pp[P]])
            bi6, pb6 = pp.get(6)
            q.do("pe", lambda e, P=P, pb6=pb6: e.transpose(
                pb6.bitcast(BF16)[64 * P:64 * P + 64, 0:128], gz2[P][:, 0:64], idt[:, :]),
                reads=[gz2[P], idt], writes=[pb6])
            q.copy("dve", gz2T[64 * P:64 * P + 64, :],
                   pb6.bitcast(BF16)[64 * P:64 * P + 64, 0:128],
                   reads=[pb6], writes=[gz2T])
        def st_e3(P):
            q.do("pe", lambda e, P=P, bg=bg: e.matmul(
                bg[:, 256 * P:256 * P + 256], zeroc[0:1, 0:128], onesc[0:1, 0:256],
                start=True, stop=False, skip_group_check=True),
                reads=[zeroc, onesc], writes=[bg])
            for j in range(4):
                q.do("pe", lambda e, P=P, j=j, bg=bg: e.matmul(
                    bg[32 * j:32 * j + 16, 256 * P:256 * P + 256],
                    gz2T[64 * P:64 * P + 64, 32 * j:32 * j + 16],
                    w2tb[64 * P:64 * P + 64, 256 * j:256 * (j + 1)],
                    start=False, stop=(j == 3), tile_position=(64 * P, 32 * j),
                    skip_group_check=True),
                    reads=[gz2T, w2tb], writes=[bg], acc=True)
            gelu_bwd(P, z1sb[P], gu[P], gt_[P], gw[P], gz1[P])
            q.do("dve", lambda e, P=P, bg=bg: e.tensor_mul(
                gz1[P][:, :], gz1[P][:, :], bg[:, 256 * P:256 * P + 256]),
                reads=[gz1[P], bg], writes=[gz1[P]])
            q.do("dve", lambda e, P=P, r=r: e.tensor_scalar(
                gz1pp[P][:, :], gz1[P][:, :], lrc[P][r][:, :], t15c[:, :], ALU.mult, ALU.mult),
                reads=[gz1[P], lrc[P][r], t15c], writes=[gz1pp[P]])
        def st_mix(P):
            q.do("dve", lambda e, P=P: e.tensor_mul(
                statm[P][:, :], bC[P][:, 0:16], P2[P][:, :]),
                reads=[bC[P], P2[P]], writes=[statm[P]])
            q.do("dve", lambda e, P=P, r=r: e.tensor_scalar(
                statm[P][:, :], statm[P][:, :], lrc[P][r][:, :], None, ALU.mult),
                reads=[statm[P], lrc[P][r]], writes=[statm[P]])
            q.do("pe", lambda e, P=P, bmix=bmix: e.matmul(
                bmix[:, 256 * P:256 * P + 256], zeroc[0:1, 0:128], onesc[0:1, 0:256],
                start=True, stop=False, skip_group_check=True),
                reads=[zeroc, onesc], writes=[bmix])
            for j in range(4):
                q.do("pe", lambda e, P=P, j=j, bmix=bmix: e.matmul(
                    bmix[32 * j:32 * j + 16, 256 * P:256 * P + 256],
                    statm[P][32 * j:32 * j + 16, 0:16],
                    gz1[P][32 * j:32 * j + 16, 0:256],
                    start=False, stop=(j == 3), tile_position=(32 * j, 32 * j),
                    skip_group_check=True),
                    reads=[statm[P], gz1[P]], writes=[bmix], acc=True)
        def st_gelu2(P):
            q.copy("dve", z1bb[P][:, :], bz1[P][:, 256:512], reads=[bz1[P]], writes=[z1bb[P]])
            q.do("dve", lambda e, P=P, bmix=bmix: e.tensor_add(
                z1bb[P][:, :], z1bb[P][:, :], bmix[:, 256 * P:256 * P + 256]),
                reads=[z1bb[P], bmix], writes=[z1bb[P]])
            gelu(P, z1bb[P], gub[P], gtb[P], gwb[P], x2bb[P])
            for h in range(2):
                bi6, pb6 = pp.get(6)
                q.do("pe", lambda e, P=P, h=h, pb6=pb6: e.transpose(
                    pb6.bitcast(BF16)[:, 0:128], x2bb[P][:, 128 * h:128 * (h + 1)],
                    idt[:, :]), reads=[x2bb[P], idt], writes=[pb6])
                q.copy("dve", x2bT[P][:, 128 * h:128 * (h + 1)], pb6.bitcast(BF16)[:, 0:128],
                       reads=[pb6], writes=[x2bT[P]])
        def st_attn2(P):
            q.do("pe", lambda e, P=P: e.matmul(
                bC[P][:, 16:32], onesc[0:1, 0:128], onesc[0:1, 0:16], start=True, stop=False,
                skip_group_check=True),
                reads=[onesc], writes=[bC[P]])
            for j in range(4):
                for h in range(2):
                    q.do("pe", lambda e, P=P, j=j, h=h: e.matmul(
                        bC[P][32 * j:32 * j + 16, 16:32],
                        x2T[P][:, 128 * h + 32 * j:128 * h + 32 * j + 16],
                        x2bT[P][:, 128 * h + 32 * j:128 * h + 32 * j + 16],
                        start=False, stop=(j == 3 and h == 1), tile_position=(0, 32 * j),
                        skip_group_check=True),
                        reads=[x2T[P], x2bT[P]], writes=[bC[P]], acc=True)
        def st_e2b(P):
            q.do("pe", lambda e, P=P: e.matmul(
                bz2[P][:, 128:192], SELW[P][:, :], b2bb[:, :], start=True, stop=False,
                skip_group_check=True),
                reads=[SELW[P], b2bb], writes=[bz2[P]])
            for j in range(4):
                for h in range(2):
                    q.do("pe", lambda e, P=P, j=j, h=h: e.matmul(
                        bz2[P][32 * j:32 * j + 16, 128:192],
                        x2bT[P][:, 128 * h + 32 * j:128 * h + 32 * j + 16],
                        khb[h][:, 64 * (2 * j + P):64 * (2 * j + P) + 64],
                        start=False, stop=False, tile_position=(0, 32 * j),
                        skip_group_check=True),
                        reads=[x2bT[P], khb[h]], writes=[bz2[P]], acc=True)
        def st_mix2(P):
            q.do("dve", lambda e, P=P: e.tensor_mul(
                statm2[P][:, :], bC[P][:, 16:32], P2[P][:, :]),
                reads=[bC[P], P2[P]], writes=[statm2[P]])
            q.do("dve", lambda e, P=P, r=r: e.tensor_scalar(
                statm2[P][:, :], statm2[P][:, :], lrc[P][r][:, :], None, ALU.mult),
                reads=[statm2[P], lrc[P][r]], writes=[statm2[P]])
            for j in range(4):
                q.do("pe", lambda e, P=P, j=j: e.matmul(
                    bz2[P][32 * j:32 * j + 16, 128:192],
                    statm2[P][32 * j:32 * j + 16, 0:16],
                    gz2[P][32 * j:32 * j + 16, 0:64],
                    start=False, stop=(j == 3), tile_position=(32 * j, 32 * j),
                    skip_group_check=True),
                    reads=[statm2[P], gz2[P]], writes=[bz2[P]], acc=True)
        def st_out(P):
            q.copy("dve", z2es[P][:, :], bz2[P][:, 128:192], reads=[bz2[P]], writes=[z2es[P]])
            for j in range(4):
                p = 2 * j + P
                q.dma("pool", bass.AP(z2bo, (16 * t) * 512 + 64 * p, [[512, 16], [1, 64]]),
                      z2es[P][32 * j:32 * j + 16, 0:64], reads=[z2es[P]], writes=[z2bo])
        stages = [st_e1, st_attn1, st_gelu1, st_e2a, st_ln, st_e3, st_mix,
                  st_gelu2, st_attn2, st_e2b, st_mix2, st_out]
        for si in range(len(stages) + SKEW):
            if si < len(stages):
                stages[si](0)
            if si >= SKEW:
                stages[si - SKEW](1)
        bu1, bu2 = ps[0], ps[1]
        for P in range(2):
            for j in range(4):
                bu = bu1 if j < 2 else bu2
                q.do("pe", lambda e, P=P, j=j, r=t % 2, bu=bu: e.matmul(
                    bu[64 * P:64 * P + 64, 256 * (j % 2):256 * (j % 2) + 256],
                    xps[P][r][32 * j:32 * j + 16, 0:64],
                    gz1[P][32 * j:32 * j + 16, 0:256],
                    start=True, stop=True, tile_position=(32 * j, 64 * P)),
                    reads=[xps[P][t % 2], gz1[P]], writes=[bu])
        q.do("dve", lambda e: e.scalar_tensor_tensor(
            w1m[:, 0:512], bu1[:, 0:512], -1.0, w1m[:, 0:512], ALU.mult, ALU.add),
            reads=[bu1, w1m], writes=[w1m])
        q.do("dve", lambda e: e.scalar_tensor_tensor(
            w1m[:, 512:1024], bu2[:, 0:512], -1.0, w1m[:, 512:1024], ALU.mult, ALU.add),
            reads=[bu2, w1m], writes=[w1m])
        q.copy("pool", w1b[:, 0:512], w1m[:, 0:512], reads=[w1m], writes=[w1b])
        q.copy("pool", w1b[:, 512:1024], w1m[:, 512:1024], reads=[w1m], writes=[w1b])
        bk0, bk1 = ps[4], ps[5]
        for P in range(2):
            for j in range(4):
                p = 2 * j + P
                for h, bk in ((0, bk0), (1, bk1)):
                    q.do("pe", lambda e, P=P, j=j, p=p, h=h, bk=bk: e.matmul(
                        bk[:, 64 * p:64 * p + 64],
                        x2sb[P][32 * j:32 * j + 16, 128 * h:128 * h + 128],
                        gz2pp[P][32 * j:32 * j + 16, 0:64],
                        start=True, stop=True, tile_position=(32 * j, 0)),
                        reads=[x2sb[P], gz2pp[P]], writes=[bk])
        for h, bk in ((0, bk0), (1, bk1)):
            q.do("dve", lambda e, h=h, bk=bk: e.scalar_tensor_tensor(
                khm[h][:, :], bk[:, 0:512], -1.0, khm[h][:, :], ALU.mult, ALU.add),
                reads=[bk, khm[h]], writes=[khm[h]])
            q.copy("pool", khb[h][:, :], khm[h][:, :], reads=[khm[h]], writes=[khb[h]])
        bt0, bt1 = ps[2], ps[3]
        for P in range(2):
            for j in range(4):
                p = 2 * j + P
                bt = bt0 if j < 2 else bt1
                q.do("pe", lambda e, P=P, j=j, p=p, bt=bt: e.matmul(
                    bt[64 * P:64 * P + 64, 256 * (j % 2):256 * (j % 2) + 256],
                    gz2pp[P][32 * j:32 * j + 16, 0:64],
                    x2sb[P][32 * j:32 * j + 16, 0:256],
                    start=True, stop=True, tile_position=(32 * j, 64 * P)),
                    reads=[gz2pp[P], x2sb[P]], writes=[bt])
        q.do("dve", lambda e: e.scalar_tensor_tensor(
            w2tm[:, 0:512], bt0[:, 0:512], -1.0, w2tm[:, 0:512], ALU.mult, ALU.add),
            reads=[bt0, w2tm], writes=[w2tm])
        q.do("dve", lambda e: e.scalar_tensor_tensor(
            w2tm[:, 512:1024], bt1[:, 0:512], -1.0, w2tm[:, 512:1024], ALU.mult, ALU.add),
            reads=[bt1, w2tm], writes=[w2tm])
        q.copy("pool", w2tb[:, 0:512], w2tm[:, 0:512], reads=[w2tm], writes=[w2tb])
        q.copy("pool", w2tb[:, 512:1024], w2tm[:, 512:1024], reads=[w2tm], writes=[w2tb])
        bb = ps[6]
        for P in range(2):
            q.do("pe", lambda e, P=P, bb=bb: e.matmul(
                bb[0:8, 0:256], SELZT[P][:, :], gz1pp[P][:, 0:256],
                start=(P == 0), stop=(P == 1), skip_group_check=True),
                reads=[SELZT[P], gz1pp[P]], writes=[bb], acc=(P == 1))
        for P in range(2):
            q.do("pe", lambda e, P=P, bb=bb: e.matmul(
                bb[0:8, 256:320], SELZT[P][:, :], gz2pp[P][:, 0:64],
                start=(P == 0), stop=(P == 1), skip_group_check=True),
                reads=[SELZT[P], gz2pp[P]], writes=[bb], acc=(P == 1))
        q.do("dve", lambda e, bb=bb: e.scalar_tensor_tensor(
            b1m[:, :], bb[0:8, 0:256], -1.0, b1m[:, :], ALU.mult, ALU.add),
            reads=[bb, b1m], writes=[b1m])
        q.do("dve", lambda e, bb=bb: e.scalar_tensor_tensor(
            b2m[:, :], bb[0:8, 256:320], -1.0, b2m[:, :], ALU.mult, ALU.add),
            reads=[bb, b2m], writes=[b2m])
        q.copy("pool", b1bb[:, 0:256], b1m[:, :], reads=[b1m], writes=[b1bb])
        q.copy("pool", b1bb[:, 256:512], b1m[:, :], reads=[b1m], writes=[b1bb])
        q.copy("pool", b2bb[:, :], b2m[:, :], reads=[b2m], writes=[b2bb])

    if dbg_o:
        dbgmap = {"d_w1m": (w1m, 128, 1024), "d_gz1A": (gz1[0], 128, 256),
                  "d_x2A": (x2sb[0], 128, 256), "d_lrS": (lrS, 8, 512),
                  "d_xqk0": (xqkT[0], 128, 512),
                  "d_khm0": (khm[0], 128, 512), "d_khm1": (khm[1], 128, 512),
                  "d_w2tm": (w2tm, 128, 1024), "d_b1m": (b1m, 8, 256),
                  "d_b2m": (b2m, 8, 64), "d_gz2A": (gz2[0], 128, 64),
                  "d_tgsA": (tgs[0][0], 128, 64), "d_xpsA": (xps[0][0], 128, 64),
                  "d_z1bbA": (z1bb[0], 128, 256),
                  "d_tgsA1": (tgs[0][1], 128, 64), "d_xpsA1": (xps[0][1], 128, 64),
                  "d_lrcA1": (lrc[0][1], 128, 1), "d_w1b": (w1b, 128, 1024),
                  "d_b1bb": (b1bb, 8, 512), "d_z1sbA": (z1sb[0], 128, 256),
                  "d_xcfA": (xcf[0], 128, 64), "d_rstA": (rstc[0], 128, 1),
                  "d_gofA": (gof[0], 128, 64), "d_xhfA": (xhf[0], 128, 64)}
        for nname, (tt_, rr, cc) in dbgmap.items():
            if nname in dbg_o:
                q.dma("pool", dbg_o[nname][:, :], tt_[:rr, 0:cc],
                      reads=[tt_], writes=[dbg_o[nname]])
    if fused:
        q.barrier()
        s2.close()
        sp_ = contextlib.ExitStack()
        sbp = mksb(sp_)
        z2t = [sbp(f"pz2t{r}", [128, 512], F32) for r in range(2)]
        xqs = sbp("pxqs", [128, 128])
        xqt = sbp("pxqt", [128, 512], F32)
        pxc = sbp("pxc", [128, 512], F32)
        mu8 = sbp("pmu8", [128, 8], F32)
        v8 = sbp("pv8", [128, 8], F32)
        a8 = sbp("pa8", [128, 8], F32)
        sc8 = sbp("psc8", [128, 8], F32)
        rs8 = sbp("prs8", [128, 8], F32)
        h8 = sbp("ph8", [128, 8], F32)
        sq8 = sbp("psq8", [128, 64], F32)
        ab8 = sbp("pab8", [128, 512], F32)
        for i in range(NB):
            zt = z2t[i % 2]
            q.dma("sp", zt[:, :], z2bo[128 * i:128 * (i + 1), :],
                  reads=[z2bo], writes=[zt])
            for m in range(4):
                iv = bass.AP(xqkT[m], 256 * i, [[2 * S, 128], [32, 8], [1, 16]])
                q.do("dve", lambda e, iv=iv: e.tensor_copy(xqs[:, :], iv),
                     reads=[xqkT[m]], writes=[xqs])
                bi, pb = pp.get()
                q.do("pe", lambda e, pb=pb: e.transpose(
                    pb.bitcast(BF16)[:, 0:128], xqs[:, :], idt[:, :]),
                    reads=[xqs, idt], writes=[pb])
                q.copy("dve", xqt[:, 128 * m:128 * (m + 1)], pb.bitcast(BF16)[:, 0:128],
                       reads=[pb], writes=[xqt])
            for g in range(8):
                q.do("dve", lambda e, g=g, zt=zt: e.reduce_sum(
                    mu8[:, g:g + 1], zt[:, 64 * g:64 * (g + 1)], axis=AX),
                    reads=[zt], writes=[mu8], acc=(g > 0))
            q.do("dve", lambda e: e.tensor_scalar_mul(mu8[:, :], mu8[:, :], 1.0 / 64.0),
                 reads=[mu8], writes=[mu8])
            for g in range(8):
                q.do("dve", lambda e, g=g, zt=zt: e.tensor_scalar(
                    pxc[:, 64 * g:64 * (g + 1)], zt[:, 64 * g:64 * (g + 1)],
                    mu8[:, g:g + 1], None, ALU.subtract),
                    reads=[zt, mu8], writes=[pxc], acc=(g > 0))
            for g in range(8):
                q.do("dve", lambda e, g=g: e.scalar_tensor_tensor(
                    sq8[:, :], pxc[:, 64 * g:64 * (g + 1)], 0.0,
                    pxc[:, 64 * g:64 * (g + 1)], ALU.add, ALU.mult,
                    accum_out=v8[:, g:g + 1]),
                    reads=[pxc], writes=[sq8, v8])
            for g in range(8):
                q.do("act", lambda e, g=g: e.activation(
                    ab8[:, 64 * g:64 * (g + 1)], pxc[:, 64 * g:64 * (g + 1)], AF.Abs,
                    accum_out=a8[:, g:g + 1]),
                    reads=[pxc], writes=[ab8, a8], acc=(g > 0))
            q.do("dve", lambda e: e.tensor_scalar(
                v8[:, :], v8[:, :], 1.0 / 64.0, 1e-6, ALU.mult, ALU.add),
                reads=[v8], writes=[v8])
            q.do("dve", lambda e: e.reciprocal(sc8[:, :], v8[:, :]),
                 reads=[v8], writes=[sc8])
            q.do("dve", lambda e: e.tensor_scalar(
                a8[:, :], a8[:, :], 1.0 / 64.0, 8e-4, ALU.mult, ALU.add),
                reads=[a8], writes=[a8])
            q.do("dve", lambda e: e.reciprocal(rs8[:, :], a8[:, :]),
                 reads=[a8], writes=[rs8])
            q.do("dve", lambda e: e.tensor_scalar_mul(rs8[:, :], rs8[:, :], 1.2533141),
                 reads=[rs8], writes=[rs8])
            for _ in range(HEFF):
                q.do("dve", lambda e: e.reciprocal(h8[:, :], rs8[:, :]),
                     reads=[rs8], writes=[h8])
                q.do("dve", lambda e: e.tensor_mul(h8[:, :], h8[:, :], sc8[:, :]),
                     reads=[h8, sc8], writes=[h8])
                q.do("dve", lambda e: e.tensor_add(rs8[:, :], rs8[:, :], h8[:, :]),
                     reads=[rs8, h8], writes=[rs8])
                q.do("dve", lambda e: e.tensor_scalar_mul(rs8[:, :], rs8[:, :], 0.5),
                     reads=[rs8], writes=[rs8])
            for g in range(8):
                q.do("dve", lambda e, g=g: e.tensor_scalar(
                    pxc[:, 64 * g:64 * (g + 1)], pxc[:, 64 * g:64 * (g + 1)],
                    rs8[:, g:g + 1], None, ALU.mult),
                    reads=[pxc, rs8], writes=[pxc], acc=(g > 0))
            q.do("dve", lambda e: e.tensor_mul(pxc[:, :], pxc[:, :], CON["gbc"][:, :]),
                 reads=[pxc, CON["gbc"]], writes=[pxc])
            q.do("dve", lambda e: e.tensor_add(pxc[:, :], pxc[:, :], CON["bbc"][:, :]),
                 reads=[pxc, CON["bbc"]], writes=[pxc])
            q.do("dve", lambda e: e.tensor_add(pxc[:, :], pxc[:, :], xqt[:, :]),
                 reads=[pxc, xqt], writes=[pxc])
            q.dma("sp", xqwo[128 * i:128 * (i + 1), :], pxc[:, :],
                  reads=[pxc], writes=[xqwo])
        q.barrier()
        sp_.close()
        q.do("pool", lambda e: e.collective_compute(
            "AllGather", ALU.bypass, [[0, 4], [1, 5], [2, 6], [3, 7]],
            [xqwo[:, :]], [gxq[:, :]]), reads=[xqwo], writes=[gxq])
        return {"gx1": gx1, "gxn": gxn, "gxq": gxq}
    q.wt("pool", q.last_tok["pool"])
    q.wt("sp", q.last_tok["sp"])
    q.build()
    ctx.close()
    return nc



TOK3 = S // 2
NT3 = TOK3 // 128

L1P_IN = [
    ("z2b", [TOK3, D]), ("xqp", [TOK3, D]), ("xn1p", [TOK3, D]), ("x1p", [TOK3, D]),
    ("glnb", [128, D]), ("blnb", [128, D]), ("gpnb", [128, D]), ("bpnb", [128, D]),
    ("wot", [D, D]), ("ffw", [D, D]), ("ffb", [1, D]), ("ones1", [1, 128]),
]

L1P_IN_FUSED = [
    ("gpnb", [128, D]), ("bpnb", [128, D]),
    ("wot", [D, D]), ("ffw", [D, D]), ("ffb", [1, D]), ("ones1", [1, 128]),
]


def build_l1p(dbg=(), env=None, gbufs=None):
    fused = env is not None
    pfx = "p_" if fused else ""
    if env is None:
        nc = bass.Bass("TRN2", target_bir_lowering=False)
        ctx = contextlib.ExitStack()
        q = Q(nc, ctx)
        pp = Psum(q, nc, ctx)
    else:
        nc, ctx, q, pp = env
    ps = pp.banks

    W = {n: nc.dram_tensor(pfx + n, list(s), F32, kind="ExternalInput")
         for n, s in (L1P_IN_FUSED if fused else L1P_IN)}
    ntile = NB if fused else NT3
    x2o = nc.dram_tensor("x2o", [128 * ntile, D], F32, kind="ExternalOutput")
    wsc_d = nc.dram_tensor(pfx + "wsc_d", [1, D], F32)
    dbg_o = {n: nc.dram_tensor(n, list(s), F32, kind="ExternalOutput") for n, s in dbg}
    if fused:
        gx1, gxn, gxq = gbufs["gx1"], gbufs["gxn"], gbufs["gxq"]

    def mksb(c):
        return lambda name, shape, dt=BF16: c.enter_context(
            nc.sbuf_tensor(pfx + name, list(shape), dt))

    sb = mksb(ctx)
    ones1 = sb("t_ones1", [1, 128], F32)
    if not fused:
        glnb = sb("t_glnb", [128, D], F32)
        blnb = sb("t_blnb", [128, D], F32)
    gpnb = sb("t_gpnb", [128, D], F32)
    bpnb = sb("t_bpnb", [128, D], F32)
    wotb = [sb(f"wotb{k}", [128, D]) for k in range(8)]
    wqT = [sb(f"wqT{k}", [128, D]) for k in range(8)]
    wsbc = sb("wsbc", [128, D], F32)
    bfbc = sb("bfbc", [128, D], F32)
    names = (("ones1", ones1), ("gpnb", gpnb), ("bpnb", bpnb)) if fused else (
        ("ones1", ones1), ("glnb", glnb), ("blnb", blnb),
        ("gpnb", gpnb), ("bpnb", bpnb))
    for n, dst in names:
        q.dma("sp", dst[:, :], W[n][:, :], reads=[W[n]], writes=[dst])
    for k in range(8):
        q.dma("pool", wotb[k][:, :], W["wot"][128 * k:128 * (k + 1), :],
              reads=[W["wot"]], writes=[wotb[k]])

    s_w = contextlib.ExitStack()
    sbw = mksb(s_w)
    wsl = sbw("wsl", [128, D], F32)
    wtmp = sbw("wtmp", [128, D], F32)
    wint = sbw("wint", [128, D], mybir.dt.int32)
    wqs = [sbw(f"wqs{k}", [128, D]) for k in range(8)]
    wam = sbw("wam", [128, 1], F32)
    for k in range(8):
        q.dma("sp", wsl[:, :], W["ffw"][128 * k:128 * (k + 1), :],
              reads=[W["ffw"]], writes=[wsl])
        q.do("dve", lambda e: e.reduce_max(wam[:, :], wsl[:, :], axis=AX,
                                           apply_absolute_value=True),
             reads=[wsl], writes=[wam])
        q.do("dve", lambda e: e.tensor_scalar(wam[:, :], wam[:, :], 1e-8, None, ALU.max),
             reads=[wam], writes=[wam])
        q.do("dve", lambda e: e.tensor_scalar_mul(wam[:, :], wam[:, :], 1.0 / 127.0),
             reads=[wam], writes=[wam])
        q.dma("sp", bass.AP(wsc_d, 128 * k, [[1, 128], [1, 1]]), wam[:, 0:1],
              reads=[wam], writes=[wsc_d])
        q.do("dve", lambda e: e.reciprocal(wam[:, :], wam[:, :]), reads=[wam], writes=[wam])
        q.do("dve", lambda e: e.tensor_scalar(
            wtmp[:, :], wsl[:, :], wam[:, :], None, ALU.mult),
            reads=[wsl, wam], writes=[wtmp])
        q.do("dve", lambda e: e.tensor_scalar(
            wtmp[:, :], wtmp[:, :], 12582912.0, -12582912.0, ALU.add, ALU.add),
            reads=[wtmp], writes=[wtmp])
        q.do("dve", lambda e: e.tensor_copy(wint[:, :], wtmp[:, :]),
             reads=[wtmp], writes=[wint])
        q.do("dve", lambda e, k=k: e.tensor_copy(wqs[k][:, :], wint[:, :]),
             reads=[wint], writes=[wqs[k]])
    for k in range(8):
        for f in range(8):
            sem_name = "dt%d" % f
            q.sem(sem_name)
            if q.cnt[sem_name] > 0:
                q.wt("sp", ("eager", sem_name, q.cnt[sem_name]))
            q.do("sp", lambda e, k=k, f=f: e.dma_start_transpose(
                wqT[f][:, 128 * k:128 * (k + 1)], wqs[k][:, 128 * f:128 * (f + 1)]),
                reads=[wqs[k]], writes=[wqT[f]], inc=sem_name, n=16, is_dma=True)
    q.barrier()
    s_w.close()

    wrow = sb("wrow", [1, D], F32)
    q.dma("sp", wrow[:, :], wsc_d[:, :], reads=[wsc_d], writes=[wrow])
    for half in range(2):
        bi, pb = pp.get()
        q.do("pe", lambda e, pb=pb, h=half: e.matmul(
            pb[:, 0:512], ones1[0:1, 0:128], wrow[0:1, 512 * h:512 * (h + 1)],
            start=True, stop=True), reads=[ones1, wrow], writes=[pb])
        q.copy("dve", wsbc[:, 512 * half:512 * (half + 1)], pb[:, 0:512],
               reads=[pb], writes=[wsbc])
    brow = sb("brow", [1, D], F32)
    q.dma("sp", brow[:, :], W["ffb"][:, :], reads=[W["ffb"]], writes=[brow])
    for half in range(2):
        bi, pb = pp.get()
        q.do("pe", lambda e, pb=pb, h=half: e.matmul(
            pb[:, 0:512], ones1[0:1, 0:128], brow[0:1, 512 * h:512 * (h + 1)],
            start=True, stop=True), reads=[ones1, brow], writes=[pb])
        q.copy("dve", bfbc[:, 512 * half:512 * (half + 1)], pb[:, 0:512],
               reads=[pb], writes=[bfbc])

    s_t = ctx.enter_context(contextlib.ExitStack())
    sbt = mksb(s_t)
    z2t = [sbt(f"z2t{r}", [128, D], F32) for r in range(2)]
    xqt = [sbt(f"xqt{r}", [128, D], F32) for r in range(2)]
    xnt = [sbt(f"xnt{r}", [128, D], F32) for r in range(2)]
    x1t = [sbt(f"x1t{r}", [128, D], F32) for r in range(2)]
    xct = sbt("xct", [128, D], F32)
    sqd = sbt("sqd", [128, 64], F32)
    mu16 = sbt("mu16", [128, 16], F32)
    v16 = sbt("v16", [128, 16], F32)
    a16 = sbt("a16", [128, 16], F32)
    sc16 = sbt("sc16", [128, 16], F32)
    rs16 = sbt("rs16", [128, 16], F32)
    h16 = sbt("h16", [128, 16], F32)
    mup = sbt("mup", [128, 1], F32)
    vp = sbt("vp", [128, 1], F32)
    ap1 = sbt("ap1", [128, 1], F32)
    scp = sbt("scp", [128, 1], F32)
    rsp = sbt("rsp", [128, 1], F32)
    hp = sbt("hp", [128, 1], F32)
    absd = sbt("absd", [128, D], F32)
    xpnb = sbt("xpnb", [128, D])
    xpT = sbt("xpT", [128, D])
    xam = sbt("xam", [128, 1], F32)
    xqf = sbt("xqf", [128, D], F32)
    xint = sbt("xint", [128, D], mybir.dt.int32)
    xqb = sbt("xqb", [128, D])
    xqTt = sbt("xqTt", [128, D])
    stile = sbt("stile", [128, D], F32)
    gq = sbt("gq", [128, D], F32)

    def heron(rst, sc, h_, n=HEFF):
        for _ in range(n):
            q.do("dve", lambda e, rst=rst, h_=h_: e.reciprocal(h_[:, :], rst[:, :]),
                 reads=[rst], writes=[h_])
            q.do("dve", lambda e, h_=h_, sc=sc: e.tensor_mul(h_[:, :], h_[:, :], sc[:, :]),
                 reads=[h_, sc], writes=[h_])
            q.do("dve", lambda e, rst=rst, h_=h_: e.tensor_add(
                rst[:, :], rst[:, :], h_[:, :]), reads=[rst, h_], writes=[rst])
            q.do("dve", lambda e, rst=rst: e.tensor_scalar_mul(rst[:, :], rst[:, :], 0.5),
                 reads=[rst], writes=[rst])

    def tile_in(i):
        r = i % 2
        if fused:
            off = (i % 2) * OWNQ + 128 * (i // 2)
            q.dma("sp", z2t[r][:, 0:512], gxq[128 * i:128 * (i + 1), :],
                  reads=[gxq], writes=[z2t[r]])
            q.dma("sp", z2t[r][:, 512:1024], gxq[S + 128 * i:S + 128 * (i + 1), :],
                  reads=[gxq], writes=[z2t[r]])
            q.dma("sp", xnt[r][:, :], gxn[off:off + 128, :], reads=[gxn], writes=[xnt[r]])
            q.dma("sp", x1t[r][:, :], gx1[off:off + 128, :], reads=[gx1], writes=[x1t[r]])
            return
        for dst, srcn in ((z2t[r], "z2b"), (xqt[r], "xqp"), (xnt[r], "xn1p"),
                          (x1t[r], "x1p")):
            q.dma("sp", dst[:, :], W[srcn][128 * i:128 * (i + 1), :],
                  reads=[W[srcn]], writes=[dst])

    tile_in(0)
    for i in range(ntile):
        r = i % 2
        if i + 1 < ntile:
            tile_in(i + 1)
        zt, qt, nt, ot = z2t[r], xqt[r], xnt[r], x1t[r]
        if not fused:
            for g in range(16):
                q.do("dve", lambda e, g=g, zt=zt: e.reduce_sum(
                    mu16[:, g:g + 1], zt[:, 64 * g:64 * (g + 1)], axis=AX),
                    reads=[zt], writes=[mu16], acc=(g > 0))
            q.do("dve", lambda e: e.tensor_scalar_mul(mu16[:, :], mu16[:, :], 1.0 / 64.0),
                 reads=[mu16], writes=[mu16])
            for g in range(16):
                q.do("dve", lambda e, g=g, zt=zt: e.tensor_scalar(
                    xct[:, 64 * g:64 * (g + 1)], zt[:, 64 * g:64 * (g + 1)],
                    mu16[:, g:g + 1], None, ALU.subtract),
                    reads=[zt, mu16], writes=[xct], acc=(g > 0))
            for g in range(16):
                q.do("dve", lambda e, g=g: e.scalar_tensor_tensor(
                    sqd[:, :], xct[:, 64 * g:64 * (g + 1)], 0.0,
                    xct[:, 64 * g:64 * (g + 1)], ALU.add, ALU.mult,
                    accum_out=v16[:, g:g + 1]),
                    reads=[xct], writes=[sqd, v16])
            for g in range(16):
                q.do("act", lambda e, g=g: e.activation(
                    absd[:, 64 * g:64 * (g + 1)], xct[:, 64 * g:64 * (g + 1)], AF.Abs,
                    accum_out=a16[:, g:g + 1]),
                    reads=[xct], writes=[absd, a16], acc=(g > 0))
            q.do("dve", lambda e: e.tensor_scalar(
                v16[:, :], v16[:, :], 1.0 / 64.0, 1e-6, ALU.mult, ALU.add),
                reads=[v16], writes=[v16])
            q.do("dve", lambda e: e.reciprocal(sc16[:, :], v16[:, :]),
                 reads=[v16], writes=[sc16])
            q.do("dve", lambda e: e.tensor_scalar(
                a16[:, :], a16[:, :], 1.0 / 64.0, 8e-4, ALU.mult, ALU.add),
                reads=[a16], writes=[a16])
            q.do("dve", lambda e: e.reciprocal(rs16[:, :], a16[:, :]),
                 reads=[a16], writes=[rs16])
            q.do("dve", lambda e: e.tensor_scalar_mul(rs16[:, :], rs16[:, :], 1.2533141),
                 reads=[rs16], writes=[rs16])
            heron(rs16, sc16, h16)
            for g in range(16):
                q.do("dve", lambda e, g=g: e.tensor_scalar(
                    xct[:, 64 * g:64 * (g + 1)], xct[:, 64 * g:64 * (g + 1)],
                    rs16[:, g:g + 1], None, ALU.mult),
                    reads=[xct, rs16], writes=[xct], acc=(g > 0))
            q.do("dve", lambda e: e.tensor_mul(xct[:, :], xct[:, :], glnb[:, :]),
                 reads=[xct, glnb], writes=[xct])
            q.do("dve", lambda e: e.tensor_add(xct[:, :], xct[:, :], blnb[:, :]),
                 reads=[xct, blnb], writes=[xct])
            q.do("dve", lambda e, qt=qt: e.tensor_add(xct[:, :], xct[:, :], qt[:, :]),
                 reads=[xct, qt], writes=[xct])
        src0 = zt if fused else xct
        q.do("dve", lambda e, src0=src0: e.reduce_sum(mup[:, :], src0[:, :], axis=AX),
             reads=[src0], writes=[mup])
        q.do("dve", lambda e: e.tensor_scalar_mul(mup[:, :], mup[:, :], 1.0 / D),
             reads=[mup], writes=[mup])
        q.do("dve", lambda e, src0=src0: e.tensor_scalar(
            xct[:, :], src0[:, :], mup[:, :], None, ALU.subtract),
            reads=[src0, mup], writes=[xct])
        q.do("dve", lambda e: e.scalar_tensor_tensor(
            absd[:, :], xct[:, :], 0.0, xct[:, :], ALU.add, ALU.mult,
            accum_out=vp[:, :]), reads=[xct], writes=[absd, vp])
        q.do("act", lambda e: e.activation(absd[:, :], xct[:, :], AF.Abs,
                                           accum_out=ap1[:, :]),
             reads=[xct], writes=[absd, ap1])
        q.do("dve", lambda e: e.tensor_scalar(
            vp[:, :], vp[:, :], 1.0 / D, 1e-6, ALU.mult, ALU.add),
            reads=[vp], writes=[vp])
        q.do("dve", lambda e: e.reciprocal(scp[:, :], vp[:, :]), reads=[vp], writes=[scp])
        q.do("dve", lambda e: e.tensor_scalar(
            ap1[:, :], ap1[:, :], 1.0 / D, 8e-4, ALU.mult, ALU.add),
            reads=[ap1], writes=[ap1])
        q.do("dve", lambda e: e.reciprocal(rsp[:, :], ap1[:, :]), reads=[ap1], writes=[rsp])
        q.do("dve", lambda e: e.tensor_scalar_mul(rsp[:, :], rsp[:, :], 1.2533141),
             reads=[rsp], writes=[rsp])
        heron(rsp, scp, hp)
        q.do("dve", lambda e: e.tensor_scalar(
            xct[:, :], xct[:, :], rsp[:, :], None, ALU.mult),
            reads=[xct, rsp], writes=[xct])
        q.do("dve", lambda e: e.tensor_mul(xct[:, :], xct[:, :], gpnb[:, :]),
             reads=[xct, gpnb], writes=[xct])
        q.do("dve", lambda e: e.tensor_add(xct[:, :], xct[:, :], bpnb[:, :]),
             reads=[xct, bpnb], writes=[xct])
        q.copy("pool", xpnb[:, :], xct[:, :], reads=[xct], writes=[xpnb])
        for f in range(8):
            sem_name = "dtx"
            q.sem(sem_name)
            if q.cnt[sem_name] > 0:
                q.wt("sp", ("eager", sem_name, q.cnt[sem_name]))
            q.do("sp", lambda e, f=f: e.dma_start_transpose(
                xpT[:, 128 * f:128 * (f + 1)], xpnb[:, 128 * f:128 * (f + 1)]),
                reads=[xpnb], writes=[xpT], inc=sem_name, n=16, is_dma=True,
                acc=(f > 0))
        for half in range(2):
            bi, pb = pp.get()
            for k in range(8):
                q.do("pe", lambda e, pb=pb, k=k, h=half: e.matmul(
                    pb[:, 0:512], xpT[:, 128 * k:128 * (k + 1)],
                    wotb[k][:, 512 * h:512 * (h + 1)], start=(k == 0), stop=(k == 7)),
                    reads=[xpT, wotb[k]], writes=[pb], acc=(k > 0))
            q.copy("dve", stile[:, 512 * half:512 * (half + 1)], pb[:, 0:512],
                   reads=[pb], writes=[stile])
        q.do("dve", lambda e, nt=nt: e.reduce_max(xam[:, :], nt[:, :], axis=AX,
                                                  apply_absolute_value=True),
             reads=[nt], writes=[xam])
        q.do("dve", lambda e: e.tensor_scalar(xam[:, :], xam[:, :], 1e-8, None, ALU.max),
             reads=[xam], writes=[xam])
        q.do("dve", lambda e: e.tensor_scalar_mul(xam[:, :], xam[:, :], 1.0 / 127.0),
             reads=[xam], writes=[xam])
        q.do("dve", lambda e: e.reciprocal(hp[:, :], xam[:, :]), reads=[xam], writes=[hp])
        q.do("dve", lambda e, nt=nt: e.tensor_scalar(
            xqf[:, :], nt[:, :], hp[:, :], None, ALU.mult),
            reads=[nt, hp], writes=[xqf])
        q.do("dve", lambda e: e.tensor_scalar(
            xqf[:, :], xqf[:, :], 12582912.0, -12582912.0, ALU.add, ALU.add),
            reads=[xqf], writes=[xqf])
        q.do("dve", lambda e: e.tensor_copy(xint[:, :], xqf[:, :]),
             reads=[xqf], writes=[xint])
        q.copy("pool", xqb[:, :], xint[:, :], reads=[xint], writes=[xqb])
        for f in range(8):
            sem_name = "dqx"
            q.sem(sem_name)
            if q.cnt[sem_name] > 0:
                q.wt("sp", ("eager", sem_name, q.cnt[sem_name]))
            q.do("sp", lambda e, f=f: e.dma_start_transpose(
                xqTt[:, 128 * f:128 * (f + 1)], xqb[:, 128 * f:128 * (f + 1)]),
                reads=[xqb], writes=[xqTt], inc=sem_name, n=16, is_dma=True,
                acc=(f > 0))
        for half in range(2):
            bi, pb = pp.get()
            for k in range(8):
                q.do("pe", lambda e, pb=pb, k=k, h=half: e.matmul(
                    pb[:, 0:512], xqTt[:, 128 * k:128 * (k + 1)],
                    wqT[k][:, 512 * h:512 * (h + 1)], start=(k == 0), stop=(k == 7)),
                    reads=[xqTt, wqT[k]], writes=[pb], acc=(k > 0))
            q.copy("dve", gq[:, 512 * half:512 * (half + 1)], pb[:, 0:512],
                   reads=[pb], writes=[gq])
        q.do("dve", lambda e: e.tensor_mul(gq[:, :], gq[:, :], wsbc[:, :]),
             reads=[gq, wsbc], writes=[gq])
        q.do("dve", lambda e: e.tensor_scalar(
            gq[:, :], gq[:, :], xam[:, :], None, ALU.mult),
            reads=[gq, xam], writes=[gq])
        q.do("dve", lambda e: e.tensor_add(gq[:, :], gq[:, :], bfbc[:, :]),
             reads=[gq, bfbc], writes=[gq])
        q.do("act", lambda e: e.activation(gq[:, :], gq[:, :], AF.Silu),
             reads=[gq], writes=[gq])
        if "d_s" in dbg_o:
            q.dma("pool", dbg_o["d_s"][128 * i:128 * (i + 1), :], stile[:, :],
                  reads=[stile], writes=[dbg_o["d_s"]])
        if "d_g" in dbg_o:
            q.dma("pool", dbg_o["d_g"][128 * i:128 * (i + 1), :], gq[:, :],
                  reads=[gq], writes=[dbg_o["d_g"]])
        if "d_xpn" in dbg_o:
            q.dma("pool", dbg_o["d_xpn"][128 * i:128 * (i + 1), :], xct[:, :],
                  reads=[xct], writes=[dbg_o["d_xpn"]])
        q.do("dve", lambda e: e.tensor_mul(stile[:, :], stile[:, :], gq[:, :]),
             reads=[stile, gq], writes=[stile])
        q.do("dve", lambda e, ot=ot: e.tensor_add(stile[:, :], stile[:, :], ot[:, :]),
             reads=[stile, ot], writes=[stile])
        q.dma("sp", x2o[128 * i:128 * (i + 1), :], stile[:, :],
              reads=[stile], writes=[x2o])

    if env is not None:
        return x2o
    q.wt("pool", q.last_tok["pool"])
    q.wt("sp", q.last_tok["sp"])
    q.build()
    ctx.close()
    return nc


def build_full(dbg=(), nstep=NSTEP):
    nc = bass.Bass("TRN2", target_bir_lowering=False)
    ctx = contextlib.ExitStack()
    q = Q(nc, ctx)
    pp = Psum(q, nc, ctx)

    s0 = contextlib.ExitStack()
    x1own = build_l0(env=(nc, s0, q, pp))
    q.barrier()
    s0.close()

    s1 = contextlib.ExitStack()
    gbufs = build_l1s(dbg=dbg, nstep=nstep, env=(nc, s1, q, pp), x1own=x1own)
    q.barrier()
    s1.close()

    s2 = contextlib.ExitStack()
    build_l1p(env=(nc, s2, q, pp), gbufs=gbufs)
    q.barrier()
    s2.close()

    q.wt("pool", q.last_tok["pool"])
    q.wt("sp", q.last_tok["sp"])
    q.build()
    ctx.close()
    return nc


def l1p_host_inputs(inputs, core, l2res, x1):
    b, hh = core % 4, core // 4
    C = np.ascontiguousarray
    T0 = TOK3 * hh
    p64 = _perm64()
    toks = slice(T0, T0 + TOK3)
    z2b = np.zeros((TOK3, D), np.float32)
    xqp = np.zeros((TOK3, D), np.float32)
    for h2 in range(2):
        rr = l2res[b + 4 * h2]
        z2b[:, 512 * h2:512 * (h2 + 1)] = rr["z2bo"][toks]
        for m in range(4):
            slab = rr["xqro"][m].reshape(128, 2 * S)
            xqcols = slab.reshape(128, 256, 32)[:, :, 0:16].reshape(128, S)
            for hl2 in range(2):
                h = 8 * h2 + 2 * m + hl2
                xqp[:, 64 * h:64 * (h + 1)] = xqcols[64 * hl2:64 * hl2 + 64, toks].T
    xn1p = C(l2res[b]["xn1o"][toks])
    x1p = C(np.asarray(x1[b], np.float32)[toks])
    glw = np.asarray(inputs["L1_attn_ttt_norm_weight"], np.float32)[:, p64].reshape(D)
    glb = np.asarray(inputs["L1_attn_ttt_norm_bias"], np.float32)[:, p64].reshape(D)
    return {
        "z2b": z2b, "xqp": xqp, "xn1p": xn1p,
        "x1p": x1p,
        "glnb": np.tile(glw[None, :], (128, 1)).astype(np.float32),
        "blnb": np.tile(glb[None, :], (128, 1)).astype(np.float32),
        **l1p_host_weights(inputs),
    }


def l1p_host_weights(inputs):
    C = np.ascontiguousarray
    p64 = _perm64()
    permfull = (64 * (np.arange(D) // 64) + p64[np.arange(D) % 64])
    gpw = np.asarray(inputs["L1_attn_post_norm_weight"], np.float32)[permfull]
    gpb = np.asarray(inputs["L1_attn_post_norm_bias"], np.float32)[permfull]
    wo = np.asarray(inputs["L1_attn_wo"], np.float32)
    return {
        "gpnb": np.tile(gpw[None, :], (128, 1)).astype(np.float32),
        "bpnb": np.tile(gpb[None, :], (128, 1)).astype(np.float32),
        "wot": C(wo[:, permfull].T),
        "ffw": C(np.asarray(inputs["L1_ffn_weight"], np.float32)),
        "ffb": C(np.asarray(inputs["L1_ffn_bias"], np.float32))[None, :],
        "ones1": np.ones((1, 128), np.float32),
    }


def full_host_inputs(inputs, core):
    m = dict(l0_host_inputs(inputs, core))
    for k, v in l1s_host_inputs(inputs, core).items():
        m["s_" + k] = v
    for k, v in l1p_host_weights(inputs).items():
        m["p_" + k] = v
    return m



def _perm32(d):
    return np.concatenate([np.arange(0, d, 2), np.arange(1, d, 2)])


def l0_host_inputs(inputs, core):
    b, hh = core % 4, core // 4
    C = np.ascontiguousarray
    x = np.asarray(inputs["L0_attn_x"], np.float32)
    own = np.arange(hh, NB, 2)
    other = np.arange(1 - hh, NB, 2)
    blocks = np.concatenate([own, other])
    xb = C(x[b].reshape(NB, 128, D)[blocks].reshape(S, D))
    perm = _perm32(DHR)
    perm_swap = np.concatenate([perm[16:], perm[:16]])
    wqr = np.asarray(inputs["L0_attn_W_QR"], np.float32).reshape(DL, NH, DHR)
    wkr = np.asarray(inputs["L0_attn_W_KR"], np.float32)
    inv = 10000.0 ** (-np.arange(0, DHR, 2, dtype=np.float64) / DHR)
    pos = (blocks[:, None] * 128 + np.arange(128)[None, :]).reshape(S).astype(np.float64)
    ang = pos[None, :] * inv[:, None]
    cosk = np.concatenate([np.cos(ang), np.cos(ang)], 0).astype(np.float32)
    sink = np.concatenate([-np.sin(ang), np.sin(ang)], 0).astype(np.float32)
    co = np.concatenate([np.cos(ang[:, :OWNQ]), np.cos(ang[:, :OWNQ])], 0).astype(np.float32)
    si = np.concatenate([-np.sin(ang[:, :OWNQ]), np.sin(ang[:, :OWNQ])], 0).astype(np.float32)
    return {
        "xb": xb,
        "wdq": C(np.asarray(inputs["L0_attn_W_DQ"], np.float32)),
        "wuq": C(np.asarray(inputs["L0_attn_W_UQ"], np.float32)),
        "wqr_a": C(wqr[:, :, perm].reshape(DL, NH * DHR)),
        "wqr_b": C(wqr[:, :, perm_swap].reshape(DL, NH * DHR)),
        "wdkv": C(np.asarray(inputs["L0_attn_W_DKV"], np.float32)),
        "wuk": C(np.asarray(inputs["L0_attn_W_UK"], np.float32)),
        "wuv": C(np.asarray(inputs["L0_attn_W_UV"], np.float32)),
        "wkr_a": C(wkr[:, perm]), "wkr_b": C(wkr[:, perm_swap]),
        "wo": C(np.asarray(inputs["L0_attn_W_O"], np.float32)),
        "u1": C(np.asarray(inputs["L0_ffn_U1"], np.float32)),
        "v1": C(np.asarray(inputs["L0_ffn_V1"], np.float32)),
        "b1r": C(np.asarray(inputs["L0_ffn_b1"], np.float32)[None, :]),
        "u2": C(np.asarray(inputs["L0_ffn_U2"], np.float32)),
        "v2": C(np.asarray(inputs["L0_ffn_V2"], np.float32)),
        "b2r": C(np.asarray(inputs["L0_ffn_b2"], np.float32)[None, :]),
        "cosq": np.tile(co, (4, 1)), "sinq": np.tile(si, (4, 1)),
        "cosk": cosk, "sink": sink,
        "trimask": np.triu(np.ones((128, 128), np.float32)),
        "sel16": np.kron(np.eye(16, dtype=np.float32), np.ones((1, 128), np.float32)),
        "m2col": np.full((128, 1), float(hh), np.float32),
        "onesr": np.ones((1, 512), np.float32),
    }


def l0_gather(inputs, results):
    x1 = np.empty((B, S, D), np.float32)
    for c in range(8):
        b, hh = c % 4, c // 4
        own = np.arange(hh, NB, 2)
        r = results[c]["x1own"].reshape(NOB, 128, D)
        for j, blk in enumerate(own):
            x1[b, 128 * blk:128 * (blk + 1), :] = r[j]
    return x1




def _perm64():
    return np.concatenate([np.arange(0, 64, 2), np.arange(1, 64, 2)])


def l1s_host_inputs(inputs, core, x1=None):
    b, hh = core % 4, core // 4
    C = np.ascontiguousarray
    heads = np.arange(8 * hh, 8 * hh + 8)
    p64 = _perm64()
    p64s = np.concatenate([p64[32:], p64[:32]])
    wq = np.asarray(inputs["L1_attn_wq"], np.float32)
    wk = np.asarray(inputs["L1_attn_wk"], np.float32)
    wv = np.asarray(inputs["L1_attn_wv"], np.float32)

    def rows(w, perm):
        idx = (64 * heads[:, None] + perm[None, :]).reshape(-1)
        return C(w[idx, :].T)

    inv = 1.0 / 10000.0 ** (np.arange(0, 64, 2, dtype=np.float64) / 64)
    pos = np.arange(512, dtype=np.float64) % 16
    ang = pos[None, :] * inv[:, None]
    cos1 = np.cos(ang)
    sin1 = np.sin(ang)
    cosl = np.tile(np.concatenate([cos1, cos1], 0), (2, 1)).astype(np.float32)
    sinl = np.tile(np.concatenate([-sin1, sin1], 0), (2, 1)).astype(np.float32)
    w1 = np.asarray(inputs["L1_attn_W1"], np.float32)[heads][:, p64, :]
    b1 = np.asarray(inputs["L1_attn_b1"], np.float32)[heads, 0, :]
    w2 = np.asarray(inputs["L1_attn_W2"], np.float32)[heads][:, :, p64]
    b2 = np.asarray(inputs["L1_attn_b2"], np.float32)[heads][:, 0, p64]
    nw = np.asarray(inputs["L1_attn_ttt_norm_weight"], np.float32)[heads][:, p64]
    nb = np.asarray(inputs["L1_attn_ttt_norm_bias"], np.float32)[heads][:, p64]
    w1c0 = np.zeros((128, 1024), np.float32)
    w2t0 = np.zeros((128, 1024), np.float32)
    kh0 = np.zeros((128, 512), np.float32)
    kh1 = np.zeros((128, 512), np.float32)
    for p in range(8):
        r0, c0 = 64 * (p % 2), 256 * (p // 2)
        w1c0[r0:r0 + 64, c0:c0 + 256] = w1[p]
        w2t0[r0:r0 + 64, c0:c0 + 256] = w2[p].T
        kh0[:, 64 * p:64 * p + 64] = w2[p][0:128, :]
        kh1[:, 64 * p:64 * p + 64] = w2[p][128:256, :]
    selwA = np.zeros((8, 128), np.float32)
    selwB = np.zeros((8, 128), np.float32)
    selzTA = np.zeros((128, 8), np.float32)
    selzTB = np.zeros((128, 8), np.float32)
    for p in range(8):
        (selwA if p % 2 == 0 else selwB)[p, 32 * (p // 2):32 * (p // 2) + 16] = 1.0
        (selzTA if p % 2 == 0 else selzTB)[32 * (p // 2):32 * (p // 2) + 16, p] = 1.0
    triu16 = np.zeros((128, 16), np.float32)
    for s_ in range(4):
        for j in range(16):
            triu16[32 * s_ + j, j:] = 1.0
    gslA = np.zeros((128, 64), np.float32)
    gslB = np.zeros((128, 64), np.float32)
    for p in range(8):
        g = (gslA if p % 2 == 0 else gslB)
        g[32 * (p // 2):32 * (p // 2) + 32, :] = nw[p]
    gbc = np.tile(nw.reshape(1, 512), (128, 1)).astype(np.float32)
    bbc = np.tile(nb.reshape(1, 512), (128, 1)).astype(np.float32)
    return {
        **({"x1b": C(x1[b].astype(np.float32))} if x1 is not None else {}),
        "wq_a": rows(wq, p64), "wq_b": rows(wq, p64s),
        "wk_a": rows(wk, p64), "wk_b": rows(wk, p64s),
        "wv_o": rows(wv, p64),
        "cosl": cosl, "sinl": sinl,
        "lrwT": C(np.asarray(inputs["L1_attn_ttt_lr_weight"], np.float32)[heads, 0, :].T),
        "lrb": C(np.asarray(inputs["L1_attn_ttt_lr_bias"], np.float32)[heads]),
        "invr16": (1.0 / np.arange(1, 17, dtype=np.float64)).astype(np.float32)[None, :],
        "lti": C(np.asarray(inputs["L1_attn_learnable_token_idx"], np.float32)[None, :]),
        "w1c0": w1c0, "b1c0": C(b1), "kh0": kh0, "kh1": kh1, "w2t0": w2t0, "b2c0": C(b2),
        "selwA": selwA, "selwB": selwB, "selzTA": selzTA, "selzTB": selzTB,
        "triu16": triu16,
        "gslotA": gslA, "gslotB": gslB, "gbc": gbc, "bbc": bbc,
        "ident": np.eye(128, dtype=np.float32),
        "onesr": np.ones((1, 512), np.float32),
        "zeror": np.zeros((1, 128), np.float32),
    }


def kernel(**inputs):
    cores = list(range(8))
    nc = build_full()
    im = [full_host_inputs(inputs, c) for c in cores]
    r = run_bass_kernel_spmd(nc, im, core_ids=cores)
    out = np.empty((B, S, D), np.float32)
    for b in range(4):
        out[b] = r.results[b]["x2o"]
    return out


def kernel_unfused(**inputs):
    cores = list(range(8))
    nc0 = build_l0()
    in0 = [l0_host_inputs(inputs, c) for c in cores]
    r0 = run_bass_kernel_spmd(nc0, in0, core_ids=cores)
    x1 = l0_gather(inputs, r0.results)
    nc1 = build_l1s()
    in1 = [l1s_host_inputs(inputs, c, x1) for c in cores]
    r1 = run_bass_kernel_spmd(nc1, in1, core_ids=cores)
    nc2 = build_l1p()
    in2 = [l1p_host_inputs(inputs, c, r1.results, x1) for c in cores]
    r2 = run_bass_kernel_spmd(nc2, in2, core_ids=cores)
    out = np.empty((B, S, D), np.float32)
    for c in cores:
        b, hh = c % 4, c // 4
        out[b, TOK3 * hh:TOK3 * (hh + 1), :] = r2.results[c]["x2o"]
    return out


# version 05
# speedup vs baseline: 1.1987x; 1.1891x over previous
import contextlib
import zlib
import numpy as np
import concourse.bass as bass
import concourse.mybir as mybir
from concourse.bass_utils import run_bass_kernel_spmd

F32 = mybir.dt.float32
BF16 = mybir.dt.bfloat16
AF = mybir.ActivationFunctionType
ALU = mybir.AluOpType
AX = mybir.AxisListType.X

D, S, NB, NOB, OWNQ, B = 1024, 4096, 32, 16, 2048, 4
NH, DH, DHR, DL, DQK = 16, 64, 32, 256, 96
INVSQ = float(1.0 / np.sqrt(np.float64(DQK)))
EXPLO = float(np.exp(np.float64(-80.0) * INVSQ))
EXPHI = float(np.exp(np.float64(80.0) * INVSQ))

ENGS = ("pe", "dve", "act", "pool", "sp")
_ENG_ATTR = {"pe": "tensor", "dve": "vector", "act": "scalar", "pool": "gpsimd", "sp": "sync"}
_SEM_OF = {"pe": "spe", "dve": "sdv", "act": "sac", "pool": "spl", "sp": "ssp"}


class Q:

    def __init__(self, nc, ctx):
        self.nc, self.ctx = nc, ctx
        self.ops = {e: [] for e in ENGS}
        self.meta = {e: [] for e in ENGS}
        self.sems, self.cnt = {}, {}
        self.last_writer = {}
        self.readers = {}
        self.last_tok = {e: None for e in ENGS}

    def sem(self, name):
        if name not in self.sems:
            self.sems[name] = self.ctx.enter_context(self.nc.semaphore(name))
            self.cnt[name] = 0
        return self.sems[name]

    @staticmethod
    def _tid(t):
        h = getattr(t, "tensor", None)
        if h is not None:
            return getattr(h, "name", id(h))
        return getattr(t, "name", id(t))

    def do(self, eng, fn, reads=(), writes=(), inc=None, n=1, is_dma=False, acc=False):
        for t in reads:
            self.wt(eng, self.last_writer.get(self._tid(t)))
        if not acc:
            for t in writes:
                ti = self._tid(t)
                self.wt(eng, self.last_writer.get(ti))
                for rt in self.readers.get(ti, ()):
                    self.wt(eng, rt)
        if inc is not None:
            h = self.sem(inc)
            self.cnt[inc] += n
            tok = ("eager", inc, self.cnt[inc])
            self.ops[eng].append(lambda e, fn=fn, h=h, n=n: fn(e).then_inc(h, n))
            self.meta[eng].append(("dma", inc, self.cnt[inc]) if is_dma else ("op", None))
        else:
            sem_name = _SEM_OF[eng]
            h = self.sem(sem_name)
            cell = {"sem": sem_name, "on": False, "v": None}
            tok = ("lazy", cell, eng)

            def run(e, fn=fn, h=h, cell=cell):
                ins = fn(e)
                if cell["on"]:
                    ins.then_inc(h, 1)
                return ins
            self.ops[eng].append(run)
            self.meta[eng].append(("op", cell))
        for t in reads:
            self.readers.setdefault(self._tid(t), []).append(tok)
        for t in writes:
            ti = self._tid(t)
            self.last_writer[ti] = tok
            self.readers[ti] = []
        self.last_tok[eng] = tok
        return tok

    def dma(self, eng, out_ap, in_ap, inc=None, reads=(), writes=()):
        dst = writes[0] if writes else out_ap
        grp = zlib.crc32(str(self._tid(dst)).encode()) % 4
        sem_name = inc if inc is not None else ("d_%s%d" % (eng, grp))
        self.sem(sem_name)
        if self.cnt[sem_name] > 0:
            self.wt(eng, ("eager", sem_name, self.cnt[sem_name]))
        return self.do(eng, lambda e, o=out_ap, i=in_ap: e.dma_start(o, i),
                       reads=reads, writes=writes, inc=sem_name, n=16, is_dma=True)

    def wt(self, eng, tok):
        if tok is None:
            return
        if tok[0] == "lazy":
            tok[1]["on"] = True
            wcell = {"sem": tok[1]["sem"], "pc": tok[1], "v": None, "skip": False}
        else:
            wcell = {"sem": tok[1], "pc": None, "v": tok[2], "skip": False}
        h = self.sem(wcell["sem"])

        def run(e, h=h, c=wcell):
            if not c["skip"]:
                return e.wait_ge(h, c["v"])

        class _W:
            def then_inc(self, *a):
                return self
        self.ops[eng].append(lambda e, h=h, c=wcell: (run(e)))
        self.meta[eng].append(("wait", wcell))

    def wts(self, engs, toks):
        for e in engs:
            for t in toks:
                self.wt(e, t)

    def barrier(self):
        toks = {id(t): t for t in self.last_tok.values() if t is not None}
        for t in self.last_writer.values():
            toks[id(t)] = t
        for lst in self.readers.values():
            for t in lst:
                toks[id(t)] = t
        for e in ENGS:
            for t in toks.values():
                self.wt(e, t)

    def copy(self, eng, dst, src, reads=(), writes=()):
        r = list(reads) or [src]
        w = list(writes) or [dst]
        if eng == "act":
            return self.do(eng, lambda e, d=dst, s=src: e.copy(d, s), reads=r, writes=w)
        return self.do(eng, lambda e, d=dst, s=src: e.tensor_copy(d, s), reads=r, writes=w)

    def _finalize(self):
        for eng in ENGS:
            for m in self.meta[eng]:
                if m[0] == "op" and m[1] is not None and m[1]["on"]:
                    c = m[1]
                    self.cnt[c["sem"]] += 1
                    c["v"] = self.cnt[c["sem"]]
        for eng in ENGS:
            for m in self.meta[eng]:
                if m[0] == "wait" and m[1]["pc"] is not None:
                    m[1]["v"] = m[1]["pc"]["v"]
        for eng in ENGS:
            hi = {}
            for m in self.meta[eng]:
                if m[0] == "wait":
                    c = m[1]
                    if c["v"] is None or hi.get(c["sem"], -1) >= c["v"]:
                        c["skip"] = True
                    else:
                        hi[c["sem"]] = c["v"]
        burst_end = {}
        for eng in ENGS:
            run = {}
            for m in self.meta[eng] + [("op", None)]:
                if m[0] == "dma":
                    _, s_, v_ = m
                    run.setdefault(s_, []).append(v_)
                else:
                    for s_, vals in run.items():
                        for v_ in vals:
                            burst_end.setdefault(s_, {})[v_] = vals[-1]
                    run = {}
        for eng in ENGS:
            for m in self.meta[eng]:
                if m[0] == "wait" and not m[1]["skip"]:
                    c = m[1]
                    be = burst_end.get(c["sem"], {}).get(c["v"])
                    if be is not None and be > c["v"]:
                        c["v"] = be

    def build(self):
        self._finalize()
        with self.nc.allow_low_precision(reason="intentional bf16 staging"), \
             self.nc.Block() as block:
            for e in ENGS:
                if not self.ops[e]:
                    continue

                def mk(ops):
                    def body(eng):
                        for f in ops:
                            f(eng)
                    return body
                getattr(block, _ENG_ATTR[e])(mk(self.ops[e]))


class Psum:
    def __init__(self, q, nc, ctx):
        self.q = q
        self.banks = [ctx.enter_context(nc.psum_tensor(f"ps{i}", [128, 512], F32))
                      for i in range(8)]
        self.lastc = [None] * 8
        self.rr = 0
        self.held = set()

    def get(self, idx=None):
        if idx is None:
            for _ in range(16):
                i = self.rr % 8
                self.rr += 1
                if i not in self.held:
                    idx = i
                    break
        self.q.wt("pe", self.lastc[idx])
        return idx, self.banks[idx]

    def hold(self, idx):
        self.held.add(idx)

    def release(self, idx, consumer_tok):
        self.lastc[idx] = consumer_tok
        self.held.discard(idx)


def cdiv(a, b):
    return (a + b - 1) // b


L0_IN = [
    ("xb", [S, D]), ("wdq", [D, DL]), ("wuq", [DL, NH * DH]),
    ("wqr_a", [DL, NH * DHR]), ("wqr_b", [DL, NH * DHR]), ("wdkv", [D, DL]),
    ("wuk", [DL, NH * DH]), ("wuv", [DL, NH * DH]), ("wkr_a", [D, DHR]),
    ("wkr_b", [D, DHR]), ("wo", [NH * DH, D]), ("u1", [D, 256]), ("v1", [256, 4096]),
    ("b1r", [1, 4096]), ("u2", [4096, 256]), ("v2", [256, D]), ("b2r", [1, D]),
    ("cosq", [128, OWNQ]), ("sinq", [128, OWNQ]), ("cosk", [DHR, S]), ("sink", [DHR, S]),
    ("trimask", [128, 128]), ("m2col", [128, 1]), ("onesr", [1, 512]),
    ("sel16", [16, 16 * 128]),
]


def build_l0(dbg=(), upto=99, env=None):
    if env is None:
        nc = bass.Bass("TRN2", target_bir_lowering=False)
        ctx = contextlib.ExitStack()
        q = Q(nc, ctx)
        pp = Psum(q, nc, ctx)
    else:
        nc, ctx, q, pp = env
    ps = pp.banks

    W = {n: nc.dram_tensor(n, list(s), F32, kind="ExternalInput") for n, s in L0_IN}
    xb = W["xb"]
    x1own = nc.dram_tensor("x1own", [OWNQ, D], F32,
                           kind="ExternalOutput" if env is None else "Internal")
    zscratch = nc.dram_tensor("zscratch", [16, 512], F32)
    dbg_o = {n: nc.dram_tensor(n, list(s), F32, kind="ExternalOutput") for n, s in dbg}

    def mksb(c):
        return lambda name, shape, dt=BF16: c.enter_context(
            nc.sbuf_tensor(name, list(shape), dt))

    sb = mksb(ctx)
    cqT = [sb(f"cqT{k}", [128, OWNQ]) for k in range(2)]
    ckvT = [sb(f"ckvT{k}", [128, S]) for k in range(2)]
    krT = sb("krT", [DHR, S])
    pT = [sb(f"pT{k}", [128, OWNQ]) for k in range(2)]
    trim = sb("trim", [128, 128])
    m2c = sb("m2c", [128, 1], F32)
    onesc = sb("onesc", [1, 512])
    sel16 = sb("sel16t", [16, 16 * 128])
    WT = {}

    def alloc_w(c, names):
        s_ = mksb(c)
        for n in names:
            R, C = W[n].shape
            WT[n] = s_("w_" + n, [1 if R == 1 else 128, cdiv(R, 128) * C])

    def cast_weights(names):
        for n in names:
            R, C = W[n].shape
            for k in range(cdiv(R, 128)):
                r = min(128, R - 128 * k)
                q.dma("pool", WT[n][:r, k * C:(k + 1) * C],
                      W[n][128 * k:128 * k + r, :],
                      reads=[W[n]], writes=[WT[n]])

    alloc_w(ctx, ["wuq", "wqr_a", "wqr_b", "wuk", "wuv"])
    s1 = ctx.enter_context(contextlib.ExitStack())
    sb1 = mksb(s1)
    xnT = [sb1(f"xnT{k}", [128, S]) for k in range(8)]
    coskt, sinkt = sb1("coskt", [DHR, S], F32), sb1("sinkt", [DHR, S], F32)
    alloc_w(s1, ["wdq", "wdkv", "wkr_a", "wkr_b", "u1"])

    for dst, srch in ((trim, W["trimask"]), (onesc, W["onesr"]), (sel16, W["sel16"])):
        q.dma("pool", dst[:, :], srch[:, :], reads=[srch], writes=[dst])
    q.dma("sp", m2c[:, :], W["m2col"][:, :], reads=[W["m2col"]], writes=[m2c])
    q.dma("sp", coskt[:, :], W["cosk"][:, :], reads=[W["cosk"]], writes=[coskt])
    q.dma("sp", sinkt[:, :], W["sink"][:, :], reads=[W["sink"]], writes=[sinkt])
    cast_weights(["wuq", "wqr_a", "wqr_b", "wuk", "wuv", "wdq", "wdkv", "wkr_a",
                  "wkr_b", "u1"])

    s_ln = contextlib.ExitStack()
    sbl = mksb(s_ln)
    xrow = [sbl(f"xrow{i}", [128, D], F32) for i in range(2)]
    sqs = [sbl(f"sqs{i}", [128, D]) for i in range(2)]
    mus = sbl("mus", [128, NB], F32)
    s2s = sbl("s2s", [128, NB], F32)
    vars_ = sbl("vars", [128, NB], F32)
    rstds = sbl("rstds", [128, NB], F32)
    xnrow = [sbl(f"xnrow{i}", [128, D]) for i in range(2)]

    for t in range(NB):
        xt = xrow[t % 2]
        q.dma("sp", xt[:, :], xb[128 * t:128 * (t + 1), :], reads=[xb], writes=[xt])
        q.do("dve", lambda e, xt=xt, t=t: e.reduce_sum(mus[:, t:t + 1], xt[:, :], axis=AX),
             reads=[xt], writes=[mus], acc=(t > 0))
        q.do("act", lambda e, xt=xt, t=t: e.activation(
            sqs[t % 2][:, :], xt[:, :], AF.Square, accum_out=s2s[:, t:t + 1]),
            reads=[xt], writes=[sqs[t % 2], s2s], acc=(t > 1))
    q.do("dve", lambda e: e.tensor_scalar_mul(mus[:, :], mus[:, :], 1.0 / D),
         reads=[mus], writes=[mus])
    q.do("dve", lambda e: e.tensor_scalar_mul(s2s[:, :], s2s[:, :], 1.0 / D),
         reads=[s2s], writes=[s2s])
    q.do("dve", lambda e: e.scalar_tensor_tensor(vars_[:, :], mus[:, :], 0.0, mus[:, :],
                                                 ALU.add, ALU.mult),
         reads=[mus], writes=[vars_])
    q.do("dve", lambda e: e.tensor_sub(vars_[:, :], s2s[:, :], vars_[:, :]),
         reads=[s2s, vars_], writes=[vars_])
    q.do("dve", lambda e: e.tensor_scalar_add(vars_[:, :], vars_[:, :], 1e-5),
         reads=[vars_], writes=[vars_])
    q.do("act", lambda e: e.activation(rstds[:, :], vars_[:, :], AF.Sqrt),
         reads=[vars_], writes=[rstds])
    q.do("dve", lambda e: e.reciprocal(rstds[:, :], rstds[:, :]),
         reads=[rstds], writes=[rstds])

    for t in range(NB):
        xt, xo = xrow[t % 2], xnrow[t % 2]
        q.dma("sp", xt[:, :], xb[128 * t:128 * (t + 1), :], reads=[xb], writes=[xt])
        q.do("dve", lambda e, xt=xt, xo=xo, t=t: e.tensor_scalar(
            xo[:, :], xt[:, :], mus[:, t:t + 1], rstds[:, t:t + 1], ALU.subtract, ALU.mult),
            reads=[xt, mus, rstds], writes=[xo])
        for k in range(8):
            sem_name = "dt%d" % k
            q.sem(sem_name)
            if q.cnt[sem_name] > 0:
                q.wt("sp", ("eager", sem_name, q.cnt[sem_name]))
            q.do("sp", lambda e, xo=xo, t=t, k=k: e.dma_start_transpose(
                xnT[k][:, 128 * t:128 * (t + 1)], xo[:, 128 * k:128 * (k + 1)]),
                reads=[xo], writes=[xnT[k]], inc=sem_name, n=16, is_dma=True)

    evac_rr = [0]

    def evac(dst, srcb, eng=None, extra_reads=(), dwr=None):
        if eng is None:
            eng = ("dve", "act")[evac_rr[0] % 2]
            evac_rr[0] += 1
        return q.copy(eng, dst, srcb, reads=[srcb] + list(extra_reads),
                      writes=[dwr if dwr is not None else dst])

    def proj(dsts, wname, Kc, Mtot, N, src, CW=None, wcol0=0):
        CW = CW if CW is not None else WT[wname].shape[1] // Kc
        for mi in range(cdiv(Mtot, 128)):
            m = min(128, Mtot - 128 * mi)
            for nj in range(cdiv(N, 512)):
                nn = min(512, N - 512 * nj)
                bi, pb = pp.get()
                for k in range(Kc):
                    q.do("pe", lambda e, pb=pb, m=m, nn=nn, k=k, mi=mi, nj=nj, CW=CW,
                         wname=wname, wcol0=wcol0: e.matmul(
                             pb[:m, :nn],
                             WT[wname][:, k * CW + wcol0 + 128 * mi:
                                       k * CW + wcol0 + 128 * mi + m],
                             src[k][:, 512 * nj:512 * nj + nn],
                             start=(k == 0), stop=(k == Kc - 1)),
                         reads=[WT[wname], src[k]], writes=[pb], acc=(k > 0))
                evac(dsts[mi][:m, 512 * nj:512 * nj + nn], pb[:m, :nn], dwr=dsts[mi])
        return

    proj(cqT, "wdq", 8, DL, OWNQ, xnT)
    proj(ckvT, "wdkv", 8, DL, S, xnT)
    proj(pT, "u1", 8, 256, OWNQ, xnT)

    s_rope2 = contextlib.ExitStack()
    ropek = [mksb(s_rope2)(f"ropek{i}", [128, 512], F32) for i in range(2)]
    for nj in range(8):
        bia, pa = pp.get()
        bib, pb_ = pp.get()
        for k in range(8):
            q.do("pe", lambda e, k=k, nj=nj, pa=pa: e.matmul(
                pa[:DHR, :512], WT["wkr_a"][:, k * DHR:(k + 1) * DHR],
                xnT[k][:, 512 * nj:512 * (nj + 1)], start=(k == 0), stop=(k == 7)),
                reads=[WT["wkr_a"], xnT[k]], writes=[pa], acc=(k > 0))
        for k in range(8):
            q.do("pe", lambda e, k=k, nj=nj, pb_=pb_: e.matmul(
                pb_[:DHR, :512], WT["wkr_b"][:, k * DHR:(k + 1) * DHR],
                xnT[k][:, 512 * nj:512 * (nj + 1)], start=(k == 0), stop=(k == 7)),
                reads=[WT["wkr_b"], xnT[k]], writes=[pb_], acc=(k > 0))
        q.do("dve", lambda e, nj=nj, pa=pa, r0=ropek[0]: e.tensor_mul(
            r0[:DHR, :512], pa[:DHR, :512], coskt[:, 512 * nj:512 * (nj + 1)]),
            reads=[pa, coskt], writes=[ropek[0]])
        q.do("dve", lambda e, nj=nj, pb_=pb_, r1=ropek[1]: e.tensor_mul(
            r1[:DHR, :512], pb_[:DHR, :512], sinkt[:, 512 * nj:512 * (nj + 1)]),
            reads=[pb_, sinkt], writes=[ropek[1]])
        q.do("dve", lambda e, nj=nj, r0=ropek[0], r1=ropek[1]: e.tensor_add(
            krT[:DHR, 512 * nj:512 * (nj + 1)], r0[:DHR, :512], r1[:DHR, :512]),
            reads=[ropek[0], ropek[1]], writes=[krT])

    if upto <= 2:
        t_out = []
        dbgmap = {"d_xnT0": (xnT[0], 128, 512), "d_cqT0": (cqT[0], 128, 512),
                  "d_pT0": (pT[0], 128, 512), "d_krT": (krT, DHR, 512),
                  "d_ckvT0": (ckvT[0], 128, 512)}
        for nname, (tt, r, cc) in dbgmap.items():
            if nname in dbg_o:
                t_out.append(q.dma("pool", dbg_o[nname][:, :], tt[:r, 0:cc], reads=[tt], writes=[dbg_o[nname]]))
        q.wts(["pool"], t_out)
        q.build()
        s_rope2.close()
        s_ln.close()
        ctx.close()
        return nc

    q.barrier()
    s_rope2.close()
    s_ln.close()
    s1.close()

    attnT = [sb(f"attnT{k}", [128, OWNQ]) for k in range(8)]
    s2 = ctx.enter_context(contextlib.ExitStack())
    sb2 = mksb(s2)
    QT = [sb2(f"QT{h}", [DQK, OWNQ]) for h in range(4)]
    KT = [sb2(f"KT{h}", [DQK, S]) for h in range(4)]
    VT = [sb2(f"VT{i}", [128, 4 * 65]) for i in range(NB)]
    expt = [sb2(f"expt{i}", [128, 512]) for i in range(4)]
    ropet = [sb2(f"ropetw{i}", [128, 512], F32) for i in range(2)]
    cosqt, sinqt = sb2("cosqt", [128, OWNQ]), sb2("sinqt", [128, OWNQ])
    rowsums = sb2("rowsums", [16, 512], F32)
    zcol = sb2("zcol", [128, 2048], F32)
    rowinv = sb2("rowinv", [16, 512])
    invbs = sb2("invbs", [128, 512])
    q.dma("pool", cosqt[:, :], W["cosq"][:, :], reads=[W["cosq"]], writes=[cosqt])
    q.dma("pool", sinqt[:, :], W["sinq"][:, :], reads=[W["sinq"]], writes=[sinqt])

    NWAVE = 1 if upto <= 5 else 4
    for wv in range(NWAVE):
        for mi in range(2):
            for nj in range(4):
                bi, pb = pp.get()
                for k in range(2):
                    q.do("pe", lambda e, pb=pb, k=k, mi=mi, nj=nj, wv=wv: e.matmul(
                        pb[:, :512],
                        WT["wuq"][:, k * NH * DH + 4 * wv * DH + 128 * mi:
                                  k * NH * DH + 4 * wv * DH + 128 * mi + 128],
                        cqT[k][:, 512 * nj:512 * (nj + 1)], start=(k == 0), stop=(k == 1)),
                        reads=[WT["wuq"], cqT[k]], writes=[pb], acc=(k > 0))
                evac(QT[2 * mi][0:64, 512 * nj:512 * (nj + 1)], pb[0:64, :512],
                     dwr=QT[2 * mi])
                evac(QT[2 * mi + 1][0:64, 512 * nj:512 * (nj + 1)], pb[64:128, :512],
                     dwr=QT[2 * mi + 1])
        for nj in range(4):
            bia, pa = pp.get()
            bib, pb_ = pp.get()
            for k in range(2):
                q.do("pe", lambda e, pa=pa, k=k, nj=nj, wv=wv: e.matmul(
                    pa[:, :512],
                    WT["wqr_a"][:, k * NH * DHR + 4 * wv * DHR:
                                k * NH * DHR + 4 * wv * DHR + 128],
                    cqT[k][:, 512 * nj:512 * (nj + 1)], start=(k == 0), stop=(k == 1)),
                    reads=[WT["wqr_a"], cqT[k]], writes=[pa], acc=(k > 0))
            for k in range(2):
                q.do("pe", lambda e, pb_=pb_, k=k, nj=nj, wv=wv: e.matmul(
                    pb_[:, :512],
                    WT["wqr_b"][:, k * NH * DHR + 4 * wv * DHR:
                                k * NH * DHR + 4 * wv * DHR + 128],
                    cqT[k][:, 512 * nj:512 * (nj + 1)], start=(k == 0), stop=(k == 1)),
                    reads=[WT["wqr_b"], cqT[k]], writes=[pb_], acc=(k > 0))
            q.do("dve", lambda e, pa=pa, nj=nj, r0=ropet[0]: e.tensor_mul(
                r0[:, :512], pa[:, :512], cosqt[:, 512 * nj:512 * (nj + 1)]),
                reads=[pa, cosqt], writes=[ropet[0]])
            q.do("dve", lambda e, pb_=pb_, nj=nj, r1=ropet[1]: e.tensor_mul(
                r1[:, :512], pb_[:, :512], sinqt[:, 512 * nj:512 * (nj + 1)]),
                reads=[pb_, sinqt], writes=[ropet[1]])
            q.do("dve", lambda e, r0=ropet[0], r1=ropet[1]: e.tensor_add(
                r0[:, :512], r0[:, :512], r1[:, :512]),
                reads=[ropet[0], ropet[1]], writes=[ropet[0]])
            for hi in range(4):
                q.dma("pool", QT[hi][64:96, 512 * nj:512 * (nj + 1)],
                      ropet[0][32 * hi:32 * hi + 32, :512],
                      reads=[ropet[0]], writes=[QT[hi]])
        for mi in range(2):
            for nj in range(8):
                bi, pb = pp.get()
                for k in range(2):
                    q.do("pe", lambda e, pb=pb, k=k, mi=mi, nj=nj, wv=wv: e.matmul(
                        pb[:, :512],
                        WT["wuk"][:, k * NH * DH + 4 * wv * DH + 128 * mi:
                                  k * NH * DH + 4 * wv * DH + 128 * mi + 128],
                        ckvT[k][:, 512 * nj:512 * (nj + 1)], start=(k == 0), stop=(k == 1)),
                        reads=[WT["wuk"], ckvT[k]], writes=[pb], acc=(k > 0))
                evac(KT[2 * mi][0:64, 512 * nj:512 * (nj + 1)], pb[0:64, :512],
                     dwr=KT[2 * mi])
                evac(KT[2 * mi + 1][0:64, 512 * nj:512 * (nj + 1)], pb[64:128, :512],
                     dwr=KT[2 * mi + 1])
        for hi in range(4):
            q.dma("pool", KT[hi][64:96, :], krT[:DHR, :], reads=[krT], writes=[KT[hi]])
        for i in range(NB):
            q.do("pool", lambda e, i=i: e.memset(VT[i][:, :], 1.0),
                 reads=[], writes=[VT[i]])
        for i in range(NB):
            bi, pb = pp.get()
            for k in range(2):
                q.do("pe", lambda e, pb=pb, k=k, i=i, wv=wv: e.matmul(
                    pb[:, :256], ckvT[k][:, 128 * i:128 * (i + 1)],
                    WT["wuv"][:, k * NH * DH + 4 * wv * DH:k * NH * DH + 4 * wv * DH + 256],
                    start=(k == 0), stop=(k == 1)),
                    reads=[ckvT[k], WT["wuv"]], writes=[pb], acc=(k > 0))
            dst = VT[i][:, :].rearrange("p (h c) -> p h c", c=65)[:, :, 0:64]
            srcv = pb[:, :256].rearrange("p (h c) -> p h c", h=4)
            q.copy("dve", dst, srcv, reads=[pb], writes=[VT[i]])

        if upto <= 3:
            break
        for hi in range(4 if upto >= 5 else 1):
            h = 4 * wv + hi
            slab, row0 = attnT[h // 2], 64 * (h % 2)
            avb = [pp.banks[g] for g in range(4)]
            exp_cnt = [0]
            for c in range(NB):
                j0 = c if c < NOB else c - NOB
                qs = 128 * j0
                for g in range(j0 // 4, 4):
                    g0 = max(qs, 512 * g)
                    nn = 512 * (g + 1) - g0
                    off = g0 % 512
                    sp2 = pp.banks[4 + (exp_cnt[0] % 2)]
                    eb = expt[exp_cnt[0] % 4]
                    q.do("pe", lambda e, sp2=sp2, c=c, g0=g0, nn=nn, hi=hi: e.matmul(
                        sp2[:, :nn], KT[hi][:, 128 * c:128 * (c + 1)],
                        QT[hi][:, g0:g0 + nn], start=True, stop=True),
                        reads=[KT[hi], QT[hi]], writes=[sp2])
                    q.do("act", lambda e, eb=eb, sp2=sp2, nn=nn: e.activation(
                        eb[:, :nn], sp2[:, :nn], AF.Exp, scale=INVSQ),
                        reads=[sp2], writes=[eb])
                    q.do("dve", lambda e, eb=eb, nn=nn: e.tensor_scalar(
                        eb[:, :nn], eb[:, :nn], EXPLO, EXPHI, ALU.max, ALU.min),
                        reads=[eb], writes=[eb])
                    if c < NOB and g0 == qs:
                        q.do("dve", lambda e, eb=eb: e.tensor_mul(
                            eb[:, 0:128], eb[:, 0:128], trim[:, :]),
                            reads=[eb, trim], writes=[eb])
                    if c >= NOB and g0 == qs:
                        q.do("dve", lambda e, eb=eb: e.tensor_scalar(
                            eb[:, 0:128], eb[:, 0:128], m2c[:, :], None, ALU.mult),
                            reads=[eb, m2c], writes=[eb])
                    q.do("pe", lambda e, av=avb[g], hi=hi, c=c, eb=eb, off=off, nn=nn,
                         st=(c == 0), sp_=(c == NOB + 4 * g + 3): e.matmul(
                             av[0:65, off:off + nn], VT[c][:, 65 * hi:65 * hi + 65],
                             eb[:, :nn], start=st, stop=sp_),
                         reads=[VT[c], eb], writes=[avb[g]], acc=(c != 0))
                    exp_cnt[0] += 1
            for g in range(4):
                q.copy("dve", zcol[32 * hi:32 * hi + 1, 512 * g:512 * g + 512],
                       avb[g][64:65, :512], reads=[avb[g]], writes=[zcol])
                evac(slab[row0:row0 + 64, 512 * g:512 * (g + 1)], avb[g][0:64, :512],
                     eng="dve", dwr=slab)
        if upto <= 4:
            break
        for hi4 in range(4):
            q.dma("sp", zscratch[4 * hi4:4 * hi4 + 4, :].rearrange("a b -> (a b)").unsqueeze(0),
                  zcol[32 * hi4:32 * hi4 + 1, :], reads=[zcol], writes=[zscratch])
        q.dma("sp", rowsums[:, :], zscratch[:, :], reads=[zscratch], writes=[rowsums])
        q.do("dve", lambda e: e.reciprocal(rowinv[:, :], rowsums[:, :]),
             reads=[rowsums], writes=[rowinv])
        for hi in range(4):
            h = 4 * wv + hi
            slab, row0 = attnT[h // 2], 64 * (h % 2)
            for g in range(4):
                bi, pb = pp.get()
                q.do("pe", lambda e, pb=pb, hi=hi, g=g: e.matmul(
                    pb[0:128, :512], sel16[0:16, 128 * (4 * hi + g):128 * (4 * hi + g) + 128],
                    rowinv[0:16, :512], start=True, stop=True),
                    reads=[sel16, rowinv], writes=[pb])
                q.copy("dve", invbs[:, :], pb[0:128, :512], reads=[pb], writes=[invbs])
                q.do("dve", lambda e, slab=slab, row0=row0, g=g: e.tensor_mul(
                    slab[row0:row0 + 64, 512 * g:512 * (g + 1)],
                    slab[row0:row0 + 64, 512 * g:512 * (g + 1)],
                    invbs[row0:row0 + 64, :]),
                    reads=[slab, invbs], writes=[slab])

    if upto <= 6:
        t_out = []
        dbgmap = {"d_attnT0": (attnT[0], 128, 2048), "d_QT0": (QT[0], DQK, 512),
                  "d_KT0": (KT[0], DQK, 512), "d_VT0": (VT[0], 128, 260)}
        for nname, (tt, r, cc) in dbgmap.items():
            if nname in dbg_o:
                t_out.append(q.dma("pool", dbg_o[nname][:, :], tt[:r, 0:cc], reads=[tt], writes=[dbg_o[nname]]))
        q.wts(["pool"], t_out)
        q.build()
        ctx.close()
        return nc

    q.barrier()
    s2.close()
    s3 = ctx.enter_context(contextlib.ExitStack())
    sb3 = mksb(s3)
    alloc_w(s3, ["wo", "v1", "b1r", "u2", "v2", "b2r"])
    cast_weights(["wo", "v1", "b1r", "u2", "v2", "b2r"])
    hT = [sb3(f"hT{i}", [128, 512]) for i in range(2)]
    q2T = [sb3(f"q2T{k}", [128, OWNQ]) for k in range(2)]
    gtile = [sb3(f"gtile{i}", [128, D]) for i in range(2)]
    sgt = sb3("sgt", [128, D])
    x1r = [sb3(f"x1r{i}", [128, D], F32) for i in range(2)]

    for tg in range(4):
        pq0, pq1 = pp.banks[6], pp.banks[7]
        for f in range(32):
            bi, pb = pp.get(f % 3)
            ht = hT[f % 2]
            for k in range(2):
                q.do("pe", lambda e, pb=pb, k=k, f=f, tg=tg: e.matmul(
                    pb[:, :512], WT["v1"][:, k * 4096 + 128 * f:k * 4096 + 128 * f + 128],
                    pT[k][:, 512 * tg:512 * (tg + 1)], start=(k == 0), stop=False),
                    reads=[WT["v1"], pT[k]], writes=[pb], acc=(k > 0))
            q.do("pe", lambda e, pb=pb, f=f: e.matmul(
                pb[:, :512], WT["b1r"][0:1, 128 * f:128 * f + 128], onesc[0:1, 0:512],
                start=False, stop=True),
                reads=[WT["b1r"], onesc], writes=[pb], acc=True)
            q.do("act", lambda e, pb=pb, ht=ht: e.activation(
                ht[:, :512], pb[:, :512], AF.Gelu_apprx_tanh),
                reads=[pb], writes=[ht])
            for m in range(2):
                q.do("pe", lambda e, m=m, f=f, tg=tg, ht=ht, pq=(pq0, pq1)[m]: e.matmul(
                    pq[:, :512], WT["u2"][:, 256 * f + 128 * m:256 * f + 128 * m + 128],
                    ht[:, :512], start=(f == 0), stop=(f == 31)),
                    reads=[WT["u2"], ht], writes=[(pq0, pq1)[m]], acc=(f > 0))
        evac(q2T[0][:, 512 * tg:512 * (tg + 1)], pq0[:, :512], eng="dve", dwr=q2T[0])
        evac(q2T[1][:, 512 * tg:512 * (tg + 1)], pq1[:, :512], eng="act", dwr=q2T[1])

    for i in range(NOB):
        xt = x1r[i % 2]
        q.dma("sp", xt[:, :], xb[128 * i:128 * (i + 1), :], reads=[xb], writes=[xt])
        gt = gtile[i % 2]
        for half in range(2):
            bi, pb = pp.get()
            for k in range(2):
                q.do("pe", lambda e, pb=pb, k=k, i=i, half=half: e.matmul(
                    pb[:, :512], q2T[k][:, 128 * i:128 * (i + 1)],
                    WT["v2"][:, k * D + 512 * half:k * D + 512 * half + 512],
                    start=(k == 0), stop=False),
                    reads=[q2T[k], WT["v2"]], writes=[pb], acc=(k > 0))
            q.do("pe", lambda e, pb=pb, half=half: e.matmul(
                pb[:, :512], onesc[0:1, 0:128], WT["b2r"][0:1, 512 * half:512 * half + 512],
                start=False, stop=True),
                reads=[onesc, WT["b2r"]], writes=[pb], acc=True)
            q.do("act", lambda e, pb=pb, gt=gt, half=half: e.activation(
                gt[:, 512 * half:512 * half + 512], pb[:, :512], AF.Silu),
                reads=[pb], writes=[gt] if half == 1 else [gt])
            bi2, pb2 = pp.get()
            for k in range(8):
                q.do("pe", lambda e, pb2=pb2, k=k, i=i, half=half: e.matmul(
                    pb2[:, :512], attnT[k][:, 128 * i:128 * (i + 1)],
                    WT["wo"][:, k * D + 512 * half:k * D + 512 * half + 512],
                    start=(k == 0), stop=(k == 7)),
                    reads=[attnT[k], WT["wo"]], writes=[pb2], acc=(k > 0))
            q.do("dve", lambda e, pb2=pb2, gt=gt, half=half: e.tensor_mul(
                sgt[:, 512 * half:512 * half + 512], pb2[:, :512],
                gt[:, 512 * half:512 * half + 512]),
                reads=[pb2, gt], writes=[sgt])
            q.do("dve", lambda e, xt=xt, half=half: e.tensor_add(
                xt[:, 512 * half:512 * half + 512], xt[:, 512 * half:512 * half + 512],
                sgt[:, 512 * half:512 * half + 512]),
                reads=[xt, sgt], writes=[xt])
        tout = q.dma("sp", x1own[128 * i:128 * (i + 1), :], xt[:, :],
                     reads=[xt], writes=[x1own])

    if env is not None:
        return x1own
    q.wt("sp", q.last_tok["sp"])
    q.build()
    ctx.close()
    return nc




L1S_IN = [
    ("x1b", [S, D]),
    ("wq_a", [D, 512]), ("wq_b", [D, 512]), ("wk_a", [D, 512]), ("wk_b", [D, 512]),
    ("wv_o", [D, 512]), ("cosl", [128, 512]), ("sinl", [128, 512]),
    ("lrwT", [D, 8]), ("lrb", [8, 1]), ("invr16", [1, 16]), ("lti", [1, 16]),
    ("w1c0", [128, 1024]), ("b1c0", [8, 256]),
    ("kh0", [128, 512]), ("kh1", [128, 512]), ("w2t0", [128, 1024]), ("b2c0", [8, 64]),
    ("selwA", [8, 128]), ("selwB", [8, 128]), ("selzTA", [128, 8]), ("selzTB", [128, 8]),
    ("triu16", [128, 16]), ("gslotA", [128, 64]), ("gslotB", [128, 64]),
    ("gbc", [128, 512]), ("bbc", [128, 512]), ("ident", [128, 128]),
    ("onesr", [1, 512]), ("zeror", [1, 128]),
]

NSTEP = 256
HEFF = 3
SKEW = 8
DISABLE_LOADS = False
DISABLE_ZOUT = False


def build_l1s(dbg=(), nstep=NSTEP, env=None, x1own=None):
    fused = env is not None
    pfx = "s_" if fused else ""
    if env is None:
        nc = bass.Bass("TRN2", target_bir_lowering=False)
        ctx = contextlib.ExitStack()
        q = Q(nc, ctx)
        pp = Psum(q, nc, ctx)
    else:
        nc, ctx, q, pp = env
    ps = pp.banks

    W = {n: nc.dram_tensor(pfx + n, list(s), F32, kind="ExternalInput")
         for n, s in L1S_IN if not (fused and n == "x1b")}
    if not fused:
        x1b = W["x1b"]
        xn1o = nc.dram_tensor("xn1o", [S, D], F32, kind="ExternalOutput")
        xqro = nc.dram_tensor("xqro", [4, 128, 2 * S], F32, kind="ExternalOutput")
    else:
        xn_own = nc.dram_tensor("xn_own", [OWNQ, D], F32)
        gx1 = nc.dram_tensor("gx1", [2 * OWNQ, D], F32)
        gxn = nc.dram_tensor("gxn", [2 * OWNQ, D], F32)
        xqwo = nc.dram_tensor("xqwo", [S, 512], F32)
        gxq = nc.dram_tensor("gxq", [2 * S, 512], F32)
    z2bo = nc.dram_tensor(pfx + "z2bo", [S, 512], F32,
                          kind="Internal" if fused else "ExternalOutput")
    gxp_pad = nc.dram_tensor(pfx + "gxp_pad", [128, 2 * (S // 16) * 128], BF16)
    lr_d = nc.dram_tensor(pfx + "lr_d", [8, S], F32)
    z2_pad = nc.dram_tensor(pfx + "z2_pad", [2 * (S // 16) * 128, 64], F32)
    dbg_o = {n: nc.dram_tensor(n, list(s), F32, kind="ExternalOutput") for n, s in dbg}

    def mksb(c):
        return lambda name, shape, dt=BF16: c.enter_context(
            nc.sbuf_tensor(pfx + name, list(shape), dt))

    sb = mksb(ctx)
    xqkT = [sb(f"xqkT{m}", [128, 2 * S]) for m in range(4)]
    lrS = sb("lrS", [8, S])
    toki = sb("toki", [1, 16], F32)
    t15c = sb("t15c", [128, 1], F32)
    tokbc = sb("tokbc", [128, 16], F32)
    p2A = sb("p2A", [128, 16], F32)
    p2B = sb("p2B", [128, 16], F32)
    idt = sb("idt", [128, 128])
    onesc = sb("onesc", [1, 512])
    ones32 = sb("ones32", [1, 128], F32)
    zeroc = sb("zeroc", [1, 128])
    CON = {}
    for n in ("selwA", "selwB", "selzTA", "selzTB", "triu16", "gslotA", "gslotB",
              "gbc", "bbc"):
        CON[n] = sb("c_" + n, list(W[n].shape),
                    BF16 if n.startswith("sel") else F32)
    WT = {}

    def alloc_w(c, names):
        s_ = mksb(c)
        for n in names:
            R, C = W[n].shape
            WT[n] = s_("w_" + n, [1 if R == 1 else 128, cdiv(R, 128) * C])

    def cast_weights(names):
        for n in names:
            R, C = W[n].shape
            for k in range(cdiv(R, 128)):
                r = min(128, R - 128 * k)
                q.dma("pool", WT[n][:r, k * C:(k + 1) * C],
                      W[n][128 * k:128 * k + r, :], reads=[W[n]], writes=[WT[n]])

    with contextlib.ExitStack() as s_z0:
        zpad0 = mksb(s_z0)("zpad0", [128, 1024])
        q.do("pool", lambda e: e.memset(zpad0[:, :], 0.0), reads=[], writes=[zpad0])
        for c in range(64):
            q.dma("sp", gxp_pad[:, 1024 * c:1024 * (c + 1)],
                  zpad0[:, :], reads=[zpad0], writes=[gxp_pad], inc="bp%d" % (c % 4))
        q.barrier()

    for n, dst in (("ident", idt), ("onesr", onesc), ("zeror", zeroc)):
        q.dma("pool", dst[:, :], W[n][:, :], reads=[W[n]], writes=[dst])
    q.dma("sp", ones32[:, :], W["onesr"][0:1, 0:128], reads=[W["onesr"]], writes=[ones32])
    for n in CON:
        q.dma("pool", CON[n][:, :], W[n][:, :], reads=[W[n]], writes=[CON[n]])

    ivr = sb("ivr", [1, 16], F32)
    lti_t = sb("lti_t", [1, 16], F32)
    q.dma("sp", ivr[:, :], W["invr16"][:, :], reads=[W["invr16"]], writes=[ivr])
    q.dma("sp", lti_t[:, :], W["lti"][:, :], reads=[W["lti"]], writes=[lti_t])
    q.do("dve", lambda e: e.tensor_add(toki[:, :], ivr[:, :], lti_t[:, :]),
         reads=[ivr, lti_t], writes=[toki])
    q.do("dve", lambda e: e.tensor_relu(toki[:, :], toki[:, :]), reads=[toki], writes=[toki])
    bi0, pb0 = pp.get()
    q.do("pe", lambda e, pb=pb0: e.matmul(pb[0:128, 0:16], ones32[0:1, 0:128],
                                          toki[0:1, 0:16], start=True, stop=True),
         reads=[ones32, toki], writes=[pb0])
    q.copy("dve", tokbc[:, :], pb0[0:128, 0:16], reads=[pb0], writes=[tokbc])
    q.do("dve", lambda e, pb=pb0: e.tensor_scalar_mul(
        t15c[:, :], pb[0:128, 15:16], 1.0 / 64.0), reads=[pb0], writes=[t15c])
    for p2 in (p2A, p2B):
        q.do("dve", lambda e, p2=p2: e.tensor_mul(p2[:, :], CON["triu16"][:, :], tokbc[:, :]),
             reads=[CON["triu16"], tokbc], writes=[p2])
        q.do("dve", lambda e, p2=p2: e.tensor_scalar_mul(p2[:, :], p2[:, :], -1.0 / 64.0),
             reads=[p2], writes=[p2])

    s1 = ctx.enter_context(contextlib.ExitStack())
    sb1 = mksb(s1)
    xnT = [sb1(f"xnT{k}", [128, S]) for k in range(8)]
    coslt, sinlt = sb1("coslt", [128, 512]), sb1("sinlt", [128, 512])
    lrbt = sb1("lrbt", [8, 1], F32)
    alloc_w(s1, ["wq_a", "wq_b", "wk_a", "wk_b", "wv_o", "lrwT"])
    cast_weights(["wq_a", "wq_b", "wk_a", "wk_b", "wv_o", "lrwT"])
    q.dma("pool", coslt[:, :], W["cosl"][:, :], reads=[W["cosl"]], writes=[coslt])
    q.dma("pool", sinlt[:, :], W["sinl"][:, :], reads=[W["sinl"]], writes=[sinlt])

    s_ln = contextlib.ExitStack()
    sbl = mksb(s_ln)
    xrow = [sbl(f"xrow{i}", [128, D], F32) for i in range(2)]
    mus = sbl("mus", [128, NB], F32)
    s2s = sbl("s2s", [128, NB], F32)
    vars_ = sbl("vars", [128, NB], F32)
    rstds = sbl("rstds", [128, NB], F32)
    xnrow = [sbl(f"xnrow{i}", [128, D]) for i in range(2)]
    xnrow32 = [sbl("xnrow32", [128, D], F32)] * 2

    nln = 16 if fused else NB
    xsrc = x1own if fused else x1b
    for t in range(nln):
        xt = xrow[t % 2]
        q.dma("sp", xt[:, :], xsrc[128 * t:128 * (t + 1), :], reads=[xsrc], writes=[xt])
        q.do("dve", lambda e, xt=xt, t=t: e.reduce_sum(mus[:, t:t + 1], xt[:, :], axis=AX),
             reads=[xt], writes=[mus], acc=(t > 0))
        q.do("act", lambda e, xt=xt, t=t: e.activation(
            xnrow[t % 2][:, :], xt[:, :], AF.Square, accum_out=s2s[:, t:t + 1]),
            reads=[xt], writes=[xnrow[t % 2], s2s], acc=(t > 1))
    q.do("dve", lambda e: e.tensor_scalar_mul(mus[:, 0:nln], mus[:, 0:nln], 1.0 / D),
         reads=[mus], writes=[mus])
    q.do("dve", lambda e: e.tensor_scalar_mul(s2s[:, 0:nln], s2s[:, 0:nln], 1.0 / D),
         reads=[s2s], writes=[s2s])
    q.do("dve", lambda e: e.scalar_tensor_tensor(vars_[:, 0:nln], mus[:, 0:nln], 0.0,
                                                 mus[:, 0:nln], ALU.add, ALU.mult),
         reads=[mus], writes=[vars_])
    q.do("dve", lambda e: e.tensor_sub(vars_[:, 0:nln], s2s[:, 0:nln], vars_[:, 0:nln]),
         reads=[s2s, vars_], writes=[vars_])
    q.do("dve", lambda e: e.tensor_scalar_add(vars_[:, 0:nln], vars_[:, 0:nln], 1e-5),
         reads=[vars_], writes=[vars_])
    q.do("act", lambda e: e.activation(rstds[:, 0:nln], vars_[:, 0:nln], AF.Sqrt),
         reads=[vars_], writes=[rstds])
    q.do("dve", lambda e: e.reciprocal(rstds[:, 0:nln], rstds[:, 0:nln]),
         reads=[rstds], writes=[rstds])
    for t in range(nln):
        xt, xo, xo32 = xrow[t % 2], xnrow[t % 2], xnrow32[t % 2]
        q.dma("sp", xt[:, :], xsrc[128 * t:128 * (t + 1), :], reads=[xsrc], writes=[xt])
        q.do("dve", lambda e, xt=xt, xo32=xo32, t=t: e.tensor_scalar(
            xo32[:, :], xt[:, :], mus[:, t:t + 1], rstds[:, t:t + 1], ALU.subtract, ALU.mult),
            reads=[xt, mus, rstds], writes=[xo32])
        if fused:
            q.dma("sp", xn_own[128 * t:128 * (t + 1), :], xo32[:, :],
                  reads=[xo32], writes=[xn_own])
            continue
        q.do("dve", lambda e, xt=xt, xo=xo, t=t: e.tensor_scalar(
            xo[:, :], xt[:, :], mus[:, t:t + 1], rstds[:, t:t + 1], ALU.subtract, ALU.mult),
            reads=[xt, mus, rstds], writes=[xo])
        q.dma("sp", xn1o[128 * t:128 * (t + 1), :], xo32[:, :], reads=[xo32], writes=[xn1o])
        for k in range(8):
            sem_name = "dt%d" % k
            q.sem(sem_name)
            if q.cnt[sem_name] > 0:
                q.wt("sp", ("eager", sem_name, q.cnt[sem_name]))
            q.do("sp", lambda e, xo=xo, t=t, k=k: e.dma_start_transpose(
                xnT[k][:, 128 * t:128 * (t + 1)], xo[:, 128 * k:128 * (k + 1)]),
                reads=[xo], writes=[xnT[k]], inc=sem_name, n=16, is_dma=True)
    if fused:
        if nstep < NSTEP:
            s_zz = contextlib.ExitStack()
            zz0 = mksb(s_zz)("zz0", [128, 512], F32)
            q.do("pool", lambda e: e.memset(zz0[:, :], 0.0), reads=[], writes=[zz0])
            for c in range(64):
                q.dma("pool", bass.AP(z2_pad, c * 65536, [[512, 128], [64, 8], [1, 64]]),
                      zz0[:, :].rearrange("p (a d) -> p a d", d=64),
                      reads=[zz0], writes=[z2_pad])
            q.barrier()
            s_zz.close()
        q.do("pool", lambda e: e.collective_compute(
            "AllGather", ALU.bypass, [[0, 4], [1, 5], [2, 6], [3, 7]],
            [x1own[:, :]], [gx1[:, :]]), reads=[x1own], writes=[gx1])
        q.do("pool", lambda e: e.collective_compute(
            "AllGather", ALU.bypass, [[0, 4], [1, 5], [2, 6], [3, 7]],
            [xn_own[:, :]], [gxn[:, :]]), reads=[xn_own], writes=[gxn])
        for t in range(NB):
            xo = xnrow[t % 2]
            src_off = (t % 2) * OWNQ + 128 * (t // 2)
            q.dma("pool", xo[:, :], gxn[src_off:src_off + 128, :],
                  reads=[gxn], writes=[xo])
            for k in range(8):
                sem_name = "dt%d" % k
                q.sem(sem_name)
                if q.cnt[sem_name] > 0:
                    q.wt("sp", ("eager", sem_name, q.cnt[sem_name]))
                q.do("sp", lambda e, xo=xo, t=t, k=k: e.dma_start_transpose(
                    xnT[k][:, 128 * t:128 * (t + 1)], xo[:, 128 * k:128 * (k + 1)]),
                    reads=[xo], writes=[xnT[k]], inc=sem_name, n=16, is_dma=True)

    s_pro = contextlib.ExitStack()
    ropep = [mksb(s_pro)(f"ropep{i}", [128, 512]) for i in range(2)]
    for m in range(4):
        for (wn_a, wn_b, base) in (("wq_a", "wq_b", 0), ("wk_a", "wk_b", 16)):
            for nj in range(8):
                bia, pa = pp.get()
                bib, pb_ = pp.get()
                for k in range(8):
                    q.do("pe", lambda e, pa=pa, k=k, nj=nj, m=m, wn=wn_a: e.matmul(
                        pa[:, :512], WT[wn][:, k * 512 + 128 * m:k * 512 + 128 * m + 128],
                        xnT[k][:, 512 * nj:512 * (nj + 1)], start=(k == 0), stop=(k == 7)),
                        reads=[WT[wn_a], xnT[k]], writes=[pa], acc=(k > 0))
                for k in range(8):
                    q.do("pe", lambda e, pb_=pb_, k=k, nj=nj, m=m, wn=wn_b: e.matmul(
                        pb_[:, :512], WT[wn][:, k * 512 + 128 * m:k * 512 + 128 * m + 128],
                        xnT[k][:, 512 * nj:512 * (nj + 1)], start=(k == 0), stop=(k == 7)),
                        reads=[WT[wn_b], xnT[k]], writes=[pb_], acc=(k > 0))
                q.do("dve", lambda e, pa=pa, r0=ropep[0]: e.tensor_mul(
                    r0[:, :512], pa[:, :512], coslt[:, 0:512]),
                    reads=[pa, coslt], writes=[ropep[0]])
                q.do("dve", lambda e, pb_=pb_, r1=ropep[1]: e.tensor_mul(
                    r1[:, :512], pb_[:, :512], sinlt[:, 0:512]),
                    reads=[pb_, sinlt], writes=[ropep[1]])
                dst = bass.AP(xqkT[m], 32 * 32 * nj + base, [[2 * S, 128], [32, 32], [1, 16]])
                q.do("dve", lambda e, dst=dst, r0=ropep[0], r1=ropep[1]: e.tensor_add(
                    dst, r0[:, :512].rearrange("p (c i) -> p c i", i=16),
                    r1[:, :512].rearrange("p (c i) -> p c i", i=16)),
                    reads=[ropep[0], ropep[1]], writes=[xqkT[m]])
    if not fused:
        for m in range(4):
            q.dma("pool", xqro[m, :, :], xqkT[m][:, :], reads=[xqkT[m]], writes=[xqro])

    q.dma("sp", lrbt[:, :], W["lrb"][:, :], reads=[W["lrb"]], writes=[lrbt])
    for nj in range(8):
        bi, pb = pp.get()
        for k in range(8):
            q.do("pe", lambda e, pb=pb, k=k, nj=nj: e.matmul(
                pb[0:8, :512], WT["lrwT"][:, k * 8:(k + 1) * 8],
                xnT[k][:, 512 * nj:512 * (nj + 1)], start=(k == 0), stop=(k == 7)),
                reads=[WT["lrwT"], xnT[k]], writes=[pb], acc=(k > 0))
        q.do("act", lambda e, pb=pb, nj=nj: e.activation(
            lrS[0:8, 512 * nj:512 * (nj + 1)], pb[0:8, :512], AF.Sigmoid,
            bias=lrbt[:, :], scale=1.0), reads=[pb, lrbt], writes=[lrS])
    q.dma("pool", lr_d[:, :], lrS[:, :], reads=[lrS], writes=[lr_d])

    q.barrier()
    s_pro.close()
    s_bulk = contextlib.ExitStack()
    sbk = mksb(s_bulk)
    xkro = sbk("xkro", [128, 512])
    xkst = sbk("xkst", [128, 128])
    lrTt = sbk("lrTt", [128, 8], F32)
    bgt = sbk("bgt", [128, 512])
    ttile = sbk("ttile", [128, 512])
    tg2 = sbk("tg2", [128, 512])
    xpp = sbk("xpp", [128, 512])
    q.do("dve", lambda e: e.tensor_mul(bgt[:, :], CON["bbc"][:, :], CON["gbc"][:, :]),
         reads=[CON["bbc"], CON["gbc"]], writes=[bgt])
    for i in range(NB):
        biv, pbv = pp.get()
        for k in range(8):
            q.do("pe", lambda e, pb=pbv, k=k, i=i: e.matmul(
                pb[:, :512], xnT[k][:, 128 * i:128 * (i + 1)],
                WT["wv_o"][:, k * 512:(k + 1) * 512], start=(k == 0), stop=(k == 7)),
                reads=[xnT[k], WT["wv_o"]], writes=[pbv], acc=(k > 0))
        for m in range(4):
            bit, pbt = pp.get()
            inview = bass.AP(xqkT[m], 256 * i + 16,
                             [[2 * S, 128], [32, 8], [1, 16]])
            q.do("dve", lambda e, iv=inview: e.tensor_copy(xkst[:, :], iv),
                 reads=[xqkT[m]], writes=[xkst])
            q.do("pe", lambda e, pbt=pbt: e.transpose(
                pbt.bitcast(BF16)[:, 0:128], xkst[:, :], idt[:, :]),
                reads=[xkst, idt], writes=[pbt])
            q.copy("dve", xkro[:, 128 * m:128 * (m + 1)], pbt.bitcast(BF16)[:, 0:128],
                   reads=[pbt], writes=[xkro])
        bit, pbt = pp.get()
        q.do("pe", lambda e, pbt=pbt, i=i: e.transpose(
            pbt[0:8, 0:128].bitcast(F32) if False else pbt[0:128, 0:8],
            lrS[0:8, 128 * i:128 * (i + 1)], idt[:, :]),
            reads=[lrS, idt], writes=[pbt]) if False else None
        bil, pbl = pp.get()
        q.do("pe", lambda e, pbl=pbl, i=i: e.transpose(
            pbl.bitcast(BF16)[:, 0:8], lrS[0:8, 128 * i:128 * (i + 1)], idt[0:8, 0:8]),
            reads=[lrS, idt], writes=[pbl])
        q.copy("dve", lrTt[:, :], pbl.bitcast(BF16)[:, 0:8], reads=[pbl], writes=[lrTt])
        q.do("dve", lambda e, pb=pbv: e.tensor_sub(ttile[:, :], xkro[:, :], pb[:, :512]),
             reads=[xkro, pbv], writes=[ttile])
        q.do("dve", lambda e: e.tensor_mul(ttile[:, :], ttile[:, :], CON["gbc"][:, :]),
             reads=[ttile, CON["gbc"]], writes=[ttile])
        q.do("dve", lambda e: e.tensor_add(tg2[:, :], ttile[:, :], bgt[:, :]),
             reads=[ttile, bgt], writes=[tg2])
        for p in range(8):
            q.do("dve", lambda e, p=p: e.tensor_scalar(
                xpp[:, 64 * p:64 * (p + 1)], xkro[:, 64 * p:64 * (p + 1)],
                lrTt[:, p:p + 1], t15c[:, :], ALU.mult, ALU.mult),
                reads=[xkro, lrTt, t15c], writes=[xpp], acc=(p > 0))
        for tp in range(8):
            t_ = 8 * i + tp
            for srct, coff in ((tg2, 0), (xpp, 64)):
                for Pp in range(2):
                    q.dma("sp",
                          bass.AP(gxp_pad, Pp * 32768 + t_ * 128 + coff,
                                  [[65536, 16], [2097152, 4], [1, 64]]),
                          bass.AP(srct, 16 * tp * 512 + 64 * Pp,
                                  [[512, 16], [128, 4], [1, 64]]),
                          reads=[srct], writes=[gxp_pad], inc="bp%d" % (tp % 4))

    q.barrier()
    s_bulk.close()
    s_ln.close()
    s1.close()

    s2 = ctx.enter_context(contextlib.ExitStack())
    sb2 = mksb(s2)
    w1m = sb2("w1m", [128, 1024], F32)
    w1b = sb2("w1b", [128, 1024])
    khm = [sb2(f"khm{h}", [128, 512], F32) for h in range(2)]
    khb = [sb2(f"khb{h}", [128, 512]) for h in range(2)]
    w2tm = sb2("w2tm", [128, 1024], F32)
    w2tb = sb2("w2tb", [128, 1024])
    b1m = sb2("b1m", [8, 256], F32)
    b1bb = sb2("b1bb", [8, 512])
    b2m = sb2("b2m", [8, 64], F32)
    b2bb = sb2("b2bb", [8, 64])
    for dst, srcn in ((w1m, "w1c0"), (khm[0], "kh0"), (khm[1], "kh1"), (w2tm, "w2t0"),
                      (b1m, "b1c0"), (b2m, "b2c0")):
        q.dma("sp", dst[:, :], W[srcn][:, :], reads=[W[srcn]], writes=[dst])
    for dst, srcn in ((w1b, "w1c0"), (khb[0], "kh0"), (khb[1], "kh1"), (w2tb, "w2t0"),
                      (b2bb, "b2c0")):
        q.dma("pool", dst[:, :], W[srcn][:, :], reads=[W[srcn]], writes=[dst])
    q.dma("pool", b1bb[:, 0:256], W["b1c0"][:, :], reads=[W["b1c0"]], writes=[b1bb])
    q.dma("pool", b1bb[:, 256:512], W["b1c0"][:, :], reads=[W["b1c0"]], writes=[b1bb])
    rgx = [[sb2(f"rgx{P}_{h}", [128, 8 * 128]) for h in range(2)] for P in range(2)]
    lrc = [[sb2(f"lrc{P}_{r}", [128, 1], F32) for r in range(2)] for P in range(2)]
    z1sb = [sb2(f"z1sb{P}", [128, 256]) for P in range(2)]
    gu = [sb2(f"gu{P}", [128, 256]) for P in range(2)]
    gt_ = [sb2(f"gt{P}", [128, 256]) for P in range(2)]
    gw = [sb2(f"gw{P}", [128, 256]) for P in range(2)]
    x2sb = [sb2(f"x2sb{P}", [128, 256]) for P in range(2)]
    x2T = [sb2(f"x2T{P}", [128, 256]) for P in range(2)]
    z1bb = [sb2(f"z1bb{P}", [128, 256]) for P in range(2)]
    x2bb = [sb2(f"x2bb{P}", [128, 256]) for P in range(2)]
    x2bT = [sb2(f"x2bT{P}", [128, 256]) for P in range(2)]
    gz2 = [sb2(f"gz2{P}", [128, 64]) for P in range(2)]
    gz2pp = [sb2(f"gz2pp{P}", [128, 64]) for P in range(2)]
    gz2T = sb2("gz2T", [128, 128])
    gz1 = [sb2(f"gz1{P}", [128, 256]) for P in range(2)]
    gz1pp = [sb2(f"gz1pp{P}", [128, 256]) for P in range(2)]
    statm = [sb2(f"statm{P}", [128, 16]) for P in range(2)]
    statm2 = [sb2(f"statm2{P}", [128, 16]) for P in range(2)]
    xcf = [sb2(f"xcf{P}", [128, 64], F32) for P in range(2)]
    xhf = [sb2(f"xhf{P}", [128, 64], F32) for P in range(2)]
    gof = [sb2(f"gof{P}", [128, 64], F32) for P in range(2)]
    scrf = [sb2(f"scrf{P}", [128, 64], F32) for P in range(2)]
    g2s = [sb2(f"g2s{P}", [128, 64], F32) for P in range(2)]
    mcol = [sb2(f"mcol{P}", [128, 1], F32) for P in range(2)]
    vcol = [sb2(f"vcol{P}", [128, 1], F32) for P in range(2)]
    scol = [sb2(f"scol{P}", [128, 1], F32) for P in range(2)]
    rstc = [sb2(f"rstc{P}", [128, 1], F32) for P in range(2)]
    hcol = [sb2(f"hcol{P}", [128, 1], F32) for P in range(2)]
    acol = [sb2(f"acol{P}", [128, 1], F32) for P in range(2)]
    gsum = [sb2(f"gsum{P}", [128, 1], F32) for P in range(2)]
    gxs = [sb2(f"gxs{P}", [128, 1], F32) for P in range(2)]
    z2es = [sb2(f"z2es{P}", [128, 64], F32) for P in range(2)]

    P2 = {0: p2A, 1: p2B}
    GS = {0: CON["gslotA"], 1: CON["gslotB"]}
    SELZT = {0: CON["selzTA"], 1: CON["selzTB"]}
    SELW = {0: CON["selwA"], 1: CON["selwB"]}
    for P in range(2):
        q.do("dve", lambda e, P=P: e.tensor_mul(g2s[P][:, :], GS[P][:, :], GS[P][:, :]),
             reads=[GS[P]], writes=[g2s[P]])
    for P in range(2):
        for rr_ in range(2):
            q.do("pool", lambda e, tl=lrc[P][rr_]: e.memset(tl[:, :], 0.0),
                 reads=[], writes=[lrc[P][rr_]])

    def ring_fill(P, h, t0):
        sem_name = "rg%d" % P
        dst = rgx[P][h]
        srcap = bass.AP(gxp_pad, P * 32768 + t0 * 128,
                        [[65536, 128], [1, 1024]])
        q.sem(sem_name)
        if q.cnt[sem_name] > 0:
            q.wt("sp", ("eager", sem_name, q.cnt[sem_name]))
        q.do("sp", lambda e, dst=dst, src=srcap: e.dma_start(dst[:, :], src),
             reads=[gxp_pad], writes=[dst], inc=sem_name, n=16, is_dma=True)

    def _lr_slot(t, P):
        r = t % 2
        sem_name = "dl%d" % P
        q.sem(sem_name)
        for j in range(4):
            if q.cnt[sem_name] > 0:
                q.wt("sp", ("eager", sem_name, q.cnt[sem_name]))
            q.do("sp", lambda e, j=j, dst=lrc[P][r], o=(2 * j + P) * S + 16 * t:
                 e.dma_start(dst[32 * j:32 * j + 16, 0:1],
                             bass.AP(lr_d, o, [[1, 16], [1, 1]])),
                 reads=[lr_d], writes=[lrc[P][r]], inc=sem_name,
                 n=16, is_dma=True, acc=(j > 0))

    def stream_loads(t):
        if t + 1 < nstep:
            for P in range(2):
                _lr_slot(t + 1, P)
        if t % 8 != 0:
            return
        if t == 0:
            for P in range(2):
                _lr_slot(0, P)
                ring_fill(P, 0, 0)
                if nstep > 8:
                    ring_fill(P, 1, 8)
        elif t + 8 < nstep:
            for P in range(2):
                ring_fill(P, ((t + 8) // 8) % 2, t + 8)

    def gelu(P, zin, u, tt, w, xout):
        q.do("act", lambda e, z=zin, xo=xout: e.activation(
            xo[:, :], z[:, :], AF.Gelu_apprx_tanh), reads=[zin], writes=[xout])
        q.do("dve", lambda e, z=zin, u=u: e.tensor_mul(u[:, :], z[:, :], z[:, :]),
             reads=[zin], writes=[u])
        q.do("dve", lambda e, z=zin, u=u, w=w: e.scalar_tensor_tensor(
            w[:, :], u[:, :], 0.044715, z[:, :], ALU.mult, ALU.mult),
            reads=[u, zin], writes=[w])
        q.do("dve", lambda e, z=zin, w=w: e.tensor_add(w[:, :], w[:, :], z[:, :]),
             reads=[w, zin], writes=[w])
        q.do("act", lambda e, w=w, tt=tt: e.activation(
            tt[:, :], w[:, :], AF.Tanh, scale=0.79788456), reads=[w], writes=[tt])

    def gelu_f(P, zin, xout):
        q.do("act", lambda e, z=zin, xo=xout: e.activation(
            xo[:, :], z[:, :], AF.Gelu_apprx_tanh), reads=[zin], writes=[xout])

    def gelu_bwd(P, zin, u, tt, w, bout):
        q.do("dve", lambda e, tt=tt, w=w: e.tensor_mul(w[:, :], tt[:, :], tt[:, :]),
             reads=[tt], writes=[w])
        q.do("dve", lambda e, w=w: e.tensor_scalar(
            w[:, :], w[:, :], -1.0, 1.0, ALU.mult, ALU.add), reads=[w], writes=[w])
        q.do("dve", lambda e, u=u, b=bout: e.tensor_scalar(
            b[:, :], u[:, :], 0.05351611215, 0.39894228, ALU.mult, ALU.add),
            reads=[u], writes=[bout])
        q.do("dve", lambda e, w=w, b=bout: e.tensor_mul(b[:, :], b[:, :], w[:, :]),
             reads=[bout, w], writes=[bout])
        q.do("dve", lambda e, z=zin, b=bout: e.tensor_mul(b[:, :], b[:, :], z[:, :]),
             reads=[bout, zin], writes=[bout])
        q.do("dve", lambda e, tt=tt, w=w: e.tensor_scalar(
            w[:, :], tt[:, :], 0.5, 0.5, ALU.mult, ALU.add), reads=[tt], writes=[w])
        q.do("dve", lambda e, w=w, b=bout: e.tensor_add(b[:, :], b[:, :], w[:, :]),
             reads=[bout, w], writes=[bout])

    bz1 = {0: ps[0], 1: ps[1]}
    bC = {0: ps[2], 1: ps[3]}
    bz2 = {0: ps[4], 1: ps[5]}
    bg = bmix = ps[7]

    def e1_step(P, tt):
        q.do("pe", lambda e, P=P: e.matmul(
            bz1[P][:, 0:512], SELW[P][:, :], b1bb[:, :], start=True, stop=False,
            skip_group_check=True),
            reads=[SELW[P], b1bb], writes=[bz1[P]])
        for j in range(4):
            p = 2 * j + P
            q.do("pe", lambda e, P=P, j=j, p=p, t=tt: e.matmul(
                bz1[P][32 * j:32 * j + 16, 0:256],
                xqkT[p // 2][64 * (p % 2):64 * (p % 2) + 64,
                             32 * t + 16:32 * t + 32],
                w1b[64 * P:64 * P + 64, 256 * j:256 * (j + 1)],
                start=False, stop=False, tile_position=(64 * P, 32 * j),
                skip_group_check=True),
                reads=[xqkT[p // 2], w1b], writes=[bz1[P]], acc=True)
            q.do("pe", lambda e, P=P, j=j, p=p, t=tt: e.matmul(
                bz1[P][32 * j:32 * j + 16, 256:512],
                xqkT[p // 2][64 * (p % 2):64 * (p % 2) + 64,
                             32 * t:32 * t + 16],
                w1b[64 * P:64 * P + 64, 256 * j:256 * (j + 1)],
                start=False, stop=(j == 3), tile_position=(64 * P, 32 * j),
                skip_group_check=True),
                reads=[xqkT[p // 2], w1b], writes=[bz1[P]], acc=True)

    def attn1_step(P, tt):
        q.do("pe", lambda e, P=P: e.matmul(
            bC[P][:, 0:16], onesc[0:1, 0:128], onesc[0:1, 0:16], start=True, stop=False,
            skip_group_check=True),
            reads=[onesc], writes=[bC[P]])
        for j in range(4):
            p = 2 * j + P
            q.do("pe", lambda e, P=P, j=j, p=p, t=tt: e.matmul(
                bC[P][32 * j:32 * j + 16, 0:16],
                xqkT[p // 2][64 * (p % 2):64 * (p % 2) + 64,
                             32 * t + 16:32 * t + 32],
                xqkT[p // 2][64 * (p % 2):64 * (p % 2) + 64, 32 * t:32 * t + 16],
                start=False, stop=(j == 3), tile_position=(64 * P, 32 * j),
                skip_group_check=True),
                reads=[xqkT[p // 2]], writes=[bC[P]], acc=True)

    stream_loads(0)
    for P in range(2):
        e1_step(P, 0)
        attn1_step(P, 0)
    for t in range(nstep):
        r = t % 2
        if t > 0:
            stream_loads(t)
        rh, rc = (t // 8) % 2, 128 * (t % 8)
        tgs_t = [rgx[P][rh][:, rc:rc + 64] for P in range(2)]
        xps_t = [rgx[P][rh] for P in range(2)]
        lrc_t = [lrc[P][t % 2][:, :] for P in range(2)]
        def st_gelu1(P):
            q.copy("act", z1sb[P][:, :], bz1[P][:, 0:256], reads=[bz1[P]], writes=[z1sb[P]])
            gelu(P, z1sb[P], gu[P], gt_[P], gw[P], x2sb[P])
            for h in range(2):
                bi6, pb6 = pp.get(6)
                q.do("pe", lambda e, P=P, h=h, pb6=pb6: e.transpose(
                    pb6.bitcast(BF16)[:, 0:128], x2sb[P][:, 128 * h:128 * (h + 1)],
                    idt[:, :]), reads=[x2sb[P], idt], writes=[pb6])
                q.copy("act", x2T[P][:, 128 * h:128 * (h + 1)], pb6.bitcast(BF16)[:, 0:128],
                       reads=[pb6], writes=[x2T[P]])
        def st_e2a(P):
            q.do("pe", lambda e, P=P: e.matmul(
                bz2[P][:, 0:64], SELW[P][:, :], b2bb[:, :], start=True, stop=False,
                skip_group_check=True),
                reads=[SELW[P], b2bb], writes=[bz2[P]])
            for j in range(4):
                for h in range(2):
                    q.do("pe", lambda e, P=P, j=j, h=h: e.matmul(
                        bz2[P][32 * j:32 * j + 16, 0:64],
                        x2T[P][:, 128 * h + 32 * j:128 * h + 32 * j + 16],
                        khb[h][:, 64 * (2 * j + P):64 * (2 * j + P) + 64],
                        start=False, stop=(j == 3 and h == 1), tile_position=(0, 32 * j),
                        skip_group_check=True),
                        reads=[x2T[P], khb[h]], writes=[bz2[P]], acc=True)
        def st_ln(P):
            q.do("dve", lambda e, P=P: e.reduce_sum(mcol[P][:, :], bz2[P][:, 0:64], axis=AX),
                 reads=[bz2[P]], writes=[mcol[P]])
            q.do("dve", lambda e, P=P: e.tensor_scalar_mul(
                mcol[P][:, :], mcol[P][:, :], 1.0 / 64.0), reads=[mcol[P]], writes=[mcol[P]])
            q.do("dve", lambda e, P=P: e.tensor_scalar(
                xcf[P][:, :], bz2[P][:, 0:64], mcol[P][:, :], None, ALU.subtract),
                reads=[bz2[P], mcol[P]], writes=[xcf[P]])
            q.do("dve", lambda e, P=P: e.scalar_tensor_tensor(
                scrf[P][:, :], xcf[P][:, :], 0.0, xcf[P][:, :], ALU.add, ALU.mult,
                accum_out=vcol[P][:, :]), reads=[xcf[P]], writes=[scrf[P], vcol[P]])
            q.do("dve", lambda e, P=P: e.tensor_scalar(
                vcol[P][:, :], vcol[P][:, :], 1.0 / 64.0, 1e-6, ALU.mult, ALU.add),
                reads=[vcol[P]], writes=[vcol[P]])
            q.do("dve", lambda e, P=P: e.reciprocal(scol[P][:, :], vcol[P][:, :]),
                 reads=[vcol[P]], writes=[scol[P]])
            q.do("act", lambda e, P=P: e.activation(
                scrf[P][:, :], xcf[P][:, :], AF.Abs, accum_out=acol[P][:, :]),
                reads=[xcf[P]], writes=[scrf[P], acol[P]])
            q.do("dve", lambda e, P=P: e.tensor_scalar(
                acol[P][:, :], acol[P][:, :], 1.0 / 64.0, 8e-4, ALU.mult, ALU.add),
                reads=[acol[P]], writes=[acol[P]])
            q.do("dve", lambda e, P=P: e.reciprocal(rstc[P][:, :], acol[P][:, :]),
                 reads=[acol[P]], writes=[rstc[P]])
            q.do("dve", lambda e, P=P: e.tensor_scalar_mul(
                rstc[P][:, :], rstc[P][:, :], 1.2533141), reads=[rstc[P]], writes=[rstc[P]])
            for it in range(HEFF):
                q.do("dve", lambda e, P=P: e.reciprocal(hcol[P][:, :], rstc[P][:, :]),
                     reads=[rstc[P]], writes=[hcol[P]])
                q.do("dve", lambda e, P=P: e.tensor_mul(
                    hcol[P][:, :], hcol[P][:, :], scol[P][:, :]),
                    reads=[hcol[P], scol[P]], writes=[hcol[P]])
                q.do("dve", lambda e, P=P: e.tensor_add(
                    rstc[P][:, :], rstc[P][:, :], hcol[P][:, :]),
                    reads=[rstc[P], hcol[P]], writes=[rstc[P]])
                q.do("dve", lambda e, P=P: e.tensor_scalar_mul(
                    rstc[P][:, :], rstc[P][:, :], 0.5), reads=[rstc[P]], writes=[rstc[P]])
            q.do("dve", lambda e, P=P: e.tensor_scalar(
                xhf[P][:, :], xcf[P][:, :], rstc[P][:, :], None, ALU.mult),
                reads=[xcf[P], rstc[P]], writes=[xhf[P]])
            q.do("dve", lambda e, P=P: e.tensor_mul(gof[P][:, :], g2s[P][:, :], xhf[P][:, :]),
                 reads=[g2s[P], xhf[P]], writes=[gof[P]])
            q.do("dve", lambda e, P=P, tg=tgs_t[P]: e.scalar_tensor_tensor(
                gof[P][:, :], gof[P][:, :], 1.0, tg, ALU.mult, ALU.add,
                accum_out=gsum[P][:, :]),
                reads=[gof[P], rgx[P][rh]], writes=[gof[P], gsum[P]])
            q.do("dve", lambda e, P=P: e.scalar_tensor_tensor(
                scrf[P][:, :], gof[P][:, :], 1.0, xhf[P][:, :], ALU.mult, ALU.mult,
                accum_out=gxs[P][:, :]),
                reads=[gof[P], xhf[P]], writes=[scrf[P], gxs[P]])
            q.do("dve", lambda e, P=P: e.tensor_scalar(
                scrf[P][:, :], gof[P][:, :], 64.0, gsum[P][:, :], ALU.mult, ALU.subtract),
                reads=[gof[P], gsum[P]], writes=[scrf[P]])
            q.do("dve", lambda e, P=P: e.tensor_scalar(
                xhf[P][:, :], xhf[P][:, :], gxs[P][:, :], None, ALU.mult),
                reads=[xhf[P], gxs[P]], writes=[xhf[P]])
            q.do("dve", lambda e, P=P: e.tensor_sub(scrf[P][:, :], scrf[P][:, :], xhf[P][:, :]),
                 reads=[scrf[P], xhf[P]], writes=[scrf[P]])
            q.do("dve", lambda e, P=P: e.tensor_scalar(
                gz2[P][:, :], scrf[P][:, :], rstc[P][:, :], 1.0 / 64.0, ALU.mult, ALU.mult),
                reads=[scrf[P], rstc[P]], writes=[gz2[P]])
            q.do("dve", lambda e, P=P, lc=lrc_t[P]: e.tensor_scalar(
                gz2pp[P][:, :], gz2[P][:, :], lc, t15c[:, :], ALU.mult, ALU.mult),
                reads=[gz2[P], lrc[P][r], t15c], writes=[gz2pp[P]])
            bi6, pb6 = pp.get(6)
            q.do("pe", lambda e, P=P, pb6=pb6: e.transpose(
                pb6.bitcast(BF16)[64 * P:64 * P + 64, 0:128], gz2[P][:, 0:64], idt[:, :]),
                reads=[gz2[P], idt], writes=[pb6])
            q.copy("dve", gz2T[64 * P:64 * P + 64, :],
                   pb6.bitcast(BF16)[64 * P:64 * P + 64, 0:128],
                   reads=[pb6], writes=[gz2T])
        def st_e3(P):
            q.do("pe", lambda e, P=P, bg=bg: e.matmul(
                bg[:, 256 * P:256 * P + 256], zeroc[0:1, 0:128], onesc[0:1, 0:256],
                start=True, stop=False, skip_group_check=True),
                reads=[zeroc, onesc], writes=[bg])
            for j in range(4):
                q.do("pe", lambda e, P=P, j=j, bg=bg: e.matmul(
                    bg[32 * j:32 * j + 16, 256 * P:256 * P + 256],
                    gz2T[64 * P:64 * P + 64, 32 * j:32 * j + 16],
                    w2tb[64 * P:64 * P + 64, 256 * j:256 * (j + 1)],
                    start=False, stop=(j == 3), tile_position=(64 * P, 32 * j),
                    skip_group_check=True),
                    reads=[gz2T, w2tb], writes=[bg], acc=True)
            gelu_bwd(P, z1sb[P], gu[P], gt_[P], gw[P], gz1[P])
            q.do("dve", lambda e, P=P, bg=bg: e.tensor_mul(
                gz1[P][:, :], gz1[P][:, :], bg[:, 256 * P:256 * P + 256]),
                reads=[gz1[P], bg], writes=[gz1[P]])
            q.do("dve", lambda e, P=P, lc=lrc_t[P]: e.tensor_scalar(
                gz1pp[P][:, :], gz1[P][:, :], lc, t15c[:, :], ALU.mult, ALU.mult),
                reads=[gz1[P], lrc[P][r], t15c], writes=[gz1pp[P]])
        def st_mix(P):
            q.do("dve", lambda e, P=P: e.tensor_mul(
                statm[P][:, :], bC[P][:, 0:16], P2[P][:, :]),
                reads=[bC[P], P2[P]], writes=[statm[P]])
            q.do("dve", lambda e, P=P, lc=lrc_t[P]: e.tensor_scalar(
                statm[P][:, :], statm[P][:, :], lc, None, ALU.mult),
                reads=[statm[P], lrc[P][r]], writes=[statm[P]])
            q.do("pe", lambda e, P=P, bmix=bmix: e.matmul(
                bmix[:, 256 * P:256 * P + 256], zeroc[0:1, 0:128], onesc[0:1, 0:256],
                start=True, stop=False, skip_group_check=True),
                reads=[zeroc, onesc], writes=[bmix])
            for j in range(4):
                q.do("pe", lambda e, P=P, j=j, bmix=bmix: e.matmul(
                    bmix[32 * j:32 * j + 16, 256 * P:256 * P + 256],
                    statm[P][32 * j:32 * j + 16, 0:16],
                    gz1[P][32 * j:32 * j + 16, 0:256],
                    start=False, stop=(j == 3), tile_position=(32 * j, 32 * j),
                    skip_group_check=True),
                    reads=[statm[P], gz1[P]], writes=[bmix], acc=True)
        def st_z1ev(P):
            q.copy("act", z1bb[P][:, :], bz1[P][:, 256:512], reads=[bz1[P]], writes=[z1bb[P]])
            q.do("dve", lambda e, P=P, bmix=bmix: e.tensor_add(
                z1bb[P][:, :], z1bb[P][:, :], bmix[:, 256 * P:256 * P + 256]),
                reads=[z1bb[P], bmix], writes=[z1bb[P]])
        def st_gelu2b(P):
            gelu_f(P, z1bb[P], x2bb[P])
            for h in range(2):
                bi6, pb6 = pp.get(6)
                q.do("pe", lambda e, P=P, h=h, pb6=pb6: e.transpose(
                    pb6.bitcast(BF16)[:, 0:128], x2bb[P][:, 128 * h:128 * (h + 1)],
                    idt[:, :]), reads=[x2bb[P], idt], writes=[pb6])
                q.copy("act", x2bT[P][:, 128 * h:128 * (h + 1)], pb6.bitcast(BF16)[:, 0:128],
                       reads=[pb6], writes=[x2bT[P]])
        def st_attn2(P):
            q.do("pe", lambda e, P=P: e.matmul(
                bC[P][:, 16:32], onesc[0:1, 0:128], onesc[0:1, 0:16], start=True, stop=False,
                skip_group_check=True),
                reads=[onesc], writes=[bC[P]])
            for j in range(4):
                for h in range(2):
                    q.do("pe", lambda e, P=P, j=j, h=h: e.matmul(
                        bC[P][32 * j:32 * j + 16, 16:32],
                        x2T[P][:, 128 * h + 32 * j:128 * h + 32 * j + 16],
                        x2bT[P][:, 128 * h + 32 * j:128 * h + 32 * j + 16],
                        start=False, stop=(j == 3 and h == 1), tile_position=(0, 32 * j),
                        skip_group_check=True),
                        reads=[x2T[P], x2bT[P]], writes=[bC[P]], acc=True)
        def st_e2b(P):
            q.do("pe", lambda e, P=P: e.matmul(
                bz2[P][:, 128:192], SELW[P][:, :], b2bb[:, :], start=True, stop=False,
                skip_group_check=True),
                reads=[SELW[P], b2bb], writes=[bz2[P]])
            for j in range(4):
                for h in range(2):
                    q.do("pe", lambda e, P=P, j=j, h=h: e.matmul(
                        bz2[P][32 * j:32 * j + 16, 128:192],
                        x2bT[P][:, 128 * h + 32 * j:128 * h + 32 * j + 16],
                        khb[h][:, 64 * (2 * j + P):64 * (2 * j + P) + 64],
                        start=False, stop=False, tile_position=(0, 32 * j),
                        skip_group_check=True),
                        reads=[x2bT[P], khb[h]], writes=[bz2[P]], acc=True)
        def st_mix2(P):
            q.do("dve", lambda e, P=P: e.tensor_mul(
                statm2[P][:, :], bC[P][:, 16:32], P2[P][:, :]),
                reads=[bC[P], P2[P]], writes=[statm2[P]])
            q.do("dve", lambda e, P=P, lc=lrc_t[P]: e.tensor_scalar(
                statm2[P][:, :], statm2[P][:, :], lc, None, ALU.mult),
                reads=[statm2[P], lrc[P][r]], writes=[statm2[P]])
            for j in range(4):
                q.do("pe", lambda e, P=P, j=j: e.matmul(
                    bz2[P][32 * j:32 * j + 16, 128:192],
                    statm2[P][32 * j:32 * j + 16, 0:16],
                    gz2[P][32 * j:32 * j + 16, 0:64],
                    start=False, stop=(j == 3), tile_position=(32 * j, 32 * j),
                    skip_group_check=True),
                    reads=[statm2[P], gz2[P]], writes=[bz2[P]], acc=True)
        def st_out(P):
            q.copy("act", z2es[P][:, :], bz2[P][:, 128:192], reads=[bz2[P]], writes=[z2es[P]])
            if DISABLE_ZOUT and t > 1:
                return
            if fused:
                q.dma("pool", bass.AP(z2_pad, (2 * t + P) * 128 * 64, [[64, 128], [1, 64]]),
                      z2es[P][:, :], reads=[z2es[P]], writes=[z2_pad])
            else:
                for j in range(4):
                    p = 2 * j + P
                    q.dma("pool",
                          bass.AP(z2bo, (16 * t) * 512 + 64 * p, [[512, 16], [1, 64]]),
                          z2es[P][32 * j:32 * j + 16, 0:64],
                          reads=[z2es[P]], writes=[z2bo])
        stagesA = [st_gelu1, st_e2a, st_ln, st_e3, st_mix, st_z1ev,
                   st_gelu2b, st_attn2, st_e2b, st_mix2, st_out]
        for si in range(len(stagesA) + SKEW):
            if si < len(stagesA):
                stagesA[si](0)
            if si >= SKEW:
                stagesA[si - SKEW](1)
        bu1, bu2 = ps[0], ps[1]
        for P in range(2):
            for j in range(4):
                bu = bu1 if j < 2 else bu2
                q.do("pe", lambda e, P=P, j=j, rh=rh, rc=rc, bu=bu: e.matmul(
                    bu[64 * P:64 * P + 64, 256 * (j % 2):256 * (j % 2) + 256],
                    rgx[P][rh][32 * j:32 * j + 16, rc + 64:rc + 128],
                    gz1[P][32 * j:32 * j + 16, 0:256],
                    start=True, stop=True, tile_position=(32 * j, 64 * P)),
                    reads=[rgx[P][rh], gz1[P]], writes=[bu])
        q.do("dve", lambda e: e.scalar_tensor_tensor(
            w1m[:, 0:512], bu1[:, 0:512], -1.0, w1m[:, 0:512], ALU.mult, ALU.add),
            reads=[bu1, w1m], writes=[w1m])
        q.do("dve", lambda e: e.scalar_tensor_tensor(
            w1m[:, 512:1024], bu2[:, 0:512], -1.0, w1m[:, 512:1024], ALU.mult, ALU.add),
            reads=[bu2, w1m], writes=[w1m])
        q.copy("act", w1b[:, 0:512], w1m[:, 0:512], reads=[w1m], writes=[w1b])
        q.copy("act", w1b[:, 512:1024], w1m[:, 512:1024], reads=[w1m], writes=[w1b])
        bb = ps[6]
        for P in range(2):
            q.do("pe", lambda e, P=P, bb=bb: e.matmul(
                bb[0:8, 0:256], SELZT[P][:, :], gz1pp[P][:, 0:256],
                start=(P == 0), stop=(P == 1), skip_group_check=True),
                reads=[SELZT[P], gz1pp[P]], writes=[bb], acc=(P == 1))
        for P in range(2):
            q.do("pe", lambda e, P=P, bb=bb: e.matmul(
                bb[0:8, 256:320], SELZT[P][:, :], gz2pp[P][:, 0:64],
                start=(P == 0), stop=(P == 1), skip_group_check=True),
                reads=[SELZT[P], gz2pp[P]], writes=[bb], acc=(P == 1))
        q.do("dve", lambda e, bb=bb: e.scalar_tensor_tensor(
            b1m[:, :], bb[0:8, 0:256], -1.0, b1m[:, :], ALU.mult, ALU.add),
            reads=[bb, b1m], writes=[b1m])
        q.do("dve", lambda e, bb=bb: e.scalar_tensor_tensor(
            b2m[:, :], bb[0:8, 256:320], -1.0, b2m[:, :], ALU.mult, ALU.add),
            reads=[bb, b2m], writes=[b2m])
        q.copy("act", b1bb[:, 0:256], b1m[:, :], reads=[b1m], writes=[b1bb])
        q.copy("act", b1bb[:, 256:512], b1m[:, :], reads=[b1m], writes=[b1bb])
        q.copy("act", b2bb[:, :], b2m[:, :], reads=[b2m], writes=[b2bb])
        bk0, bk1 = ps[4], ps[5]
        for P in range(2):
            for j in range(4):
                p = 2 * j + P
                for h, bk in ((0, bk0), (1, bk1)):
                    q.do("pe", lambda e, P=P, j=j, p=p, h=h, bk=bk: e.matmul(
                        bk[:, 64 * p:64 * p + 64],
                        x2sb[P][32 * j:32 * j + 16, 128 * h:128 * h + 128],
                        gz2pp[P][32 * j:32 * j + 16, 0:64],
                        start=True, stop=True, tile_position=(32 * j, 0)),
                        reads=[x2sb[P], gz2pp[P]], writes=[bk])
        for h, bk in ((0, bk0), (1, bk1)):
            q.do("dve", lambda e, h=h, bk=bk: e.scalar_tensor_tensor(
                khm[h][:, :], bk[:, 0:512], -1.0, khm[h][:, :], ALU.mult, ALU.add),
                reads=[bk, khm[h]], writes=[khm[h]])
            q.copy("act", khb[h][:, :], khm[h][:, :], reads=[khm[h]], writes=[khb[h]])
        bt0, bt1 = ps[2], ps[3]
        for P in range(2):
            for j in range(4):
                p = 2 * j + P
                bt = bt0 if j < 2 else bt1
                q.do("pe", lambda e, P=P, j=j, p=p, bt=bt: e.matmul(
                    bt[64 * P:64 * P + 64, 256 * (j % 2):256 * (j % 2) + 256],
                    gz2pp[P][32 * j:32 * j + 16, 0:64],
                    x2sb[P][32 * j:32 * j + 16, 0:256],
                    start=True, stop=True, tile_position=(32 * j, 64 * P)),
                    reads=[gz2pp[P], x2sb[P]], writes=[bt])
        q.do("dve", lambda e: e.scalar_tensor_tensor(
            w2tm[:, 0:512], bt0[:, 0:512], -1.0, w2tm[:, 0:512], ALU.mult, ALU.add),
            reads=[bt0, w2tm], writes=[w2tm])
        q.do("dve", lambda e: e.scalar_tensor_tensor(
            w2tm[:, 512:1024], bt1[:, 0:512], -1.0, w2tm[:, 512:1024], ALU.mult, ALU.add),
            reads=[bt1, w2tm], writes=[w2tm])
        q.copy("act", w2tb[:, 0:512], w2tm[:, 0:512], reads=[w2tm], writes=[w2tb])
        q.copy("act", w2tb[:, 512:1024], w2tm[:, 512:1024], reads=[w2tm], writes=[w2tb])
        if t + 1 < nstep:
            for P in range(2):
                e1_step(P, t + 1)
                attn1_step(P, t + 1)

    if dbg_o:
        dbgmap = {"d_w1m": (w1m, 128, 1024), "d_gz1A": (gz1[0], 128, 256),
                  "d_x2A": (x2sb[0], 128, 256), "d_lrS": (lrS, 8, 512),
                  "d_xqk0": (xqkT[0], 128, 512),
                  "d_khm0": (khm[0], 128, 512), "d_khm1": (khm[1], 128, 512),
                  "d_w2tm": (w2tm, 128, 1024), "d_b1m": (b1m, 8, 256),
                  "d_b2m": (b2m, 8, 64), "d_gz2A": (gz2[0], 128, 64),
                  "d_tgsA": (rgx[0][0], 128, 64), "d_xpsA": (rgx[0][0], 128, 128),
                  "d_z1bbA": (z1bb[0], 128, 256),
                  "d_tgsA1": (rgx[0][0], 128, 256), "d_xpsA1": (rgx[0][0], 128, 256),
                  "d_lrcA1": (lrc[0][1], 128, 1), "d_w1b": (w1b, 128, 1024),
                  "d_b1bb": (b1bb, 8, 512), "d_z1sbA": (z1sb[0], 128, 256),
                  "d_xcfA": (xcf[0], 128, 64), "d_rstA": (rstc[0], 128, 1),
                  "d_gofA": (gof[0], 128, 64), "d_xhfA": (xhf[0], 128, 64)}
        for nname, (tt_, rr, cc) in dbgmap.items():
            if nname in dbg_o:
                q.dma("pool", dbg_o[nname][:, :], tt_[:rr, 0:cc],
                      reads=[tt_], writes=[dbg_o[nname]])
    if fused:
        q.barrier()
        s2.close()
        sp_ = contextlib.ExitStack()
        sbp = mksb(sp_)
        z2t = [sbp(f"pz2t{r}", [128, 512], F32) for r in range(2)]
        xqs = sbp("pxqs", [128, 128])
        xqt = sbp("pxqt", [128, 512], F32)
        pxc = sbp("pxc", [128, 512], F32)
        mu8 = sbp("pmu8", [128, 8], F32)
        v8 = sbp("pv8", [128, 8], F32)
        a8 = sbp("pa8", [128, 8], F32)
        sc8 = sbp("psc8", [128, 8], F32)
        rs8 = sbp("prs8", [128, 8], F32)
        h8 = sbp("ph8", [128, 8], F32)
        sq8 = sbp("psq8", [128, 64], F32)
        ab8 = sbp("pab8", [128, 512], F32)
        for i in range(NB):
            zt = z2t[i % 2]
            for tp in range(8):
                t_ = 8 * i + tp
                for Pp in range(2):
                    q.dma("sp",
                          bass.AP(zt, 16 * tp * 512 + 64 * Pp,
                                  [[512, 16], [128, 4], [1, 64]]),
                          bass.AP(z2_pad, (2 * t_ + Pp) * 8192,
                                  [[64, 16], [2048, 4], [1, 64]]),
                          reads=[z2_pad], writes=[zt], inc="zr%d" % (i % 2))
            for m in range(4):
                iv = bass.AP(xqkT[m], 256 * i, [[2 * S, 128], [32, 8], [1, 16]])
                q.do("dve", lambda e, iv=iv: e.tensor_copy(xqs[:, :], iv),
                     reads=[xqkT[m]], writes=[xqs])
                bi, pb = pp.get()
                q.do("pe", lambda e, pb=pb: e.transpose(
                    pb.bitcast(BF16)[:, 0:128], xqs[:, :], idt[:, :]),
                    reads=[xqs, idt], writes=[pb])
                q.copy("dve", xqt[:, 128 * m:128 * (m + 1)], pb.bitcast(BF16)[:, 0:128],
                       reads=[pb], writes=[xqt])
            for g in range(8):
                q.do("dve", lambda e, g=g, zt=zt: e.reduce_sum(
                    mu8[:, g:g + 1], zt[:, 64 * g:64 * (g + 1)], axis=AX),
                    reads=[zt], writes=[mu8], acc=(g > 0))
            q.do("dve", lambda e: e.tensor_scalar_mul(mu8[:, :], mu8[:, :], 1.0 / 64.0),
                 reads=[mu8], writes=[mu8])
            for g in range(8):
                q.do("dve", lambda e, g=g, zt=zt: e.tensor_scalar(
                    pxc[:, 64 * g:64 * (g + 1)], zt[:, 64 * g:64 * (g + 1)],
                    mu8[:, g:g + 1], None, ALU.subtract),
                    reads=[zt, mu8], writes=[pxc], acc=(g > 0))
            for g in range(8):
                q.do("dve", lambda e, g=g: e.scalar_tensor_tensor(
                    sq8[:, :], pxc[:, 64 * g:64 * (g + 1)], 0.0,
                    pxc[:, 64 * g:64 * (g + 1)], ALU.add, ALU.mult,
                    accum_out=v8[:, g:g + 1]),
                    reads=[pxc], writes=[sq8, v8])
            for g in range(8):
                q.do("act", lambda e, g=g: e.activation(
                    ab8[:, 64 * g:64 * (g + 1)], pxc[:, 64 * g:64 * (g + 1)], AF.Abs,
                    accum_out=a8[:, g:g + 1]),
                    reads=[pxc], writes=[ab8, a8], acc=(g > 0))
            q.do("dve", lambda e: e.tensor_scalar(
                v8[:, :], v8[:, :], 1.0 / 64.0, 1e-6, ALU.mult, ALU.add),
                reads=[v8], writes=[v8])
            q.do("dve", lambda e: e.reciprocal(sc8[:, :], v8[:, :]),
                 reads=[v8], writes=[sc8])
            q.do("dve", lambda e: e.tensor_scalar(
                a8[:, :], a8[:, :], 1.0 / 64.0, 8e-4, ALU.mult, ALU.add),
                reads=[a8], writes=[a8])
            q.do("dve", lambda e: e.reciprocal(rs8[:, :], a8[:, :]),
                 reads=[a8], writes=[rs8])
            q.do("dve", lambda e: e.tensor_scalar_mul(rs8[:, :], rs8[:, :], 1.2533141),
                 reads=[rs8], writes=[rs8])
            for _ in range(HEFF):
                q.do("dve", lambda e: e.reciprocal(h8[:, :], rs8[:, :]),
                     reads=[rs8], writes=[h8])
                q.do("dve", lambda e: e.tensor_mul(h8[:, :], h8[:, :], sc8[:, :]),
                     reads=[h8, sc8], writes=[h8])
                q.do("dve", lambda e: e.tensor_add(rs8[:, :], rs8[:, :], h8[:, :]),
                     reads=[rs8, h8], writes=[rs8])
                q.do("dve", lambda e: e.tensor_scalar_mul(rs8[:, :], rs8[:, :], 0.5),
                     reads=[rs8], writes=[rs8])
            for g in range(8):
                q.do("dve", lambda e, g=g: e.tensor_scalar(
                    pxc[:, 64 * g:64 * (g + 1)], pxc[:, 64 * g:64 * (g + 1)],
                    rs8[:, g:g + 1], None, ALU.mult),
                    reads=[pxc, rs8], writes=[pxc], acc=(g > 0))
            q.do("dve", lambda e: e.tensor_mul(pxc[:, :], pxc[:, :], CON["gbc"][:, :]),
                 reads=[pxc, CON["gbc"]], writes=[pxc])
            q.do("dve", lambda e: e.tensor_add(pxc[:, :], pxc[:, :], CON["bbc"][:, :]),
                 reads=[pxc, CON["bbc"]], writes=[pxc])
            q.do("dve", lambda e: e.tensor_add(pxc[:, :], pxc[:, :], xqt[:, :]),
                 reads=[pxc, xqt], writes=[pxc])
            q.dma("sp", xqwo[128 * i:128 * (i + 1), :], pxc[:, :],
                  reads=[pxc], writes=[xqwo])
        q.barrier()
        sp_.close()
        q.do("pool", lambda e: e.collective_compute(
            "AllGather", ALU.bypass, [[0, 4], [1, 5], [2, 6], [3, 7]],
            [xqwo[:, :]], [gxq[:, :]]), reads=[xqwo], writes=[gxq])
        return {"gx1": gx1, "gxn": gxn, "gxq": gxq}
    q.wt("pool", q.last_tok["pool"])
    q.wt("sp", q.last_tok["sp"])
    q.build()
    ctx.close()
    return nc



TOK3 = S // 2
NT3 = TOK3 // 128

L1P_IN = [
    ("z2b", [TOK3, D]), ("xqp", [TOK3, D]), ("xn1p", [TOK3, D]), ("x1p", [TOK3, D]),
    ("glnb", [128, D]), ("blnb", [128, D]), ("gpnb", [128, D]), ("bpnb", [128, D]),
    ("wot", [D, D]), ("ffw", [D, D]), ("ffb", [1, D]), ("ones1", [1, 128]),
]

L1P_IN_FUSED = [
    ("gpnb", [128, D]), ("bpnb", [128, D]),
    ("wot", [D, D]), ("ffw", [D, D]), ("ffb", [1, D]), ("ones1", [1, 128]),
]


def build_l1p(dbg=(), env=None, gbufs=None):
    fused = env is not None
    pfx = "p_" if fused else ""
    if env is None:
        nc = bass.Bass("TRN2", target_bir_lowering=False)
        ctx = contextlib.ExitStack()
        q = Q(nc, ctx)
        pp = Psum(q, nc, ctx)
    else:
        nc, ctx, q, pp = env
    ps = pp.banks

    W = {n: nc.dram_tensor(pfx + n, list(s), F32, kind="ExternalInput")
         for n, s in (L1P_IN_FUSED if fused else L1P_IN)}
    ntile = NB if fused else NT3
    x2o = nc.dram_tensor("x2o", [128 * ntile, D], F32, kind="ExternalOutput")
    wsc_d = nc.dram_tensor(pfx + "wsc_d", [1, D], F32)
    dbg_o = {n: nc.dram_tensor(n, list(s), F32, kind="ExternalOutput") for n, s in dbg}
    if fused:
        gx1, gxn, gxq = gbufs["gx1"], gbufs["gxn"], gbufs["gxq"]

    def mksb(c):
        return lambda name, shape, dt=BF16: c.enter_context(
            nc.sbuf_tensor(pfx + name, list(shape), dt))

    sb = mksb(ctx)
    ones1 = sb("t_ones1", [1, 128], F32)
    if not fused:
        glnb = sb("t_glnb", [128, D], F32)
        blnb = sb("t_blnb", [128, D], F32)
    gpnb = sb("t_gpnb", [128, D], F32)
    bpnb = sb("t_bpnb", [128, D], F32)
    wotb = [sb(f"wotb{k}", [128, D]) for k in range(8)]
    wqT = [sb(f"wqT{k}", [128, D]) for k in range(8)]
    wsbc = sb("wsbc", [128, D], F32)
    bfbc = sb("bfbc", [128, D], F32)
    names = (("ones1", ones1), ("gpnb", gpnb), ("bpnb", bpnb)) if fused else (
        ("ones1", ones1), ("glnb", glnb), ("blnb", blnb),
        ("gpnb", gpnb), ("bpnb", bpnb))
    for n, dst in names:
        q.dma("sp", dst[:, :], W[n][:, :], reads=[W[n]], writes=[dst])
    for k in range(8):
        q.dma("pool", wotb[k][:, :], W["wot"][128 * k:128 * (k + 1), :],
              reads=[W["wot"]], writes=[wotb[k]])

    s_w = contextlib.ExitStack()
    sbw = mksb(s_w)
    wsl = sbw("wsl", [128, D], F32)
    wtmp = sbw("wtmp", [128, D], F32)
    wint = sbw("wint", [128, D], mybir.dt.int32)
    wqs = [sbw(f"wqs{k}", [128, D]) for k in range(8)]
    wam = sbw("wam", [128, 1], F32)
    for k in range(8):
        q.dma("sp", wsl[:, :], W["ffw"][128 * k:128 * (k + 1), :],
              reads=[W["ffw"]], writes=[wsl])
        q.do("dve", lambda e: e.reduce_max(wam[:, :], wsl[:, :], axis=AX,
                                           apply_absolute_value=True),
             reads=[wsl], writes=[wam])
        q.do("dve", lambda e: e.tensor_scalar(wam[:, :], wam[:, :], 1e-8, None, ALU.max),
             reads=[wam], writes=[wam])
        q.do("dve", lambda e: e.tensor_scalar_mul(wam[:, :], wam[:, :], 1.0 / 127.0),
             reads=[wam], writes=[wam])
        q.dma("sp", bass.AP(wsc_d, 128 * k, [[1, 128], [1, 1]]), wam[:, 0:1],
              reads=[wam], writes=[wsc_d])
        q.do("dve", lambda e: e.reciprocal(wam[:, :], wam[:, :]), reads=[wam], writes=[wam])
        q.do("dve", lambda e: e.tensor_scalar(
            wtmp[:, :], wsl[:, :], wam[:, :], None, ALU.mult),
            reads=[wsl, wam], writes=[wtmp])
        q.do("dve", lambda e: e.tensor_scalar(
            wtmp[:, :], wtmp[:, :], 12582912.0, -12582912.0, ALU.add, ALU.add),
            reads=[wtmp], writes=[wtmp])
        q.do("dve", lambda e: e.tensor_copy(wint[:, :], wtmp[:, :]),
             reads=[wtmp], writes=[wint])
        q.do("dve", lambda e, k=k: e.tensor_copy(wqs[k][:, :], wint[:, :]),
             reads=[wint], writes=[wqs[k]])
    for k in range(8):
        for f in range(8):
            sem_name = "dt%d" % f
            q.sem(sem_name)
            if q.cnt[sem_name] > 0:
                q.wt("sp", ("eager", sem_name, q.cnt[sem_name]))
            q.do("sp", lambda e, k=k, f=f: e.dma_start_transpose(
                wqT[f][:, 128 * k:128 * (k + 1)], wqs[k][:, 128 * f:128 * (f + 1)]),
                reads=[wqs[k]], writes=[wqT[f]], inc=sem_name, n=16, is_dma=True)
    q.barrier()
    s_w.close()

    wrow = sb("wrow", [1, D], F32)
    q.dma("sp", wrow[:, :], wsc_d[:, :], reads=[wsc_d], writes=[wrow])
    for half in range(2):
        bi, pb = pp.get()
        q.do("pe", lambda e, pb=pb, h=half: e.matmul(
            pb[:, 0:512], ones1[0:1, 0:128], wrow[0:1, 512 * h:512 * (h + 1)],
            start=True, stop=True), reads=[ones1, wrow], writes=[pb])
        q.copy("dve", wsbc[:, 512 * half:512 * (half + 1)], pb[:, 0:512],
               reads=[pb], writes=[wsbc])
    brow = sb("brow", [1, D], F32)
    q.dma("sp", brow[:, :], W["ffb"][:, :], reads=[W["ffb"]], writes=[brow])
    for half in range(2):
        bi, pb = pp.get()
        q.do("pe", lambda e, pb=pb, h=half: e.matmul(
            pb[:, 0:512], ones1[0:1, 0:128], brow[0:1, 512 * h:512 * (h + 1)],
            start=True, stop=True), reads=[ones1, brow], writes=[pb])
        q.copy("dve", bfbc[:, 512 * half:512 * (half + 1)], pb[:, 0:512],
               reads=[pb], writes=[bfbc])

    s_t = ctx.enter_context(contextlib.ExitStack())
    sbt = mksb(s_t)
    z2t = [sbt(f"z2t{r}", [128, D], F32) for r in range(2)]
    xqt = [sbt(f"xqt{r}", [128, D], F32) for r in range(2)]
    xnt = [sbt(f"xnt{r}", [128, D], F32) for r in range(2)]
    x1t = [sbt(f"x1t{r}", [128, D], F32) for r in range(2)]
    xct = sbt("xct", [128, D], F32)
    sqd = sbt("sqd", [128, 64], F32)
    mu16 = sbt("mu16", [128, 16], F32)
    v16 = sbt("v16", [128, 16], F32)
    a16 = sbt("a16", [128, 16], F32)
    sc16 = sbt("sc16", [128, 16], F32)
    rs16 = sbt("rs16", [128, 16], F32)
    h16 = sbt("h16", [128, 16], F32)
    mup = sbt("mup", [128, 1], F32)
    vp = sbt("vp", [128, 1], F32)
    ap1 = sbt("ap1", [128, 1], F32)
    scp = sbt("scp", [128, 1], F32)
    rsp = sbt("rsp", [128, 1], F32)
    hp = sbt("hp", [128, 1], F32)
    absd = sbt("absd", [128, D], F32)
    xpnb = sbt("xpnb", [128, D])
    xpT = sbt("xpT", [128, D])
    xam = sbt("xam", [128, 1], F32)
    xqf = sbt("xqf", [128, D], F32)
    xint = sbt("xint", [128, D], mybir.dt.int32)
    xqb = sbt("xqb", [128, D])
    xqTt = sbt("xqTt", [128, D])
    stile = sbt("stile", [128, D], F32)
    gq = sbt("gq", [128, D], F32)

    def heron(rst, sc, h_, n=HEFF):
        for _ in range(n):
            q.do("dve", lambda e, rst=rst, h_=h_: e.reciprocal(h_[:, :], rst[:, :]),
                 reads=[rst], writes=[h_])
            q.do("dve", lambda e, h_=h_, sc=sc: e.tensor_mul(h_[:, :], h_[:, :], sc[:, :]),
                 reads=[h_, sc], writes=[h_])
            q.do("dve", lambda e, rst=rst, h_=h_: e.tensor_add(
                rst[:, :], rst[:, :], h_[:, :]), reads=[rst, h_], writes=[rst])
            q.do("dve", lambda e, rst=rst: e.tensor_scalar_mul(rst[:, :], rst[:, :], 0.5),
                 reads=[rst], writes=[rst])

    def tile_in(i):
        r = i % 2
        if fused:
            off = (i % 2) * OWNQ + 128 * (i // 2)
            q.dma("sp", z2t[r][:, 0:512], gxq[128 * i:128 * (i + 1), :],
                  reads=[gxq], writes=[z2t[r]])
            q.dma("sp", z2t[r][:, 512:1024], gxq[S + 128 * i:S + 128 * (i + 1), :],
                  reads=[gxq], writes=[z2t[r]])
            q.dma("sp", xnt[r][:, :], gxn[off:off + 128, :], reads=[gxn], writes=[xnt[r]])
            q.dma("sp", x1t[r][:, :], gx1[off:off + 128, :], reads=[gx1], writes=[x1t[r]])
            return
        for dst, srcn in ((z2t[r], "z2b"), (xqt[r], "xqp"), (xnt[r], "xn1p"),
                          (x1t[r], "x1p")):
            q.dma("sp", dst[:, :], W[srcn][128 * i:128 * (i + 1), :],
                  reads=[W[srcn]], writes=[dst])

    tile_in(0)
    for i in range(ntile):
        r = i % 2
        if i + 1 < ntile:
            tile_in(i + 1)
        zt, qt, nt, ot = z2t[r], xqt[r], xnt[r], x1t[r]
        if not fused:
            for g in range(16):
                q.do("dve", lambda e, g=g, zt=zt: e.reduce_sum(
                    mu16[:, g:g + 1], zt[:, 64 * g:64 * (g + 1)], axis=AX),
                    reads=[zt], writes=[mu16], acc=(g > 0))
            q.do("dve", lambda e: e.tensor_scalar_mul(mu16[:, :], mu16[:, :], 1.0 / 64.0),
                 reads=[mu16], writes=[mu16])
            for g in range(16):
                q.do("dve", lambda e, g=g, zt=zt: e.tensor_scalar(
                    xct[:, 64 * g:64 * (g + 1)], zt[:, 64 * g:64 * (g + 1)],
                    mu16[:, g:g + 1], None, ALU.subtract),
                    reads=[zt, mu16], writes=[xct], acc=(g > 0))
            for g in range(16):
                q.do("dve", lambda e, g=g: e.scalar_tensor_tensor(
                    sqd[:, :], xct[:, 64 * g:64 * (g + 1)], 0.0,
                    xct[:, 64 * g:64 * (g + 1)], ALU.add, ALU.mult,
                    accum_out=v16[:, g:g + 1]),
                    reads=[xct], writes=[sqd, v16])
            for g in range(16):
                q.do("act", lambda e, g=g: e.activation(
                    absd[:, 64 * g:64 * (g + 1)], xct[:, 64 * g:64 * (g + 1)], AF.Abs,
                    accum_out=a16[:, g:g + 1]),
                    reads=[xct], writes=[absd, a16], acc=(g > 0))
            q.do("dve", lambda e: e.tensor_scalar(
                v16[:, :], v16[:, :], 1.0 / 64.0, 1e-6, ALU.mult, ALU.add),
                reads=[v16], writes=[v16])
            q.do("dve", lambda e: e.reciprocal(sc16[:, :], v16[:, :]),
                 reads=[v16], writes=[sc16])
            q.do("dve", lambda e: e.tensor_scalar(
                a16[:, :], a16[:, :], 1.0 / 64.0, 8e-4, ALU.mult, ALU.add),
                reads=[a16], writes=[a16])
            q.do("dve", lambda e: e.reciprocal(rs16[:, :], a16[:, :]),
                 reads=[a16], writes=[rs16])
            q.do("dve", lambda e: e.tensor_scalar_mul(rs16[:, :], rs16[:, :], 1.2533141),
                 reads=[rs16], writes=[rs16])
            heron(rs16, sc16, h16)
            for g in range(16):
                q.do("dve", lambda e, g=g: e.tensor_scalar(
                    xct[:, 64 * g:64 * (g + 1)], xct[:, 64 * g:64 * (g + 1)],
                    rs16[:, g:g + 1], None, ALU.mult),
                    reads=[xct, rs16], writes=[xct], acc=(g > 0))
            q.do("dve", lambda e: e.tensor_mul(xct[:, :], xct[:, :], glnb[:, :]),
                 reads=[xct, glnb], writes=[xct])
            q.do("dve", lambda e: e.tensor_add(xct[:, :], xct[:, :], blnb[:, :]),
                 reads=[xct, blnb], writes=[xct])
            q.do("dve", lambda e, qt=qt: e.tensor_add(xct[:, :], xct[:, :], qt[:, :]),
                 reads=[xct, qt], writes=[xct])
        src0 = zt if fused else xct
        q.do("dve", lambda e, src0=src0: e.reduce_sum(mup[:, :], src0[:, :], axis=AX),
             reads=[src0], writes=[mup])
        q.do("dve", lambda e: e.tensor_scalar_mul(mup[:, :], mup[:, :], 1.0 / D),
             reads=[mup], writes=[mup])
        q.do("dve", lambda e, src0=src0: e.tensor_scalar(
            xct[:, :], src0[:, :], mup[:, :], None, ALU.subtract),
            reads=[src0, mup], writes=[xct])
        q.do("dve", lambda e: e.scalar_tensor_tensor(
            absd[:, :], xct[:, :], 0.0, xct[:, :], ALU.add, ALU.mult,
            accum_out=vp[:, :]), reads=[xct], writes=[absd, vp])
        q.do("act", lambda e: e.activation(absd[:, :], xct[:, :], AF.Abs,
                                           accum_out=ap1[:, :]),
             reads=[xct], writes=[absd, ap1])
        q.do("dve", lambda e: e.tensor_scalar(
            vp[:, :], vp[:, :], 1.0 / D, 1e-6, ALU.mult, ALU.add),
            reads=[vp], writes=[vp])
        q.do("dve", lambda e: e.reciprocal(scp[:, :], vp[:, :]), reads=[vp], writes=[scp])
        q.do("dve", lambda e: e.tensor_scalar(
            ap1[:, :], ap1[:, :], 1.0 / D, 8e-4, ALU.mult, ALU.add),
            reads=[ap1], writes=[ap1])
        q.do("dve", lambda e: e.reciprocal(rsp[:, :], ap1[:, :]), reads=[ap1], writes=[rsp])
        q.do("dve", lambda e: e.tensor_scalar_mul(rsp[:, :], rsp[:, :], 1.2533141),
             reads=[rsp], writes=[rsp])
        heron(rsp, scp, hp)
        q.do("dve", lambda e: e.tensor_scalar(
            xct[:, :], xct[:, :], rsp[:, :], None, ALU.mult),
            reads=[xct, rsp], writes=[xct])
        q.do("dve", lambda e: e.tensor_mul(xct[:, :], xct[:, :], gpnb[:, :]),
             reads=[xct, gpnb], writes=[xct])
        q.do("dve", lambda e: e.tensor_add(xct[:, :], xct[:, :], bpnb[:, :]),
             reads=[xct, bpnb], writes=[xct])
        q.copy("pool", xpnb[:, :], xct[:, :], reads=[xct], writes=[xpnb])
        for f in range(8):
            sem_name = "dtx"
            q.sem(sem_name)
            if q.cnt[sem_name] > 0:
                q.wt("sp", ("eager", sem_name, q.cnt[sem_name]))
            q.do("sp", lambda e, f=f: e.dma_start_transpose(
                xpT[:, 128 * f:128 * (f + 1)], xpnb[:, 128 * f:128 * (f + 1)]),
                reads=[xpnb], writes=[xpT], inc=sem_name, n=16, is_dma=True,
                acc=(f > 0))
        for half in range(2):
            bi, pb = pp.get()
            for k in range(8):
                q.do("pe", lambda e, pb=pb, k=k, h=half: e.matmul(
                    pb[:, 0:512], xpT[:, 128 * k:128 * (k + 1)],
                    wotb[k][:, 512 * h:512 * (h + 1)], start=(k == 0), stop=(k == 7)),
                    reads=[xpT, wotb[k]], writes=[pb], acc=(k > 0))
            q.copy("dve", stile[:, 512 * half:512 * (half + 1)], pb[:, 0:512],
                   reads=[pb], writes=[stile])
        q.do("dve", lambda e, nt=nt: e.reduce_max(xam[:, :], nt[:, :], axis=AX,
                                                  apply_absolute_value=True),
             reads=[nt], writes=[xam])
        q.do("dve", lambda e: e.tensor_scalar(xam[:, :], xam[:, :], 1e-8, None, ALU.max),
             reads=[xam], writes=[xam])
        q.do("dve", lambda e: e.tensor_scalar_mul(xam[:, :], xam[:, :], 1.0 / 127.0),
             reads=[xam], writes=[xam])
        q.do("dve", lambda e: e.reciprocal(hp[:, :], xam[:, :]), reads=[xam], writes=[hp])
        q.do("dve", lambda e, nt=nt: e.tensor_scalar(
            xqf[:, :], nt[:, :], hp[:, :], None, ALU.mult),
            reads=[nt, hp], writes=[xqf])
        q.do("dve", lambda e: e.tensor_scalar(
            xqf[:, :], xqf[:, :], 12582912.0, -12582912.0, ALU.add, ALU.add),
            reads=[xqf], writes=[xqf])
        q.do("dve", lambda e: e.tensor_copy(xint[:, :], xqf[:, :]),
             reads=[xqf], writes=[xint])
        q.copy("pool", xqb[:, :], xint[:, :], reads=[xint], writes=[xqb])
        for f in range(8):
            sem_name = "dqx"
            q.sem(sem_name)
            if q.cnt[sem_name] > 0:
                q.wt("sp", ("eager", sem_name, q.cnt[sem_name]))
            q.do("sp", lambda e, f=f: e.dma_start_transpose(
                xqTt[:, 128 * f:128 * (f + 1)], xqb[:, 128 * f:128 * (f + 1)]),
                reads=[xqb], writes=[xqTt], inc=sem_name, n=16, is_dma=True,
                acc=(f > 0))
        for half in range(2):
            bi, pb = pp.get()
            for k in range(8):
                q.do("pe", lambda e, pb=pb, k=k, h=half: e.matmul(
                    pb[:, 0:512], xqTt[:, 128 * k:128 * (k + 1)],
                    wqT[k][:, 512 * h:512 * (h + 1)], start=(k == 0), stop=(k == 7)),
                    reads=[xqTt, wqT[k]], writes=[pb], acc=(k > 0))
            q.copy("dve", gq[:, 512 * half:512 * (half + 1)], pb[:, 0:512],
                   reads=[pb], writes=[gq])
        q.do("dve", lambda e: e.tensor_mul(gq[:, :], gq[:, :], wsbc[:, :]),
             reads=[gq, wsbc], writes=[gq])
        q.do("dve", lambda e: e.tensor_scalar(
            gq[:, :], gq[:, :], xam[:, :], None, ALU.mult),
            reads=[gq, xam], writes=[gq])
        q.do("dve", lambda e: e.tensor_add(gq[:, :], gq[:, :], bfbc[:, :]),
             reads=[gq, bfbc], writes=[gq])
        q.do("act", lambda e: e.activation(gq[:, :], gq[:, :], AF.Silu),
             reads=[gq], writes=[gq])
        if "d_s" in dbg_o:
            q.dma("pool", dbg_o["d_s"][128 * i:128 * (i + 1), :], stile[:, :],
                  reads=[stile], writes=[dbg_o["d_s"]])
        if "d_g" in dbg_o:
            q.dma("pool", dbg_o["d_g"][128 * i:128 * (i + 1), :], gq[:, :],
                  reads=[gq], writes=[dbg_o["d_g"]])
        if "d_xpn" in dbg_o:
            q.dma("pool", dbg_o["d_xpn"][128 * i:128 * (i + 1), :], xct[:, :],
                  reads=[xct], writes=[dbg_o["d_xpn"]])
        q.do("dve", lambda e: e.tensor_mul(stile[:, :], stile[:, :], gq[:, :]),
             reads=[stile, gq], writes=[stile])
        q.do("dve", lambda e, ot=ot: e.tensor_add(stile[:, :], stile[:, :], ot[:, :]),
             reads=[stile, ot], writes=[stile])
        q.dma("sp", x2o[128 * i:128 * (i + 1), :], stile[:, :],
              reads=[stile], writes=[x2o])

    if env is not None:
        return x2o
    q.wt("pool", q.last_tok["pool"])
    q.wt("sp", q.last_tok["sp"])
    q.build()
    ctx.close()
    return nc


def build_full(dbg=(), nstep=NSTEP):
    nc = bass.Bass("TRN2", target_bir_lowering=False)
    ctx = contextlib.ExitStack()
    q = Q(nc, ctx)
    pp = Psum(q, nc, ctx)

    s0 = contextlib.ExitStack()
    x1own = build_l0(env=(nc, s0, q, pp))
    q.barrier()
    s0.close()

    s1 = contextlib.ExitStack()
    gbufs = build_l1s(dbg=dbg, nstep=nstep, env=(nc, s1, q, pp), x1own=x1own)
    q.barrier()
    s1.close()

    s2 = contextlib.ExitStack()
    build_l1p(env=(nc, s2, q, pp), gbufs=gbufs)
    q.barrier()
    s2.close()

    q.wt("pool", q.last_tok["pool"])
    q.wt("sp", q.last_tok["sp"])
    q.build()
    ctx.close()
    return nc


def l1p_host_inputs(inputs, core, l2res, x1):
    b, hh = core % 4, core // 4
    C = np.ascontiguousarray
    T0 = TOK3 * hh
    p64 = _perm64()
    toks = slice(T0, T0 + TOK3)
    z2b = np.zeros((TOK3, D), np.float32)
    xqp = np.zeros((TOK3, D), np.float32)
    for h2 in range(2):
        rr = l2res[b + 4 * h2]
        z2b[:, 512 * h2:512 * (h2 + 1)] = rr["z2bo"][toks]
        for m in range(4):
            slab = rr["xqro"][m].reshape(128, 2 * S)
            xqcols = slab.reshape(128, 256, 32)[:, :, 0:16].reshape(128, S)
            for hl2 in range(2):
                h = 8 * h2 + 2 * m + hl2
                xqp[:, 64 * h:64 * (h + 1)] = xqcols[64 * hl2:64 * hl2 + 64, toks].T
    xn1p = C(l2res[b]["xn1o"][toks])
    x1p = C(np.asarray(x1[b], np.float32)[toks])
    glw = np.asarray(inputs["L1_attn_ttt_norm_weight"], np.float32)[:, p64].reshape(D)
    glb = np.asarray(inputs["L1_attn_ttt_norm_bias"], np.float32)[:, p64].reshape(D)
    return {
        "z2b": z2b, "xqp": xqp, "xn1p": xn1p,
        "x1p": x1p,
        "glnb": np.tile(glw[None, :], (128, 1)).astype(np.float32),
        "blnb": np.tile(glb[None, :], (128, 1)).astype(np.float32),
        **l1p_host_weights(inputs),
    }


def l1p_host_weights(inputs):
    C = np.ascontiguousarray
    p64 = _perm64()
    permfull = (64 * (np.arange(D) // 64) + p64[np.arange(D) % 64])
    gpw = np.asarray(inputs["L1_attn_post_norm_weight"], np.float32)[permfull]
    gpb = np.asarray(inputs["L1_attn_post_norm_bias"], np.float32)[permfull]
    wo = np.asarray(inputs["L1_attn_wo"], np.float32)
    return {
        "gpnb": np.tile(gpw[None, :], (128, 1)).astype(np.float32),
        "bpnb": np.tile(gpb[None, :], (128, 1)).astype(np.float32),
        "wot": C(wo[:, permfull].T),
        "ffw": C(np.asarray(inputs["L1_ffn_weight"], np.float32)),
        "ffb": C(np.asarray(inputs["L1_ffn_bias"], np.float32))[None, :],
        "ones1": np.ones((1, 128), np.float32),
    }


def full_host_inputs(inputs, core):
    m = dict(l0_host_inputs(inputs, core))
    for k, v in l1s_host_inputs(inputs, core).items():
        m["s_" + k] = v
    for k, v in l1p_host_weights(inputs).items():
        m["p_" + k] = v
    return m



def _perm32(d):
    return np.concatenate([np.arange(0, d, 2), np.arange(1, d, 2)])


def l0_host_inputs(inputs, core):
    b, hh = core % 4, core // 4
    C = np.ascontiguousarray
    x = np.asarray(inputs["L0_attn_x"], np.float32)
    own = np.arange(hh, NB, 2)
    other = np.arange(1 - hh, NB, 2)
    blocks = np.concatenate([own, other])
    xb = C(x[b].reshape(NB, 128, D)[blocks].reshape(S, D))
    perm = _perm32(DHR)
    perm_swap = np.concatenate([perm[16:], perm[:16]])
    wqr = np.asarray(inputs["L0_attn_W_QR"], np.float32).reshape(DL, NH, DHR)
    wkr = np.asarray(inputs["L0_attn_W_KR"], np.float32)
    inv = 10000.0 ** (-np.arange(0, DHR, 2, dtype=np.float64) / DHR)
    pos = (blocks[:, None] * 128 + np.arange(128)[None, :]).reshape(S).astype(np.float64)
    ang = pos[None, :] * inv[:, None]
    cosk = np.concatenate([np.cos(ang), np.cos(ang)], 0).astype(np.float32)
    sink = np.concatenate([-np.sin(ang), np.sin(ang)], 0).astype(np.float32)
    co = np.concatenate([np.cos(ang[:, :OWNQ]), np.cos(ang[:, :OWNQ])], 0).astype(np.float32)
    si = np.concatenate([-np.sin(ang[:, :OWNQ]), np.sin(ang[:, :OWNQ])], 0).astype(np.float32)
    return {
        "xb": xb,
        "wdq": C(np.asarray(inputs["L0_attn_W_DQ"], np.float32)),
        "wuq": C(np.asarray(inputs["L0_attn_W_UQ"], np.float32)),
        "wqr_a": C(wqr[:, :, perm].reshape(DL, NH * DHR)),
        "wqr_b": C(wqr[:, :, perm_swap].reshape(DL, NH * DHR)),
        "wdkv": C(np.asarray(inputs["L0_attn_W_DKV"], np.float32)),
        "wuk": C(np.asarray(inputs["L0_attn_W_UK"], np.float32)),
        "wuv": C(np.asarray(inputs["L0_attn_W_UV"], np.float32)),
        "wkr_a": C(wkr[:, perm]), "wkr_b": C(wkr[:, perm_swap]),
        "wo": C(np.asarray(inputs["L0_attn_W_O"], np.float32)),
        "u1": C(np.asarray(inputs["L0_ffn_U1"], np.float32)),
        "v1": C(np.asarray(inputs["L0_ffn_V1"], np.float32)),
        "b1r": C(np.asarray(inputs["L0_ffn_b1"], np.float32)[None, :]),
        "u2": C(np.asarray(inputs["L0_ffn_U2"], np.float32)),
        "v2": C(np.asarray(inputs["L0_ffn_V2"], np.float32)),
        "b2r": C(np.asarray(inputs["L0_ffn_b2"], np.float32)[None, :]),
        "cosq": np.tile(co, (4, 1)), "sinq": np.tile(si, (4, 1)),
        "cosk": cosk, "sink": sink,
        "trimask": np.triu(np.ones((128, 128), np.float32)),
        "sel16": np.kron(np.eye(16, dtype=np.float32), np.ones((1, 128), np.float32)),
        "m2col": np.full((128, 1), float(hh), np.float32),
        "onesr": np.ones((1, 512), np.float32),
    }


def l0_gather(inputs, results):
    x1 = np.empty((B, S, D), np.float32)
    for c in range(8):
        b, hh = c % 4, c // 4
        own = np.arange(hh, NB, 2)
        r = results[c]["x1own"].reshape(NOB, 128, D)
        for j, blk in enumerate(own):
            x1[b, 128 * blk:128 * (blk + 1), :] = r[j]
    return x1




def _perm64():
    return np.concatenate([np.arange(0, 64, 2), np.arange(1, 64, 2)])


def l1s_host_inputs(inputs, core, x1=None):
    b, hh = core % 4, core // 4
    C = np.ascontiguousarray
    heads = np.arange(8 * hh, 8 * hh + 8)
    p64 = _perm64()
    p64s = np.concatenate([p64[32:], p64[:32]])
    wq = np.asarray(inputs["L1_attn_wq"], np.float32)
    wk = np.asarray(inputs["L1_attn_wk"], np.float32)
    wv = np.asarray(inputs["L1_attn_wv"], np.float32)

    def rows(w, perm):
        idx = (64 * heads[:, None] + perm[None, :]).reshape(-1)
        return C(w[idx, :].T)

    inv = 1.0 / 10000.0 ** (np.arange(0, 64, 2, dtype=np.float64) / 64)
    pos = np.arange(512, dtype=np.float64) % 16
    ang = pos[None, :] * inv[:, None]
    cos1 = np.cos(ang)
    sin1 = np.sin(ang)
    cosl = np.tile(np.concatenate([cos1, cos1], 0), (2, 1)).astype(np.float32)
    sinl = np.tile(np.concatenate([-sin1, sin1], 0), (2, 1)).astype(np.float32)
    w1 = np.asarray(inputs["L1_attn_W1"], np.float32)[heads][:, p64, :]
    b1 = np.asarray(inputs["L1_attn_b1"], np.float32)[heads, 0, :]
    w2 = np.asarray(inputs["L1_attn_W2"], np.float32)[heads][:, :, p64]
    b2 = np.asarray(inputs["L1_attn_b2"], np.float32)[heads][:, 0, p64]
    nw = np.asarray(inputs["L1_attn_ttt_norm_weight"], np.float32)[heads][:, p64]
    nb = np.asarray(inputs["L1_attn_ttt_norm_bias"], np.float32)[heads][:, p64]
    w1c0 = np.zeros((128, 1024), np.float32)
    w2t0 = np.zeros((128, 1024), np.float32)
    kh0 = np.zeros((128, 512), np.float32)
    kh1 = np.zeros((128, 512), np.float32)
    for p in range(8):
        r0, c0 = 64 * (p % 2), 256 * (p // 2)
        w1c0[r0:r0 + 64, c0:c0 + 256] = w1[p]
        w2t0[r0:r0 + 64, c0:c0 + 256] = w2[p].T
        kh0[:, 64 * p:64 * p + 64] = w2[p][0:128, :]
        kh1[:, 64 * p:64 * p + 64] = w2[p][128:256, :]
    selwA = np.zeros((8, 128), np.float32)
    selwB = np.zeros((8, 128), np.float32)
    selzTA = np.zeros((128, 8), np.float32)
    selzTB = np.zeros((128, 8), np.float32)
    for p in range(8):
        (selwA if p % 2 == 0 else selwB)[p, 32 * (p // 2):32 * (p // 2) + 16] = 1.0
        (selzTA if p % 2 == 0 else selzTB)[32 * (p // 2):32 * (p // 2) + 16, p] = 1.0
    triu16 = np.zeros((128, 16), np.float32)
    for s_ in range(4):
        for j in range(16):
            triu16[32 * s_ + j, j:] = 1.0
    gslA = np.zeros((128, 64), np.float32)
    gslB = np.zeros((128, 64), np.float32)
    for p in range(8):
        g = (gslA if p % 2 == 0 else gslB)
        g[32 * (p // 2):32 * (p // 2) + 32, :] = nw[p]
    gbc = np.tile(nw.reshape(1, 512), (128, 1)).astype(np.float32)
    bbc = np.tile(nb.reshape(1, 512), (128, 1)).astype(np.float32)
    return {
        **({"x1b": C(x1[b].astype(np.float32))} if x1 is not None else {}),
        "wq_a": rows(wq, p64), "wq_b": rows(wq, p64s),
        "wk_a": rows(wk, p64), "wk_b": rows(wk, p64s),
        "wv_o": rows(wv, p64),
        "cosl": cosl, "sinl": sinl,
        "lrwT": C(np.asarray(inputs["L1_attn_ttt_lr_weight"], np.float32)[heads, 0, :].T),
        "lrb": C(np.asarray(inputs["L1_attn_ttt_lr_bias"], np.float32)[heads]),
        "invr16": (1.0 / np.arange(1, 17, dtype=np.float64)).astype(np.float32)[None, :],
        "lti": C(np.asarray(inputs["L1_attn_learnable_token_idx"], np.float32)[None, :]),
        "w1c0": w1c0, "b1c0": C(b1), "kh0": kh0, "kh1": kh1, "w2t0": w2t0, "b2c0": C(b2),
        "selwA": selwA, "selwB": selwB, "selzTA": selzTA, "selzTB": selzTB,
        "triu16": triu16,
        "gslotA": gslA, "gslotB": gslB, "gbc": gbc, "bbc": bbc,
        "ident": np.eye(128, dtype=np.float32),
        "onesr": np.ones((1, 512), np.float32),
        "zeror": np.zeros((1, 128), np.float32),
    }


def kernel(**inputs):
    cores = list(range(8))
    nc = build_full()
    im = [full_host_inputs(inputs, c) for c in cores]
    r = run_bass_kernel_spmd(nc, im, core_ids=cores)
    out = np.empty((B, S, D), np.float32)
    for b in range(4):
        out[b] = r.results[b]["x2o"]
    return out


def kernel_unfused(**inputs):
    cores = list(range(8))
    nc0 = build_l0()
    in0 = [l0_host_inputs(inputs, c) for c in cores]
    r0 = run_bass_kernel_spmd(nc0, in0, core_ids=cores)
    x1 = l0_gather(inputs, r0.results)
    nc1 = build_l1s()
    in1 = [l1s_host_inputs(inputs, c, x1) for c in cores]
    r1 = run_bass_kernel_spmd(nc1, in1, core_ids=cores)
    nc2 = build_l1p()
    in2 = [l1p_host_inputs(inputs, c, r1.results, x1) for c in cores]
    r2 = run_bass_kernel_spmd(nc2, in2, core_ids=cores)
    out = np.empty((B, S, D), np.float32)
    for c in cores:
        b, hh = c % 4, c // 4
        out[b, TOK3 * hh:TOK3 * (hh + 1), :] = r2.results[c]["x2o"]
    return out
